# version 09
# baseline (speedup 1.0000x reference)
import numpy as np
import ml_dtypes
from contextlib import ExitStack

import concourse.bass as bass
import concourse.mybir as mybir
from concourse.bass_utils import run_bass_kernel_spmd

F32 = mybir.dt.float32
BF16 = mybir.dt.bfloat16
I32 = mybir.dt.int32
AF = mybir.ActivationFunctionType
OP = mybir.AluOpType
AX = mybir.AxisListType

D = 1024
L = 4096
DI = 2048
DS = 16
DTR = 64
DC = 4
NLAY = 2
NE = 8
FF = 4096
NH = 16
HD = 64
EPS_LN = 1e-5
EPS_RMS = 1.1920929e-07

KD = D // 128
KI = DI // 128
KF = FF // 128


class Pr:
    def __init__(self, nc, blk, tok):
        self.nc = nc
        self.blk = blk
        self.tok = tok
        self.n = 0

    def ph(self, eng_name, fn):
        n0 = self.n
        def body(eng):
            eng.wait_ge(self.tok, n0)
            fn(self, eng)
        getattr(self.blk, eng_name)(body)

    def inc(self, inst, k=1):
        inst.then_inc(self.tok, k)
        self.n += k

    def dma(self, inst):
        self.inc(inst, 16)


class Ctr:
    def __init__(self, nc, name):
        self.sem = nc.semaphore(name).__enter__()
        self.name = name
        self.n = 0


class Sched:
    ENGINES = ("sync", "tensor", "scalar", "vector", "gpsimd")

    def __init__(self, p, name):
        self.p = p
        self.name = name
        self.ops = {e: [] for e in self.ENGINES}
        self.ctrs = {}
        self.events = {}

    def ctr(self, cname):
        if cname not in self.ctrs:
            self.ctrs[cname] = Ctr(self.p.nc, f"{self.name}_{cname}")
        return self.ctrs[cname]

    def add(self, eng, emit, waits=(), tick=None, dma=False, event=None):
        wv = []
        for w in waits:
            wv.append(self.events[w] if isinstance(w, str) else w)
        val = None
        if tick is not None:
            c = self.ctr(tick)
            c.n += 16 if dma else 1
            val = c.n
        self.ops[eng].append((emit, wv, tick, dma))
        if event is not None:
            assert tick is not None
            self.events[event] = (tick, val)

    def ev(self, key):
        return self.events[key]

    def run(self):
        entry = self.p.n
        used = [e for e in self.ENGINES if self.ops[e]]
        for eng in used:
            oplist = self.ops[eng]
            def body(E, oplist=oplist, eng=eng):
                E.wait_ge(self.p.tok, entry)
                seen = {}
                for emit, wv, tick, dma in oplist:
                    merged = {}
                    for cn, v in wv:
                        if v is not None and merged.get(cn, -1) < v:
                            merged[cn] = v
                    for cn, v in merged.items():
                        if seen.get(cn, -1) < v:
                            E.wait_ge(self.ctrs[cn].sem, v)
                            seen[cn] = v
                    inst = emit(E)
                    if tick is not None:
                        inst.then_inc(self.ctrs[tick].sem, 16 if dma else 1)
                own = None
                for emit, wv, tick, dma in reversed(oplist):
                    if tick is not None:
                        own = tick
                        break
                if own is not None:
                    E.wait_ge(self.ctrs[own].sem, self.ctrs[own].n)
                E.nop().then_inc(self.p.tok, 1)
            getattr(self.p.blk, eng)(body)
            self.p.n += 1


def feat_slice(dram, ktiles, c, T):
    return dram.rearrange("(a p) t -> p a t", p=128)[:, :, c * T:(c + 1) * T]


def sb3(sb_ap, ktiles, T):
    return sb_ap.rearrange("p (a t) -> p a t", a=ktiles)


def load_chunk(p, eng, sbuf, dram, ktiles, c, T):
    p.dma(eng.dma_start(out=sb3(sbuf[:], ktiles, T), in_=feat_slice(dram, ktiles, c, T)))


def store_chunk(p, eng, dram, sbuf, ktiles, c, T):
    p.dma(eng.dma_start(out=feat_slice(dram, ktiles, c, T), in_=sb3(sbuf[:], ktiles, T)))


def rstd_from_ms(p, A, ms_dst, ms_src_psum, scale, eps):
    A.activation(out=ms_dst, in_=ms_src_psum, func=AF.Ln, scale=scale, bias=eps)
    return A.activation(out=ms_dst, in_=ms_dst, func=AF.Exp, scale=-0.5)


def sum_matmuls(p, t, psum, src_sb, ktiles, T, ones_col):
    for a in range(ktiles):
        inst = t.matmul(psum[0:1, 0:T], ones_col[:], src_sb[:, a * T:(a + 1) * T],
                        start=(a == 0), stop=(a == ktiles - 1))
    return inst


def bcast_matmul(t, psum, row_ap, T, ones_row, m=128):
    return t.matmul(psum[0:m, 0:T], ones_row[0:1, 0:m], row_ap, start=True, stop=True)


class Consts:
    pass


def alloc_consts(p, stack, masks_dram, ident_dram):
    nc = p.nc
    c = Consts()
    c.ones_col = stack.enter_context(nc.sbuf_tensor("ones_col", [128, 1], F32))
    c.ones_row = stack.enter_context(nc.sbuf_tensor("ones_row", [1, 128], F32))
    c.ones_row_b = stack.enter_context(nc.sbuf_tensor("ones_row_b", [1, 128], BF16))
    c.ident_b = stack.enter_context(nc.sbuf_tensor("ident_b", [128, 128], BF16))
    c.ident_f = stack.enter_context(nc.sbuf_tensor("ident_f", [128, 128], F32))
    c.masks = stack.enter_context(nc.sbuf_tensor("masks", [128, 4 * 512], BF16))
    c.cst = stack.enter_context(nc.sbuf_tensor("cst", [128, 16], F32))

    cvals = [EPS_LN, EPS_RMS, -1.0, -2.0, 0.5, -128.0, -0.5, 1.0 / D]
    def f(p, v):
        p.inc(v.memset(c.ones_col[:], 1.0))
        p.inc(v.memset(c.ones_row[:], 1.0))
        p.inc(v.memset(c.ones_row_b[:], 1.0))
        for i, val in enumerate(cvals):
            p.inc(v.memset(c.cst[:, i:i + 1], val))
            try:
                nc.const_aps.register(F32, val, c.cst[:, i:i + 1])
            except Exception:
                pass
    p.ph("vector", f)

    def g(p, s):
        p.dma(s.dma_start(out=c.masks[:], in_=masks_dram.ap()[:]))
        p.dma(s.dma_start(out=c.ident_f[:], in_=ident_dram.ap()[:]))
    p.ph("sync", g)

    def h(p, gp):
        p.dma(gp.dma_start(out=c.ident_b[:], in_=ident_dram.ap()[:]))
    p.ph("gpsimd", h)
    return c


def stage_ln(p, C, src, dst_f=None, dst_b=None, T=512, name="ln"):
    nc = p.nc
    NCH = L // T
    with ExitStack() as st:
        x = st.enter_context(nc.sbuf_tensor(f"{name}_x", [128, KD * T], F32))
        sq = st.enter_context(nc.sbuf_tensor(f"{name}_sq", [128, KD * T], F32))
        stats = st.enter_context(nc.sbuf_tensor(f"{name}_st", [1, 4 * T], F32))
        psA = st.enter_context(nc.psum_tensor(f"{name}_pA", [1, T], F32))
        psB = st.enter_context(nc.psum_tensor(f"{name}_pB", [1, T], F32))
        psR = st.enter_context(nc.psum_tensor(f"{name}_pR", [128, T], F32))
        psM = st.enter_context(nc.psum_tensor(f"{name}_pM", [128, T], F32))
        for ci in range(NCH):
            p.ph("sync", lambda p, s, ci=ci: load_chunk(p, s, x, src, KD, ci, T))
            p.ph("scalar", lambda p, A: p.inc(A.activation(out=sq[:], in_=x[:], func=AF.Square)))
            p.ph("tensor", lambda p, t: p.inc(sum_matmuls(p, t, psA, x[:], KD, T, C.ones_col)))
            p.ph("tensor", lambda p, t: p.inc(sum_matmuls(p, t, psB, sq[:], KD, T, C.ones_col)))
            def fstats(p, v):
                v.tensor_scalar(out=stats[0:1, 0:T], in0=psA[0:1, 0:T], scalar1=1.0 / D,
                                scalar2=None, op0=OP.mult)
                v.tensor_tensor(out=stats[0:1, T:2 * T], in0=stats[0:1, 0:T],
                                in1=stats[0:1, 0:T], op=OP.mult)
                p.inc(v.scalar_tensor_tensor(out=stats[0:1, T:2 * T], in0=psB[0:1, 0:T],
                                             scalar=1.0 / D, in1=stats[0:1, T:2 * T],
                                             op0=OP.mult, op1=OP.subtract))
            p.ph("vector", fstats)
            p.ph("scalar", lambda p, A: p.inc(
                rstd_from_ms(p, A, stats[0:1, 2 * T:3 * T], stats[0:1, T:2 * T], 1.0, EPS_LN)))
            p.ph("vector", lambda p, v: p.inc(
                v.tensor_tensor(out=stats[0:1, 3 * T:4 * T], in0=stats[0:1, 0:T],
                                in1=stats[0:1, 2 * T:3 * T], op=OP.mult)))
            def fb(p, t):
                bcast_matmul(t, psR, stats[0:1, 2 * T:3 * T], T, C.ones_row)
                p.inc(bcast_matmul(t, psM, stats[0:1, 3 * T:4 * T], T, C.ones_row))
            p.ph("tensor", fb)
            def fnorm(p, v):
                for a in range(KD):
                    sl = slice(a * T, (a + 1) * T)
                    v.tensor_tensor(out=sq[:, sl], in0=x[:, sl], in1=psR[:, 0:T], op=OP.mult)
                    last = v.tensor_tensor(out=sq[:, sl], in0=sq[:, sl], in1=psM[:, 0:T],
                                           op=OP.subtract)
                p.inc(last)
            p.ph("vector", fnorm)
            if dst_f is not None:
                p.ph("sync", lambda p, s, ci=ci: store_chunk(p, s, dst_f, sq, KD, ci, T))
            if dst_b is not None:
                p.ph("gpsimd", lambda p, g, ci=ci: p.dma(
                    g.dma_start(out=feat_slice(dst_b, KD, ci, T), in_=sb3(sq[:], KD, T))))


def stage_mamba_a(p, C, W, src_b, yg_dram, bc_dram, T=256, name="ma"):
    nc = p.nc
    NCH = L // T
    SEG = 3 + T
    with ExitStack() as st:
        w_in = st.enter_context(nc.sbuf_tensor(f"{name}_win", [128, KD * (2 * DI)], BF16))
        w_xp = st.enter_context(nc.sbuf_tensor(f"{name}_wxp", [128, KI * 96], BF16))
        w_dtp = st.enter_context(nc.sbuf_tensor(f"{name}_wdtp", [64, DI], BF16))
        convw = st.enter_context(nc.sbuf_tensor(f"{name}_cw", [128, 64], F32))
        convb = st.enter_context(nc.sbuf_tensor(f"{name}_cb", [128, 16], F32))
        dtb = st.enter_context(nc.sbuf_tensor(f"{name}_dtb", [128, 16], F32))
        alogf = st.enter_context(nc.sbuf_tensor(f"{name}_alf", [128, 256], F32))
        aneg = st.enter_context(nc.sbuf_tensor(f"{name}_an", [128, 256], BF16))
        dpar = st.enter_context(nc.sbuf_tensor(f"{name}_dp", [128, 16], F32))
        hcarry = st.enter_context(nc.sbuf_tensor(f"{name}_hc", [128, 256], BF16))
        hist = st.enter_context(nc.sbuf_tensor(f"{name}_hist", [128, 16 * 3], BF16))

        xin_b = st.enter_context(nc.sbuf_tensor(f"{name}_xb", [128, KD * T], BF16))
        x_in = st.enter_context(nc.sbuf_tensor(f"{name}_xin", [128, 2 * KI * SEG], BF16))
        sz = st.enter_context(nc.sbuf_tensor(f"{name}_sz", [128, 2 * KI * T], BF16))
        xc = st.enter_context(nc.sbuf_tensor(f"{name}_xc", [128, 2 * KI * T], BF16))
        ctmp = st.enter_context(nc.sbuf_tensor(f"{name}_ctp", [128, 2 * T], F32))
        dt = st.enter_context(nc.sbuf_tensor(f"{name}_dt", [128, KI * T], BF16))
        wdt = st.enter_context(nc.sbuf_tensor(f"{name}_w", [128, KI * T], BF16))
        xdbl_b = st.enter_context(nc.sbuf_tensor(f"{name}_xdb", [64, T], BF16))
        bc_sb = st.enter_context(nc.sbuf_tensor(f"{name}_bcs", [32, T], BF16))
        bcflat = st.enter_context(nc.sbuf_tensor(f"{name}_bc", [1, 32 * T], BF16))
        brc = st.enter_context(nc.sbuf_tensor(f"{name}_brc", [128, 16 * 2 * T], BF16))
        argq = st.enter_context(nc.sbuf_tensor(f"{name}_ag", [128, 4 * 4 * (T + 2)], BF16))
        WS = T + 2
        h8q = st.enter_context(nc.sbuf_tensor(f"{name}_h8", [128, 4 * WS], BF16))
        dbu = st.enter_context(nc.sbuf_tensor(f"{name}_db", [128, 2 * 4 * WS], BF16))
        hct = st.enter_context(nc.sbuf_tensor(f"{name}_hct", [128, 2 * 4 * T], BF16))
        ytmp = st.enter_context(nc.sbuf_tensor(f"{name}_yt", [128, T], F32))
        ygq = st.enter_context(nc.sbuf_tensor(f"{name}_ygq", [128, 2 * 4 * T], BF16))

        pp = [st.enter_context(nc.psum_tensor(f"{name}_pp{i}", [128, 512], F32)) for i in range(2)]
        pbc = [st.enter_context(nc.psum_tensor(f"{name}_pb{i}", [128, 512], F32)) for i in range(2)]
        py = [st.enter_context(nc.psum_tensor(f"{name}_py{i}", [128, 1024], F32)) for i in range(2)]

        S = Sched(p, name)

        S.add("gpsimd", lambda g: g.dma_start(out=w_in[:].rearrange("p (a m) -> p a m", a=KD),
                                              in_=W["inT"].rearrange("(a p) m -> p a m", p=128)),
              tick="wl", dma=True, event="win")
        S.add("gpsimd", lambda g: g.dma_start(out=w_xp[:].rearrange("p (a m) -> p a m", a=KI),
                                              in_=W["xpT"].rearrange("(a p) m -> p a m", p=128)),
              tick="wl", dma=True, event="wxp")
        S.add("gpsimd", lambda g: g.dma_start(out=w_dtp[:], in_=W["dtpT"]),
              tick="wl", dma=True, event="wdtp")
        S.add("sync", lambda s: s.dma_start(out=convw[:], in_=W["convw"]), tick="ld", dma=True)
        S.add("sync", lambda s: s.dma_start(out=convb[:], in_=W["convb"]), tick="ld", dma=True)
        S.add("sync", lambda s: s.dma_start(out=dtb[:], in_=W["dtb"]), tick="ld", dma=True)
        S.add("sync", lambda s: s.dma_start(out=dpar[:], in_=W["dp"]), tick="ld", dma=True)
        S.add("sync", lambda s: s.dma_start(out=alogf[:], in_=W["alog"]),
              tick="ld", dma=True, event="consts")
        S.add("scalar", lambda A: A.activation(out=alogf[:], in_=alogf[:], func=AF.Exp),
              waits=["consts"], tick="ex", event="aexp")
        def zi(v):
            v.tensor_scalar(out=aneg[:], in0=alogf[:], scalar1=-1.0, scalar2=None, op0=OP.mult)
            v.memset(argq[:], 0.0)
            v.memset(argq[:].rearrange("p (g w) -> p g w", w=WS)[:, :, 1], 1.0)
            v.memset(dbu[:], 0.0)
            v.memset(hcarry[:], 0.0)
            return v.memset(hist[:], 0.0)
        S.add("vector", zi, waits=["aexp"], tick="dv", event="aneg")

        aneg3 = aneg[:].rearrange("p (g s2) -> p g s2", s2=16)

        def emit_inproj(ci, lo, hi):
            cur = ci % 2
            xof = cur * KI * SEG
            cof = cur * KI * T
            if lo == 0:
                S.add("sync", lambda s, ci=ci: s.dma_start(
                    out=sb3(xin_b[:], KD, T), in_=feat_slice(src_b, KD, ci, T)),
                    waits=([f"ipmm_{ci - 1}_{2 * KI - 1}"] if ci > 0 else []),
                    tick="ld", dma=True, event=f"xld_{ci}")
            for et in range(lo, hi):
                bank = pp[et % 2]
                bw = [f"xld_{ci}", "win"]
                if et >= 2:
                    bw.append(f"ipev_{ci}_{et - 2}")
                elif ci > 0:
                    bw.append(f"sp_{ci - 1}_{14 + et}")
                def fmm(t, et=et, bank=bank):
                    for kt in range(KD):
                        inst = t.matmul(bank[:, 0:T],
                                        w_in[:, kt * 2 * DI + et * 128: kt * 2 * DI + et * 128 + 128],
                                        xin_b[:, kt * T:(kt + 1) * T],
                                        start=(kt == 0), stop=(kt == KD - 1))
                    return inst
                S.add("tensor", fmm, waits=bw, tick="pe", event=f"ipmm_{ci}_{et}")
                if et < KI:
                    ew = [f"ipmm_{ci}_{et}"]
                    if ci >= 2:
                        ew.append(f"cv_{ci - 2}_{KI - 1}")
                    S.add("scalar", lambda A, et=et, bank=bank, xof=xof: A.copy(
                        x_in[:, xof + et * SEG + 3: xof + et * SEG + 3 + T], bank[:, 0:T]),
                        waits=ew, tick="ex", event=f"ipev_{ci}_{et}")
                else:
                    g = et - KI
                    ew = [f"ipmm_{ci}_{et}"]
                    if ci >= 2:
                        ew.append(f"yq_{ci - 2}_3")
                    S.add("scalar", lambda A, g=g, bank=bank, cof=cof: A.activation(
                        out=sz[:, cof + g * T: cof + (g + 1) * T], in_=bank[:, 0:T], func=AF.Silu),
                        waits=ew, tick="ex", event=f"ipev_{ci}_{et}")

        emit_inproj(0, 0, 2 * KI)
        for ci in range(NCH):
            cur = ci % 2
            xof = cur * KI * SEG
            cof = cur * KI * T
            for g in range(KI):
                base = xof + g * SEG
                def fconv(v, g=g, base=base):
                    v.tensor_copy(x_in[:, base: base + 3], hist[:, g * 3: g * 3 + 3])
                    v.tensor_scalar(out=ctmp[:, (g % 2) * T:(g % 2) * T + T],
                                    in0=x_in[:, base: base + T],
                                    scalar1=convw[:, 4 * g: 4 * g + 1], scalar2=None, op0=OP.mult)
                    for j in (1, 2, 3):
                        inst = v.scalar_tensor_tensor(
                            out=ctmp[:, (g % 2) * T:(g % 2) * T + T],
                            in0=x_in[:, base + j: base + j + T],
                            scalar=convw[:, 4 * g + j: 4 * g + j + 1],
                            in1=ctmp[:, (g % 2) * T:(g % 2) * T + T],
                            op0=OP.mult, op1=OP.add)
                    inst = v.tensor_copy(hist[:, g * 3: g * 3 + 3], x_in[:, base + T: base + T + 3])
                    return inst
                cw = [f"ipev_{ci}_{g}", "consts", "aneg"]
                if g >= 2:
                    cw.append(f"xc_{ci}_{g - 2}")
                S.add("vector", fconv, waits=cw, tick="dv", event=f"cv_{ci}_{g}")
                sw = [f"cv_{ci}_{g}"]
                if ci >= 2:
                    sw.append(f"yq_{ci - 2}_3")
                S.add("scalar", lambda A, g=g, cof=cof: A.activation(
                    out=xc[:, cof + g * T: cof + (g + 1) * T],
                    in_=ctmp[:, (g % 2) * T:(g % 2) * T + T],
                    func=AF.Silu, bias=convb[:, g: g + 1]),
                    waits=sw, tick="ex", event=f"xc_{ci}_{g}")
            def fxp(t, cof=cof):
                for kt in range(KI):
                    inst = t.matmul(pp[0][0:96, 0:T], w_xp[:, kt * 96:(kt + 1) * 96],
                                    xc[:, cof + kt * T: cof + kt * T + T],
                                    start=(kt == 0), stop=(kt == KI - 1))
                return inst
            S.add("tensor", fxp, waits=[f"xc_{ci}_{KI - 1}", "wxp", f"ipev_{ci}_{2 * KI - 2}"],
                  tick="pe", event=f"xpmm_{ci}")
            def fxe(v):
                v.tensor_copy(xdbl_b[:], pp[0][0:64, 0:T])
                return v.tensor_copy(bc_sb[:], pp[0][64:96, 0:T])
            S.add("vector", fxe, waits=[f"xpmm_{ci}"], tick="dv", event=f"xde_{ci}")
            S.add("scalar", lambda A: A.dma_start(out=bc_dram[0:32, 0:T], in_=bc_sb[:]),
                  waits=[f"xde_{ci}"], tick="sd", dma=True, event=f"bcst_{ci}")
            bw = [f"bcst_{ci}"]
            if ci > 0:
                bw.append(f"brc_{ci - 1}_15")
            S.add("sync", lambda s: s.dma_start(
                out=bcflat[:], in_=bc_dram[0:32, 0:T].rearrange("r t -> (r t)").unsqueeze(0)),
                waits=bw, tick="ld", dma=True, event=f"bcf_{ci}")
            for mt in range(KI):
                bank = pp[mt % 2]
                bw = [f"xde_{ci}", "wdtp"]
                if mt >= 2:
                    bw.append(f"sp_{ci}_{mt - 2}")
                elif mt == 0:
                    bw.append(f"xde_{ci}")
                else:
                    bw.append(f"ipev_{ci}_{2 * KI - 1}")
                S.add("tensor", lambda t, mt=mt, bank=bank: t.matmul(
                    bank[:, 0:T], w_dtp[:, mt * 128:(mt + 1) * 128], xdbl_b[:],
                    start=True, stop=True),
                    waits=bw, tick="pe", event=f"dtmm_{ci}_{mt}")
                spw = [f"dtmm_{ci}_{mt}", f"cv_{ci}_{KI - 1}"]
                if ci > 0:
                    spw.append(f"ag_{ci - 1}_63")
                def fsp(A, mt=mt, bank=bank):
                    A.activation(out=ctmp[:, (mt % 2) * T:(mt % 2) * T + T], in_=bank[:, 0:T],
                                 func=AF.Exp, bias=dtb[:, mt:mt + 1])
                    return A.activation(out=dt[:, mt * T:(mt + 1) * T],
                                        in_=ctmp[:, (mt % 2) * T:(mt % 2) * T + T],
                                        func=AF.Ln, bias=1.0)
                S.add("scalar", fsp, waits=spw, tick="ex", event=f"sp_{ci}_{mt}")
            for s_ in range(DS):
                sp2 = s_ % 2
                bw = [f"bcf_{ci}"]
                if s_ >= 2:
                    bw.append(f"bev_{ci}_{s_ - 2}")
                elif ci > 0:
                    bw.append(f"bev_{ci - 1}_{14 + s_}")
                def fbc(t, s_=s_, sp2=sp2):
                    bcast_matmul(t, pbc[sp2][:, 0:T], bcflat[0:1, s_ * T:(s_ + 1) * T], T, C.ones_row_b)
                    return t.matmul(pbc[sp2][:, T:2 * T], C.ones_row_b[0:1, 0:128],
                                    bcflat[0:1, (16 + s_) * T:(17 + s_) * T], start=True, stop=True)
                S.add("tensor", fbc, waits=bw, tick="pe", event=f"brc_{ci}_{s_}")
                ev = [f"brc_{ci}_{s_}"]
                if ci > 0:
                    ev.append(f"hc_{ci - 1}_{48 + s_}")
                S.add("scalar", lambda A, s_=s_, sp2=sp2: A.copy(
                    brc[:, s_ * 2 * T:(s_ + 1) * 2 * T], pbc[sp2][:, 0:2 * T]),
                    waits=ev, tick="ex", event=f"bev_{ci}_{s_}")
            if ci + 1 < NCH:
                emit_inproj(ci + 1, 0, 4)

            for q in range(4):
                S.add("vector", lambda v, q=q, cof=cof: v.tensor_tensor(
                    out=wdt[:, q * 4 * T:(q + 1) * 4 * T],
                    in0=dt[:, q * 4 * T:(q + 1) * 4 * T],
                    in1=xc[:, cof + q * 4 * T: cof + (q + 1) * 4 * T], op=OP.mult),
                    waits=[f"sp_{ci}_{q * 4 + 3}", f"xc_{ci}_{q * 4 + 3}"],
                    tick="dv", event=f"w_{ci}_{q}")
                for s_ in range(DS):
                    j = q * 16 + s_
                    pr = j % 4
                    aw = [f"sp_{ci}_{q * 4 + 3}", "aneg"]
                    if j >= 4:
                        aw.append(f"scn_{ci}_{j - 4}")
                    elif ci > 0:
                        aw.append(f"scn_{ci - 1}_{60 + j}")
                    S.add("gpsimd", lambda g_, q=q, s_=s_, pr=pr: g_.tensor_tensor(
                        out=argq[:, pr * 4 * WS:(pr + 1) * 4 * WS]
                            .rearrange("p (g w) -> p g w", g=4)[:, :, 2:WS],
                        in0=dt[:, q * 4 * T:(q + 1) * 4 * T].rearrange("p (g t) -> p g t", g=4),
                        in1=aneg3[:, q * 4:(q + 1) * 4, s_:s_ + 1].broadcast_to([128, 4, T]),
                        op=OP.mult),
                        waits=aw, tick="pl", event=f"ag_{ci}_{j}")
                    S.add("scalar", lambda A, pr=pr: A.activation(
                        out=argq[:, pr * 4 * WS:(pr + 1) * 4 * WS]
                            .rearrange("p (g w) -> p g w", g=4)[:, :, 2:WS],
                        in_=argq[:, pr * 4 * WS:(pr + 1) * 4 * WS]
                            .rearrange("p (g w) -> p g w", g=4)[:, :, 2:WS], func=AF.Exp),
                        waits=[f"ag_{ci}_{j}"], tick="ex", event=f"dA_{ci}_{j}")
                    dr = j % 2
                    def fdbu(v, q=q, s_=s_, dr=dr):
                        v.tensor_copy(
                            dbu[:, dr * 4 * WS:(dr + 1) * 4 * WS]
                                .rearrange("p (g w) -> p g w", g=4)[:, :, 0],
                            hcarry[:, s_ * 16 + q * 4: s_ * 16 + q * 4 + 4])
                        return v.tensor_tensor(
                            out=dbu[:, dr * 4 * WS:(dr + 1) * 4 * WS]
                                .rearrange("p (g w) -> p g w", g=4)[:, :, 2:WS],
                            in0=wdt[:, q * 4 * T:(q + 1) * 4 * T].rearrange("p (g t) -> p g t", g=4),
                            in1=brc[:, s_ * 2 * T: s_ * 2 * T + T].unsqueeze(1).broadcast_to([128, 4, T]),
                            op=OP.mult)
                    S.add("vector", fdbu, waits=[f"bev_{ci}_{s_}"], tick="dv")
                    S.add("vector", lambda v, pr=pr, j=j: v.tensor_tensor_scan(
                        out=h8q[:],
                        data0=argq[:, pr * 4 * WS:(pr + 1) * 4 * WS],
                        data1=dbu[:, (j % 2) * 4 * WS:(j % 2 + 1) * 4 * WS],
                        initial=0.0, op0=OP.mult, op1=OP.add),
                        waits=[f"dA_{ci}_{j}"], tick="dv", event=f"scn_{ci}_{j}")
                    hr = j % 2
                    hw = []
                    if j >= 2:
                        hw.append(f"am_{ci}_{j - 2}")
                    elif ci > 0:
                        hw.append(f"am_{ci - 1}_{62 + j}")
                    def fschc(v, q=q, s_=s_, hr=hr):
                        v.tensor_copy(
                            hcarry[:, s_ * 16 + q * 4: s_ * 16 + q * 4 + 4],
                            h8q[:].rearrange("p (g w) -> p g w", g=4)[:, :, WS - 1])
                        return v.tensor_tensor(
                            out=hct[:, hr * 4 * T:(hr + 1) * 4 * T].rearrange("p (g t) -> p g t", g=4),
                            in0=h8q[:].rearrange("p (g w) -> p g w", g=4)[:, :, 2:WS],
                            in1=brc[:, s_ * 2 * T + T: s_ * 2 * T + 2 * T].unsqueeze(1)
                                .broadcast_to([128, 4, T]),
                            op=OP.mult)
                    S.add("vector", fschc, waits=hw, tick="dv", event=f"hc_{ci}_{j}")
                    amw = [f"hc_{ci}_{j}"]
                    if s_ == 0:
                        if ci * 4 + q >= 2:
                            pq = (ci * 4 + q) - 2
                            amw.append(f"yq_{pq // 4}_{pq % 4}")
                    def fam(t, q=q, s_=s_, hr=hr):
                        t.matmul(py[q % 2][:, 0:512], C.ident_b[:],
                                 hct[:, hr * 4 * T: hr * 4 * T + 512],
                                 start=(s_ == 0), stop=(s_ == DS - 1))
                        return t.matmul(py[q % 2][:, 512:1024], C.ident_b[:],
                                        hct[:, hr * 4 * T + 512: hr * 4 * T + 1024],
                                        start=(s_ == 0), stop=(s_ == DS - 1))
                    S.add("tensor", fam, waits=amw, tick="pe", event=f"am_{ci}_{j}")
                    if s_ == 7 and ci + 1 < NCH:
                        emit_inproj(ci + 1, 4 * (2 * q + 1), 4 * (2 * q + 2))
                for gg in range(4):
                    g = q * 4 + gg
                    yw = [f"am_{ci}_{q * 16 + 15}", f"ipev_{ci}_{2 * KI - 1}"]
                    if ci * 4 + q >= 2:
                        pq = (ci * 4 + q) - 2
                        yw.append(f"ygst_{pq // 4}_{pq % 4}")
                    def fyq(v, q=q, g=g, gg=gg, cof=cof):
                        v.scalar_tensor_tensor(
                            out=ytmp[:], in0=xc[:, cof + g * T: cof + (g + 1) * T],
                            scalar=dpar[:, g:g + 1],
                            in1=py[q % 2][:, gg * T:(gg + 1) * T],
                            op0=OP.mult, op1=OP.add)
                        return v.tensor_tensor(
                            out=ygq[:, (q % 2) * 4 * T + gg * T: (q % 2) * 4 * T + (gg + 1) * T],
                            in0=ytmp[:], in1=sz[:, cof + g * T: cof + (g + 1) * T], op=OP.mult)
                    S.add("vector", fyq, waits=yw, tick="dv",
                          event=(f"yq_{ci}_{q}" if gg == 3 else None))
                S.add("sync", lambda s, ci=ci, q=q: s.dma_start(
                    out=yg_dram.rearrange("(a p) t -> p a t", p=128)[:, q * 4:(q + 1) * 4,
                                                                    ci * T:(ci + 1) * T],
                    in_=ygq[:, (q % 2) * 4 * T:(q % 2 + 1) * 4 * T].rearrange("p (g t) -> p g t", g=4)),
                    waits=[f"yq_{ci}_{q}"], tick="ld", dma=True, event=f"ygst_{ci}_{q}")
                if ci + 1 < NCH and q < 3:
                    emit_inproj(ci + 1, 4 * (2 * q + 2), 4 * (2 * q + 3))
        S.run()

def stage_mamba_b(p, C, W, yg_dram, resid_f, dst_f, dst_b, T=512, name="mb"):
    nc = p.nc
    NCH = L // T
    with ExitStack() as st:
        w_out = st.enter_context(nc.sbuf_tensor(f"{name}_wo", [128, KI * D], BF16))
        w_m1 = st.enter_context(nc.sbuf_tensor(f"{name}_w1", [128, KD * DI], BF16))
        w_m2 = st.enter_context(nc.sbuf_tensor(f"{name}_w2", [128, KI * D], BF16))
        b1T = st.enter_context(nc.sbuf_tensor(f"{name}_b1", [128, 16], F32))
        b2T = st.enter_context(nc.sbuf_tensor(f"{name}_b2", [128, 8], F32))
        lnwT = st.enter_context(nc.sbuf_tensor(f"{name}_lw", [128, 8], F32))
        yg = st.enter_context(nc.sbuf_tensor(f"{name}_yg", [128, KI * T], BF16))
        mo = st.enter_context(nc.sbuf_tensor(f"{name}_mo", [128, KD * T], BF16))
        h1 = st.enter_context(nc.sbuf_tensor(f"{name}_h1", [128, KI * T], BF16))
        mout = st.enter_context(nc.sbuf_tensor(f"{name}_mt", [128, KD * T], F32))
        resid = st.enter_context(nc.sbuf_tensor(f"{name}_rs", [128, KD * T], F32))
        sq = st.enter_context(nc.sbuf_tensor(f"{name}_sq", [128, KD * T], F32))
        stats = st.enter_context(nc.sbuf_tensor(f"{name}_st", [1, T], F32))
        pp = [st.enter_context(nc.psum_tensor(f"{name}_p{i}", [128, T], F32)) for i in range(2)]
        psS = st.enter_context(nc.psum_tensor(f"{name}_ps", [1, T], F32))
        psR = st.enter_context(nc.psum_tensor(f"{name}_pr", [128, T], F32))

        S = Sched(p, name)
        S.add("gpsimd", lambda g: g.dma_start(out=w_out[:].rearrange("p (a m) -> p a m", a=KI),
                                              in_=W["outT"].rearrange("(a p) m -> p a m", p=128)),
              tick="wl", dma=True, event="wo")
        S.add("gpsimd", lambda g: g.dma_start(out=w_m1[:].rearrange("p (a m) -> p a m", a=KD),
                                              in_=W["m1T"].rearrange("(a p) m -> p a m", p=128)),
              tick="wl", dma=True, event="w1")
        S.add("gpsimd", lambda g: g.dma_start(out=w_m2[:].rearrange("p (a m) -> p a m", a=KI),
                                              in_=W["m2T"].rearrange("(a p) m -> p a m", p=128)),
              tick="wl", dma=True, event="w2")
        S.add("sync", lambda s: s.dma_start(out=b1T[:], in_=W["b1T"]), tick="ld", dma=True)
        S.add("sync", lambda s: s.dma_start(out=b2T[:], in_=W["b2T"]), tick="ld", dma=True)
        S.add("sync", lambda s: s.dma_start(out=lnwT[:], in_=W["lnwT"]),
              tick="ld", dma=True, event="consts")

        last_bank_ev = [None, None]
        for ci in range(NCH):
            S.add("sync", lambda s, ci=ci: s.dma_start(
                out=sb3(yg[:], KI, T), in_=feat_slice(yg_dram, KI, ci, T)),
                waits=([f"opmm_{ci - 1}_{KD - 1}"] if ci > 0 else []),
                tick="ld", dma=True, event=f"ygld_{ci}")
            S.add("sync", lambda s, ci=ci: s.dma_start(
                out=sb3(resid[:], KD, T), in_=feat_slice(resid_f, KD, ci, T)),
                waits=([f"add_{ci - 1}"] if ci > 0 else []),
                tick="ld", dma=True, event=f"rsld_{ci}")
            for mt in range(KD):
                b = mt % 2
                bw = [f"ygld_{ci}", "wo"] + ([last_bank_ev[b]] if last_bank_ev[b] else [])
                def fmm(t, mt=mt, b=b):
                    for kt in range(KI):
                        inst = t.matmul(pp[b][:, 0:T],
                                        w_out[:, kt * D + mt * 128: kt * D + mt * 128 + 128],
                                        yg[:, kt * T:(kt + 1) * T], start=(kt == 0), stop=(kt == KI - 1))
                    return inst
                S.add("tensor", fmm, waits=bw, tick="pe", event=f"opmm_{ci}_{mt}")
                ew = [f"opmm_{ci}_{mt}"] + ([f"m1_{ci - 1}_{KI - 1}"] if ci > 0 else [])
                S.add("vector", lambda v, mt=mt, b=b: v.tensor_copy(
                    mo[:, mt * T:(mt + 1) * T], pp[b][:, 0:T]),
                    waits=ew, tick="dv", event=f"moev_{ci}_{mt}")
                last_bank_ev[b] = f"moev_{ci}_{mt}"
            for et in range(KI):
                b = et % 2
                bw = [f"moev_{ci}_{KD - 1}", "w1", last_bank_ev[b]]
                def fmm(t, et=et, b=b):
                    for kt in range(KD):
                        inst = t.matmul(pp[b][:, 0:T],
                                        w_m1[:, kt * DI + et * 128: kt * DI + et * 128 + 128],
                                        mo[:, kt * T:(kt + 1) * T], start=(kt == 0), stop=(kt == KD - 1))
                    return inst
                S.add("tensor", fmm, waits=bw, tick="pe", event=f"m1_{ci}_{et}")
                ew = [f"m1_{ci}_{et}", "consts"] + ([f"m2_{ci - 1}_{KD - 1}"] if ci > 0 and et < 2 else [])
                S.add("scalar", lambda A, et=et, b=b: A.activation(
                    out=h1[:, et * T:(et + 1) * T], in_=pp[b][:, 0:T],
                    func=AF.Gelu, bias=b1T[:, et:et + 1]),
                    waits=ew, tick="ex", event=f"h1_{ci}_{et}")
                last_bank_ev[b] = f"h1_{ci}_{et}"
            for mt in range(KD):
                b = mt % 2
                bw = [f"h1_{ci}_{KI - 1}", "w2", last_bank_ev[b]]
                def fmm(t, mt=mt, b=b):
                    for kt in range(KI):
                        inst = t.matmul(pp[b][:, 0:T],
                                        w_m2[:, kt * D + mt * 128: kt * D + mt * 128 + 128],
                                        h1[:, kt * T:(kt + 1) * T], start=(kt == 0), stop=(kt == KI - 1))
                    return inst
                S.add("tensor", fmm, waits=bw, tick="pe", event=f"m2_{ci}_{mt}")
                ew = [f"m2_{ci}_{mt}"] + ([f"nrm_{ci - 1}"] if ci > 0 and mt < 2 else [])
                S.add("scalar", lambda A, mt=mt, b=b: A.activation(
                    out=mout[:, mt * T:(mt + 1) * T], in_=pp[b][:, 0:T],
                    func=AF.Gelu, bias=b2T[:, mt:mt + 1]),
                    waits=ew, tick="ex", event=f"mt_{ci}_{mt}")
                last_bank_ev[b] = f"mt_{ci}_{mt}"
            S.add("vector", lambda v: v.tensor_tensor(out=mout[:], in0=mout[:], in1=resid[:], op=OP.add),
                  waits=[f"mt_{ci}_{KD - 1}", f"rsld_{ci}"], tick="dv", event=f"add_{ci}")
            sqw = [f"add_{ci}"] + ([f"sums_{ci - 1}", f"stf_{ci - 1}", f"stb_{ci - 1}"] if ci > 0 else [])
            S.add("scalar", lambda A: A.activation(out=sq[:], in_=mout[:], func=AF.Square),
                  waits=sqw, tick="ex", event=f"sq_{ci}")
            S.add("tensor", lambda t: sum_matmuls(p, t, psS, sq[:], KD, T, C.ones_col),
                  waits=[f"sq_{ci}"] + ([f"rstd_{ci - 1}"] if ci > 0 else []),
                  tick="pe", event=f"sums_{ci}")
            S.add("scalar", lambda A: rstd_from_ms(p, A, stats[0:1, 0:T], psS[0:1, 0:T], 1.0 / D, EPS_RMS),
                  waits=[f"sums_{ci}"], tick="ex", event=f"rstd_{ci}")
            S.add("tensor", lambda t: bcast_matmul(t, psR, stats[0:1, 0:T], T, C.ones_row),
                  waits=[f"rstd_{ci}"] + ([f"nrm_{ci - 1}"] if ci > 0 else []),
                  tick="pe", event=f"bc_{ci}")
            def fnorm(v):
                for a in range(KD):
                    sl = slice(a * T, (a + 1) * T)
                    inst = v.scalar_tensor_tensor(out=sq[:, sl], in0=mout[:, sl],
                                                  scalar=lnwT[:, a:a + 1], in1=psR[:, 0:T],
                                                  op0=OP.mult, op1=OP.mult)
                return inst
            S.add("vector", fnorm, waits=[f"bc_{ci}"] +
                  ([f"stf_{ci - 1}", f"stb_{ci - 1}"] if ci > 0 else []),
                  tick="dv", event=f"nrm_{ci}")
            if dst_f is not None:
                S.add("gpsimd", lambda g, ci=ci: g.dma_start(
                    out=feat_slice(dst_f, KD, ci, T), in_=sb3(sq[:], KD, T)),
                    waits=[f"nrm_{ci}"], tick="wl", dma=True, event=f"stf_{ci}")
            else:
                S.events[f"stf_{ci}"] = S.events[f"nrm_{ci}"]
            if dst_b is not None:
                S.add("gpsimd", lambda g, ci=ci: g.dma_start(
                    out=feat_slice(dst_b, KD, ci, T), in_=sb3(sq[:], KD, T)),
                    waits=[f"nrm_{ci}"], tick="wl", dma=True, event=f"stb_{ci}")
            else:
                S.events[f"stb_{ci}"] = S.events[f"nrm_{ci}"]
        S.run()

def stage_mamba_tail(p, C, ln2wT_dram, xcur_f, u1_f, x0_f, dst_f, T=512, name="mt"):
    nc = p.nc
    NCH = L // T
    with ExitStack() as st:
        lnwT = st.enter_context(nc.sbuf_tensor(f"{name}_lw", [128, 8], F32))
        a1 = st.enter_context(nc.sbuf_tensor(f"{name}_a", [128, KD * T], F32))
        a2 = st.enter_context(nc.sbuf_tensor(f"{name}_b", [128, KD * T], F32))
        sq = st.enter_context(nc.sbuf_tensor(f"{name}_sq", [128, KD * T], F32))
        stats = st.enter_context(nc.sbuf_tensor(f"{name}_st", [1, 4 * T], F32))
        psA = st.enter_context(nc.psum_tensor(f"{name}_pA", [1, T], F32))
        psB = st.enter_context(nc.psum_tensor(f"{name}_pB", [1, T], F32))
        psR = st.enter_context(nc.psum_tensor(f"{name}_pR", [128, T], F32))
        psM = st.enter_context(nc.psum_tensor(f"{name}_pM", [128, T], F32))
        p.ph("sync", lambda p, s: p.dma(s.dma_start(out=lnwT[:], in_=ln2wT_dram)))
        for ci in range(NCH):
            def ld(p, s, ci=ci):
                load_chunk(p, s, a1, xcur_f, KD, ci, T)
                load_chunk(p, s, a2, u1_f, KD, ci, T)
            p.ph("sync", ld)
            p.ph("vector", lambda p, v: p.inc(v.tensor_tensor(out=a1[:], in0=a1[:], in1=a2[:], op=OP.add)))
            p.ph("scalar", lambda p, A: p.inc(A.activation(out=sq[:], in_=a1[:], func=AF.Square)))
            p.ph("tensor", lambda p, t: p.inc(sum_matmuls(p, t, psA, sq[:], KD, T, C.ones_col)))
            p.ph("scalar", lambda p, A: p.inc(rstd_from_ms(p, A, stats[0:1, 0:T], psA[0:1, 0:T], 1.0 / D, EPS_RMS)))
            p.ph("tensor", lambda p, t: p.inc(bcast_matmul(t, psR, stats[0:1, 0:T], T, C.ones_row)))
            def fbr(p, v):
                for a in range(KD):
                    sl = slice(a * T, (a + 1) * T)
                    inst = v.scalar_tensor_tensor(out=a1[:, sl], in0=a1[:, sl],
                                                  scalar=lnwT[:, a:a + 1], in1=psR[:, 0:T],
                                                  op0=OP.mult, op1=OP.mult)
                p.inc(inst)
            p.ph("vector", fbr)
            p.ph("scalar", lambda p, A: p.inc(A.activation(out=sq[:], in_=a1[:], func=AF.Square)))
            p.ph("tensor", lambda p, t: p.inc(sum_matmuls(p, t, psA, a1[:], KD, T, C.ones_col)))
            p.ph("tensor", lambda p, t: p.inc(sum_matmuls(p, t, psB, sq[:], KD, T, C.ones_col)))
            def fstats(p, v):
                v.tensor_scalar(out=stats[0:1, 0:T], in0=psA[0:1, 0:T], scalar1=1.0 / D,
                                scalar2=None, op0=OP.mult)
                v.tensor_tensor(out=stats[0:1, T:2 * T], in0=stats[0:1, 0:T],
                                in1=stats[0:1, 0:T], op=OP.mult)
                p.inc(v.scalar_tensor_tensor(out=stats[0:1, T:2 * T], in0=psB[0:1, 0:T],
                                             scalar=1.0 / D, in1=stats[0:1, T:2 * T],
                                             op0=OP.mult, op1=OP.subtract))
            p.ph("vector", fstats)
            p.ph("scalar", lambda p, A: p.inc(
                rstd_from_ms(p, A, stats[0:1, 2 * T:3 * T], stats[0:1, T:2 * T], 1.0, EPS_LN)))
            p.ph("vector", lambda p, v: p.inc(
                v.tensor_tensor(out=stats[0:1, 3 * T:4 * T], in0=stats[0:1, 0:T],
                                in1=stats[0:1, 2 * T:3 * T], op=OP.mult)))
            def fb(p, t):
                bcast_matmul(t, psR, stats[0:1, 2 * T:3 * T], T, C.ones_row)
                p.inc(bcast_matmul(t, psM, stats[0:1, 3 * T:4 * T], T, C.ones_row))
            p.ph("tensor", fb)
            p.ph("sync", lambda p, s, ci=ci: load_chunk(p, s, a2, x0_f, KD, ci, T))
            def fnorm(p, v):
                for a in range(KD):
                    sl = slice(a * T, (a + 1) * T)
                    v.tensor_tensor(out=sq[:, sl], in0=a1[:, sl], in1=psR[:, 0:T], op=OP.mult)
                    v.tensor_tensor(out=sq[:, sl], in0=sq[:, sl], in1=psM[:, 0:T], op=OP.subtract)
                    inst = v.tensor_tensor(out=sq[:, sl], in0=sq[:, sl], in1=a2[:, sl], op=OP.add)
                p.inc(inst)
            p.ph("vector", fnorm)
            p.ph("sync", lambda p, s, ci=ci: store_chunk(p, s, dst_f, sq, KD, ci, T))


def _t(a):
    return np.ascontiguousarray(np.asarray(a).T)

def _cols(a, k):
    return np.ascontiguousarray(np.asarray(a).reshape(k, 128).T)


def build_masks():
    m = np.zeros((128, 4 * 512), dtype=ml_dtypes.bfloat16)
    for j in range(4):
        pp = np.arange(128)[:, None] + 128 * j
        ff = np.arange(512)[None, :]
        m[:, j * 512:(j + 1) * 512] = (pp <= ff).astype(np.float32)
    return m


def stage_moe_pre(p, C, W, x1_f, u2_b, comb_dram, T=512, name="mp"):
    nc = p.nc
    NCH = L // T
    with ExitStack() as st:
        gw = st.enter_context(nc.sbuf_tensor(f"{name}_gw", [128, KD * 8], BF16))
        gbr = st.enter_context(nc.sbuf_tensor(f"{name}_gb", [128, 8], F32))
        x = st.enter_context(nc.sbuf_tensor(f"{name}_x", [128, KD * T], F32))
        xb = st.enter_context(nc.sbuf_tensor(f"{name}_xb", [128, KD * T], BF16))
        sq = st.enter_context(nc.sbuf_tensor(f"{name}_sq", [128, KD * T], F32))
        stats = st.enter_context(nc.sbuf_tensor(f"{name}_st", [1, 4 * T], F32))
        gsc = st.enter_context(nc.sbuf_tensor(f"{name}_gs", [128, 64], F32))
        comb = st.enter_context(nc.sbuf_tensor(f"{name}_cb", [8, T], F32))
        psA = st.enter_context(nc.psum_tensor(f"{name}_pA", [1, T], F32))
        psB = st.enter_context(nc.psum_tensor(f"{name}_pB", [1, T], F32))
        psR = st.enter_context(nc.psum_tensor(f"{name}_pR", [128, T], F32))
        psM = st.enter_context(nc.psum_tensor(f"{name}_pM", [128, T], F32))
        psG = st.enter_context(nc.psum_tensor(f"{name}_pG", [128, 8], F32))
        psT = st.enter_context(nc.psum_tensor(f"{name}_pT", [8, 128], F32))

        def lw(p, g):
            p.dma(g.dma_start(out=gw[:].rearrange("p (a m) -> p a m", a=KD),
                              in_=W["gate_w"].rearrange("(a p) m -> p a m", p=128)))
        p.ph("gpsimd", lw)
        p.ph("sync", lambda p, s: p.dma(s.dma_start(out=gbr[:], in_=W["gate_b_rep"])))

        for ci in range(NCH):
            p.ph("sync", lambda p, s, ci=ci: load_chunk(p, s, x, x1_f, KD, ci, T))
            p.ph("scalar", lambda p, A: p.inc(A.activation(out=sq[:], in_=x[:], func=AF.Square)))
            p.ph("tensor", lambda p, t: p.inc(sum_matmuls(p, t, psA, x[:], KD, T, C.ones_col)))
            p.ph("tensor", lambda p, t: p.inc(sum_matmuls(p, t, psB, sq[:], KD, T, C.ones_col)))
            def fstats(p, v):
                v.tensor_scalar(out=stats[0:1, 0:T], in0=psA[0:1, 0:T], scalar1=1.0 / D,
                                scalar2=None, op0=OP.mult)
                v.tensor_tensor(out=stats[0:1, T:2 * T], in0=stats[0:1, 0:T],
                                in1=stats[0:1, 0:T], op=OP.mult)
                p.inc(v.scalar_tensor_tensor(out=stats[0:1, T:2 * T], in0=psB[0:1, 0:T],
                                             scalar=1.0 / D, in1=stats[0:1, T:2 * T],
                                             op0=OP.mult, op1=OP.subtract))
            p.ph("vector", fstats)
            p.ph("scalar", lambda p, A: p.inc(
                rstd_from_ms(p, A, stats[0:1, 2 * T:3 * T], stats[0:1, T:2 * T], 1.0, EPS_LN)))
            p.ph("vector", lambda p, v: p.inc(
                v.tensor_tensor(out=stats[0:1, 3 * T:4 * T], in0=stats[0:1, 0:T],
                                in1=stats[0:1, 2 * T:3 * T], op=OP.mult)))
            def fb(p, t):
                bcast_matmul(t, psR, stats[0:1, 2 * T:3 * T], T, C.ones_row)
                p.inc(bcast_matmul(t, psM, stats[0:1, 3 * T:4 * T], T, C.ones_row))
            p.ph("tensor", fb)
            def fnorm(p, v):
                for a in range(KD):
                    sl = slice(a * T, (a + 1) * T)
                    v.tensor_tensor(out=sq[:, sl], in0=x[:, sl], in1=psR[:, 0:T], op=OP.mult)
                    v.tensor_tensor(out=sq[:, sl], in0=sq[:, sl], in1=psM[:, 0:T], op=OP.subtract)
                    inst = v.tensor_copy(xb[:, sl], sq[:, sl])
                p.inc(inst)
            p.ph("vector", fnorm)
            p.ph("gpsimd", lambda p, g, ci=ci: p.dma(
                g.dma_start(out=feat_slice(u2_b, KD, ci, T), in_=sb3(sq[:], KD, T))))

            for ts in range(T // 128):
                def fg(p, t, ts=ts):
                    for kt in range(KD):
                        inst = t.matmul(psG[:, 0:8], xb[:, kt * T + ts * 128: kt * T + ts * 128 + 128],
                                        gw[:, kt * 8:(kt + 1) * 8], start=(kt == 0), stop=(kt == KD - 1))
                    p.inc(inst)
                p.ph("tensor", fg)
                g0 = gsc[:, 0:8]; m1 = gsc[:, 8:16]; m2 = gsc[:, 16:24]; t0 = gsc[:, 24:32]
                mx = gsc[:, 32:33]; v2 = gsc[:, 33:34]; den = gsc[:, 34:35]
                nv2 = gsc[:, 35:36]; r = gsc[:, 36:37]; c2 = gsc[:, 37:38]
                p.ph("vector", lambda p, v: p.inc(
                    v.tensor_tensor(out=g0, in0=psG[:, 0:8], in1=gbr[:], op=OP.add)))
                p.ph("vector", lambda p, v: p.inc(v.reduce_max(mx, g0, axis=AX.X, negate=True)))
                p.ph("scalar", lambda p, A: p.inc(
                    A.activation(out=g0, in_=g0, func=AF.Exp, bias=mx)))
                p.ph("scalar", lambda p, A: p.inc(
                    A.activation(out=m1, in_=g0, func=AF.Sign, bias=-1.0)))
                p.ph("vector", lambda p, v: p.inc(
                    v.tensor_scalar(out=m1, in0=m1, scalar1=1.0, scalar2=None, op0=OP.add)))
                p.ph("vector", lambda p, v: p.inc(
                    v.scalar_tensor_tensor(out=t0, in0=m1, scalar=-2.0, in1=g0,
                                           op0=OP.mult, op1=OP.add)))
                def fred(p, v):
                    v.reduce_max(v2, t0, axis=AX.X)
                    p.inc(v.reduce_max(nv2, t0, axis=AX.X, negate=True))
                p.ph("vector", fred)
                p.ph("scalar", lambda p, A: p.inc(
                    A.activation(out=m2, in_=t0, func=AF.Sign, bias=nv2)))
                def fcw1(p, v):
                    v.tensor_scalar(out=m2, in0=m2, scalar1=1.0, scalar2=None, op0=OP.add)
                    p.inc(v.tensor_scalar(out=den, in0=v2, scalar1=1.0, scalar2=None, op0=OP.add))
                p.ph("vector", fcw1)
                p.ph("vector", lambda p, v: p.inc(v.reciprocal(out=r, in_=den)))
                def fcw2(p, v):
                    v.tensor_tensor(out=c2, in0=v2, in1=r, op=OP.mult)
                    p.inc(v.tensor_tensor(out=m1, in0=m1, in1=r.broadcast_to([128, 8]), op=OP.mult))
                p.ph("vector", fcw2)
                p.ph("vector", lambda p, v: p.inc(
                    v.tensor_tensor(out=m2, in0=m2, in1=c2.broadcast_to([128, 8]), op=OP.mult)))
                p.ph("vector", lambda p, v: p.inc(
                    v.tensor_tensor(out=gsc[:, 40:48], in0=m1, in1=m2, op=OP.add)))
                p.ph("tensor", lambda p, t, ts=ts: p.inc(
                    t.transpose(psT[:], gsc[:, 40:48], C.ident_f[:])))
                p.ph("vector", lambda p, v, ts=ts: p.inc(
                    v.tensor_copy(comb[:, ts * 128:(ts + 1) * 128], psT[0:8, 0:128])))
            p.ph("sync", lambda p, s, ci=ci: p.dma(s.dma_start(
                out=comb_dram[ci * 8 * T:(ci + 1) * 8 * T].rearrange("(r t) -> r t", r=8),
                in_=comb[:])))


def stage_moe(p, C, W, u2_b, comb_dram, acc_f, T=512, name="moe"):
    nc = p.nc
    NCH = L // T
    with ExitStack() as st:
        w1 = st.enter_context(nc.sbuf_tensor(f"{name}_w1", [128, KD * FF], BF16))
        w2 = st.enter_context(nc.sbuf_tensor(f"{name}_w2", [128, KF * D], BF16))
        b1T = st.enter_context(nc.sbuf_tensor(f"{name}_b1", [128, NE * KF], F32))
        b2m = st.enter_context(nc.sbuf_tensor(f"{name}_b2", [8, D], BF16))
        xb = st.enter_context(nc.sbuf_tensor(f"{name}_xb", [128, 2 * KD * T], BF16))
        h = st.enter_context(nc.sbuf_tensor(f"{name}_h", [128, KF * T], BF16))
        cb8 = st.enter_context(nc.sbuf_tensor(f"{name}_c8", [8, T], BF16))
        cbe = st.enter_context(nc.sbuf_tensor(f"{name}_cf", [1, 2 * T], F32))
        ceb = st.enter_context(nc.sbuf_tensor(f"{name}_ce", [128, T], F32))
        contrib = st.enter_context(nc.sbuf_tensor(f"{name}_ct", [128, KD * T], F32))
        pm = [st.enter_context(nc.psum_tensor(f"{name}_p{i}", [128, T], F32)) for i in range(4)]
        psc = st.enter_context(nc.psum_tensor(f"{name}_pc", [128, T], F32))

        S = Sched(p, name)
        S.add("sync", lambda s: s.dma_start(out=b1T[:].rearrange("p (e m) -> p e m", e=NE),
                                            in_=W["moe_b1T"]), tick="ld", dma=True)
        S.add("gpsimd", lambda g_: g_.dma_start(out=b2m[:], in_=W["moe_b2"]),
              tick="wl", dma=True, event="b2m")

        def wload(e, which):
            def f(g, e=e, which=which):
                if which == 0:
                    return g.dma_start(out=w1[:].rearrange("p (a m) -> p a m", a=KD),
                                       in_=W["moe_w1"][e].rearrange("(a p) m -> p a m", p=128))
                return g.dma_start(out=w2[:].rearrange("p (a m) -> p a m", a=KF),
                                   in_=W["moe_w2"][e].rearrange("(a p) m -> p a m", p=128))
            return f
        S.add("gpsimd", wload(0, 0), tick="wl", dma=True, event="w1_0")
        S.add("gpsimd", wload(0, 1), tick="wl", dma=True, event="w2_0")

        for ci in range(NCH):
            S.add("gpsimd", lambda g_, ci=ci: g_.dma_start(
                out=cb8[:], in_=comb_dram[ci * 8 * T:(ci + 1) * 8 * T].rearrange("(r t) -> r t", r=8)),
                waits=([("st", 16 * ci)] if ci > 0 else []),
                tick="wl", dma=True, event=f"cb8_{ci}")
            for mt in range(KD):
                g = ci * KD + mt
                bank = pm[mt % 2]
                S.add("tensor", lambda t, mt=mt, bank=bank: t.matmul(
                    bank[:, 0:T], b2m[:, mt * 128:(mt + 1) * 128], cb8[:], start=True, stop=True),
                    waits=[f"cb8_{ci}", "b2m"] + ([("dv", g - 1)] if g >= 2 else []),
                    tick="pe", event=f"bmm_{ci}_{mt}")
                S.add("vector", lambda v, mt=mt, bank=bank: v.tensor_copy(
                    contrib[:, mt * T:(mt + 1) * T], bank[:, 0:T]),
                    waits=[f"bmm_{ci}_{mt}"] + ([("st", 16 * ci)] if ci > 0 else []),
                    tick="dv", event=f"bcp_{ci}_{mt}")
            S.add("gpsimd", lambda g_, ci=ci: g_.dma_start(
                out=feat_slice(acc_f, KD, ci, T), in_=sb3(contrib[:], KD, T)),
                waits=[f"bcp_{ci}_{KD - 1}"], tick="st", dma=True, event=f"bst_{ci}")
        bias_dv = S.ctr("dv").n

        for e in range(NE):
            for ci in range(NCH):
                u = e * NCH + ci
                xs = (u % 2) * KD * T
                cs = (u % 2) * T
                lw = [f"mm1_{u - 2}_{KF - 1}"] if u >= 2 else []
                S.add("sync", lambda s, ci=ci, xs=xs: s.dma_start(
                    out=sb3(xb[:, xs:xs + KD * T], KD, T), in_=feat_slice(u2_b, KD, ci, T)),
                    waits=lw, tick="ld", dma=True)
                S.add("sync", lambda s, ci=ci, e=e, cs=cs: s.dma_start(
                    out=cbe[0:1, cs:cs + T],
                    in_=comb_dram[(ci * 8 + e) * T:(ci * 8 + e + 1) * T].unsqueeze(0)),
                    tick="ld", dma=True, event=f"ld_{u}")
                S.add("tensor", lambda t, cs=cs: bcast_matmul(t, psc, cbe[0:1, cs:cs + T], T, C.ones_row),
                      waits=[f"ld_{u}"] + ([f"cecp_{u - 1}"] if u > 0 else []),
                      tick="pe", event=f"cemm_{u}")
                S.add("vector", lambda v: v.tensor_copy(ceb[:], psc[:, 0:T]),
                      waits=[f"cemm_{u}"] + ([f"ct_{u - 1}_{KD - 1}"] if u > 0 else []),
                      tick="dv", event=f"cecp_{u}")
                for et in range(KF):
                    bank = pm[et % 2]
                    def f_mm1(t, et=et, bank=bank, xs=xs):
                        for kt in range(KD):
                            inst = t.matmul(bank[:, 0:T],
                                            w1[:, kt * FF + et * 128: kt * FF + et * 128 + 128],
                                            xb[:, xs + kt * T: xs + kt * T + T],
                                            start=(kt == 0), stop=(kt == KD - 1))
                        return inst
                    wts = [f"ld_{u}", f"w1_{e}"]
                    if et >= 2:
                        wts.append(f"h_{u}_{et - 2}")
                    elif u > 0:
                        wts.append(f"h_{u - 1}_{KF - 2 + et}")
                    else:
                        wts.append(("dv", bias_dv - 2 + et + 1))
                    S.add("tensor", f_mm1, waits=wts, tick="pe", event=f"mm1_{u}_{et}")
                    rw = [f"mm1_{u}_{et}"]
                    if u > 0:
                        rw.append(f"mm2_{u - 1}_{KD - 1}")
                    S.add("scalar", lambda A, et=et, e=e, bank=bank: A.activation(
                        out=h[:, et * T:(et + 1) * T], in_=bank[:, 0:T],
                        func=AF.Relu, bias=b1T[:, e * KF + et: e * KF + et + 1]),
                        waits=rw, tick="ex", event=f"h_{u}_{et}")
                if ci == NCH - 1 and e < NE - 1:
                    S.add("gpsimd", wload(e + 1, 0), waits=[f"mm1_{u}_{KF - 1}"],
                          tick="wl", dma=True, event=f"w1_{e + 1}")
                for mt in range(KD):
                    bank = pm[2 + mt % 2]
                    def f_mm2(t, mt=mt, bank=bank):
                        for kt in range(KF):
                            inst = t.matmul(bank[:, 0:T],
                                            w2[:, kt * D + mt * 128: kt * D + mt * 128 + 128],
                                            h[:, kt * T:(kt + 1) * T],
                                            start=(kt == 0), stop=(kt == KF - 1))
                        return inst
                    wts = [f"h_{u}_{KF - 1}", f"w2_{e}"]
                    if mt >= 2:
                        wts.append(f"ct_{u}_{mt - 2}")
                    elif u > 0:
                        wts.append(f"ct_{u - 1}_{KD - 2 + mt}")
                    S.add("tensor", f_mm2, waits=wts, tick="pe", event=f"mm2_{u}_{mt}")
                    cw = [f"mm2_{u}_{mt}", f"cecp_{u}"]
                    if mt == 0:
                        cw.append(f"acc_{u - 1}" if u > 0 else f"bst_{NCH - 1}")
                    S.add("vector", lambda v, mt=mt: v.tensor_tensor(
                        out=contrib[:, mt * T:(mt + 1) * T], in0=pm[2 + mt % 2][:, 0:T],
                        in1=ceb[:], op=OP.mult),
                        waits=cw, tick="dv", event=f"ct_{u}_{mt}")
                S.add("gpsimd", lambda g_, ci=ci: g_.dma_start(
                    out=feat_slice(acc_f, KD, ci, T), in_=sb3(contrib[:], KD, T), accum_op=OP.add),
                    waits=[f"ct_{u}_{KD - 1}"], tick="st", dma=True, event=f"acc_{u}")
                if ci == NCH - 1 and e < NE - 1:
                    S.add("gpsimd", wload(e + 1, 1), waits=[f"mm2_{u}_{KD - 1}"],
                          tick="wl", dma=True, event=f"w2_{e + 1}")
        S.run()


def stage_moe_old(p, C, W, u2_b, comb_dram, acc_f, T=512, name="moeo"):
    nc = p.nc
    NCH = L // T
    with ExitStack() as st:
        w1 = st.enter_context(nc.sbuf_tensor(f"{name}_w1", [128, KD * FF], BF16))
        w2 = st.enter_context(nc.sbuf_tensor(f"{name}_w2", [128, KF * D], BF16))
        b1T = st.enter_context(nc.sbuf_tensor(f"{name}_b1", [128, NE * KF], F32))
        b2m = st.enter_context(nc.sbuf_tensor(f"{name}_b2", [8, D], F32))
        xb = st.enter_context(nc.sbuf_tensor(f"{name}_xb", [128, KD * T], BF16))
        h = st.enter_context(nc.sbuf_tensor(f"{name}_h", [128, KF * T], BF16))
        cb8 = st.enter_context(nc.sbuf_tensor(f"{name}_c8", [8, T], F32))
        cbe = st.enter_context(nc.sbuf_tensor(f"{name}_cf", [1, T], F32))
        ceb = st.enter_context(nc.sbuf_tensor(f"{name}_ce", [128, T], F32))
        contrib = st.enter_context(nc.sbuf_tensor(f"{name}_ct", [128, KD * T], F32))
        ps0 = st.enter_context(nc.psum_tensor(f"{name}_p0", [128, T], F32))
        ps1 = st.enter_context(nc.psum_tensor(f"{name}_p1", [128, T], F32))
        psc = st.enter_context(nc.psum_tensor(f"{name}_pc", [128, T], F32))
        psx = [ps0, ps1]

        def lb(p, s):
            p.dma(s.dma_start(out=b1T[:].rearrange("p (e m) -> p e m", e=NE),
                              in_=W["moe_b1T"]))
            p.dma(s.dma_start(out=b2m[:], in_=W["moe_b2"]))
        p.ph("sync", lb)

        for ci in range(NCH):
            def lcm(p, s, ci=ci):
                p.dma(s.dma_start(out=cb8[:], in_=comb_dram[ci * 8 * T:(ci + 1) * 8 * T]
                                  .rearrange("(r t) -> r t", r=8)))
            p.ph("sync", lcm)
            for mt in range(KD):
                bank = psx[mt % 2]
                p.ph("tensor", lambda p, t, mt=mt, bank=bank: p.inc(
                    t.matmul(bank[:, 0:T], b2m[:, mt * 128:(mt + 1) * 128], cb8[:],
                             start=True, stop=True)))
                p.ph("vector", lambda p, v, mt=mt, bank=bank: p.inc(
                    v.tensor_copy(contrib[:, mt * T:(mt + 1) * T], bank[:, 0:T])))
            p.ph("sync", lambda p, s, ci=ci: store_chunk(p, s, acc_f, contrib, KD, ci, T))

        for e in range(NE):
            def lw(p, g, e=e):
                p.dma(g.dma_start(out=w1[:].rearrange("p (a m) -> p a m", a=KD),
                                  in_=W["moe_w1"][e].rearrange("(a p) m -> p a m", p=128)))
                p.dma(g.dma_start(out=w2[:].rearrange("p (a m) -> p a m", a=KF),
                                  in_=W["moe_w2"][e].rearrange("(a p) m -> p a m", p=128)))
            p.ph("gpsimd", lw)
            for ci in range(NCH):
                def lx(p, s, ci=ci, e=e):
                    load_chunk(p, s, xb, u2_b, KD, ci, T)
                    p.dma(s.dma_start(out=cbe[:], in_=comb_dram[(ci * 8 + e) * T:(ci * 8 + e + 1) * T]
                                      .unsqueeze(0)))
                p.ph("sync", lx)
                p.ph("tensor", lambda p, t: p.inc(
                    bcast_matmul(t, psc, cbe[0:1, 0:T], T, C.ones_row)))
                p.ph("vector", lambda p, v: p.inc(v.tensor_copy(ceb[:], psc[:, 0:T])))
                for et in range(KF):
                    bank = psx[et % 2]
                    def fmm(p, t, et=et, bank=bank):
                        for kt in range(KD):
                            inst = t.matmul(bank[:, 0:T],
                                            w1[:, kt * FF + et * 128: kt * FF + et * 128 + 128],
                                            xb[:, kt * T:(kt + 1) * T],
                                            start=(kt == 0), stop=(kt == KD - 1))
                        p.inc(inst)
                    p.ph("tensor", fmm)
                    p.ph("scalar", lambda p, A, et=et, e=e, bank=bank: p.inc(
                        A.activation(out=h[:, et * T:(et + 1) * T], in_=bank[:, 0:T],
                                     func=AF.Relu, bias=b1T[:, e * KF + et: e * KF + et + 1])))
                for mt in range(KD):
                    bank = psx[mt % 2]
                    def fmm(p, t, mt=mt, bank=bank):
                        for kt in range(KF):
                            inst = t.matmul(bank[:, 0:T],
                                            w2[:, kt * D + mt * 128: kt * D + mt * 128 + 128],
                                            h[:, kt * T:(kt + 1) * T],
                                            start=(kt == 0), stop=(kt == KF - 1))
                        p.inc(inst)
                    p.ph("tensor", fmm)
                    p.ph("vector", lambda p, v, mt=mt, bank=bank: p.inc(
                        v.tensor_tensor(out=contrib[:, mt * T:(mt + 1) * T], in0=bank[:, 0:T],
                                        in1=ceb[:], op=OP.mult)))
                p.ph("gpsimd", lambda p, g, ci=ci: p.dma(
                    g.dma_start(out=feat_slice(acc_f, KD, ci, T), in_=sb3(contrib[:], KD, T),
                                accum_op=OP.add)))


def stage_add_ln(p, C, src_f, base_f, dst_f, T=512, name="al"):
    nc = p.nc
    NCH = L // T
    with ExitStack() as st:
        a1 = st.enter_context(nc.sbuf_tensor(f"{name}_a", [128, KD * T], F32))
        a2 = st.enter_context(nc.sbuf_tensor(f"{name}_b", [128, KD * T], F32))
        sq = st.enter_context(nc.sbuf_tensor(f"{name}_sq", [128, KD * T], F32))
        stats = st.enter_context(nc.sbuf_tensor(f"{name}_st", [1, 4 * T], F32))
        psA = st.enter_context(nc.psum_tensor(f"{name}_pA", [1, T], F32))
        psB = st.enter_context(nc.psum_tensor(f"{name}_pB", [1, T], F32))
        psR = st.enter_context(nc.psum_tensor(f"{name}_pR", [128, T], F32))
        psM = st.enter_context(nc.psum_tensor(f"{name}_pM", [128, T], F32))
        for ci in range(NCH):
            p.ph("sync", lambda p, s, ci=ci: load_chunk(p, s, a1, src_f, KD, ci, T))
            p.ph("scalar", lambda p, A: p.inc(A.activation(out=sq[:], in_=a1[:], func=AF.Square)))
            p.ph("tensor", lambda p, t: p.inc(sum_matmuls(p, t, psA, a1[:], KD, T, C.ones_col)))
            p.ph("tensor", lambda p, t: p.inc(sum_matmuls(p, t, psB, sq[:], KD, T, C.ones_col)))
            def fstats(p, v):
                v.tensor_scalar(out=stats[0:1, 0:T], in0=psA[0:1, 0:T], scalar1=1.0 / D,
                                scalar2=None, op0=OP.mult)
                v.tensor_tensor(out=stats[0:1, T:2 * T], in0=stats[0:1, 0:T],
                                in1=stats[0:1, 0:T], op=OP.mult)
                p.inc(v.scalar_tensor_tensor(out=stats[0:1, T:2 * T], in0=psB[0:1, 0:T],
                                             scalar=1.0 / D, in1=stats[0:1, T:2 * T],
                                             op0=OP.mult, op1=OP.subtract))
            p.ph("vector", fstats)
            p.ph("scalar", lambda p, A: p.inc(
                rstd_from_ms(p, A, stats[0:1, 2 * T:3 * T], stats[0:1, T:2 * T], 1.0, EPS_LN)))
            p.ph("vector", lambda p, v: p.inc(
                v.tensor_tensor(out=stats[0:1, 3 * T:4 * T], in0=stats[0:1, 0:T],
                                in1=stats[0:1, 2 * T:3 * T], op=OP.mult)))
            def fb(p, t):
                bcast_matmul(t, psR, stats[0:1, 2 * T:3 * T], T, C.ones_row)
                p.inc(bcast_matmul(t, psM, stats[0:1, 3 * T:4 * T], T, C.ones_row))
            p.ph("tensor", fb)
            p.ph("sync", lambda p, s, ci=ci: load_chunk(p, s, a2, base_f, KD, ci, T))
            def fnorm(p, v):
                for a in range(KD):
                    sl = slice(a * T, (a + 1) * T)
                    v.tensor_tensor(out=sq[:, sl], in0=a1[:, sl], in1=psR[:, 0:T], op=OP.mult)
                    v.tensor_tensor(out=sq[:, sl], in0=sq[:, sl], in1=psM[:, 0:T], op=OP.subtract)
                    inst = v.tensor_tensor(out=sq[:, sl], in0=sq[:, sl], in1=a2[:, sl], op=OP.add)
                p.inc(inst)
            p.ph("vector", fnorm)
            p.ph("sync", lambda p, s, ci=ci: store_chunk(p, s, dst_f, sq, KD, ci, T))


def stage_attn_qkv(p, C, W, x2_f, qkv_dram, vtm_dram, T=512, name="aq"):
    nc = p.nc
    NCH = L // T
    KQ = 3 * D // 128
    with ExitStack() as st:
        wa = st.enter_context(nc.sbuf_tensor(f"{name}_wa", [128, KD * 3 * D], BF16))
        baT = st.enter_context(nc.sbuf_tensor(f"{name}_ba", [128, KQ], F32))
        x = st.enter_context(nc.sbuf_tensor(f"{name}_x", [128, KD * T], F32))
        xb = st.enter_context(nc.sbuf_tensor(f"{name}_xb", [128, KD * T], BF16))
        sq = st.enter_context(nc.sbuf_tensor(f"{name}_sq", [128, KD * T], F32))
        stats = st.enter_context(nc.sbuf_tensor(f"{name}_st", [1, 4 * T], F32))
        qkv = st.enter_context(nc.sbuf_tensor(f"{name}_q", [128, KQ * T], BF16))
        vt = st.enter_context(nc.sbuf_tensor(f"{name}_vt", [128, 8 * 128], BF16))
        psA = st.enter_context(nc.psum_tensor(f"{name}_pA", [1, T], F32))
        psB = st.enter_context(nc.psum_tensor(f"{name}_pB", [1, T], F32))
        psR = st.enter_context(nc.psum_tensor(f"{name}_pR", [128, T], F32))
        psM = st.enter_context(nc.psum_tensor(f"{name}_pM", [128, T], F32))
        ps0 = st.enter_context(nc.psum_tensor(f"{name}_p0", [128, T], F32))
        ps1 = st.enter_context(nc.psum_tensor(f"{name}_p1", [128, T], F32))
        psT = st.enter_context(nc.psum_tensor(f"{name}_pT", [128, 128], BF16))
        psx = [ps0, ps1]

        def lw(p, g):
            p.dma(g.dma_start(out=wa[:].rearrange("p (a m) -> p a m", a=KD),
                              in_=W["W_attn"].rearrange("(a p) m -> p a m", p=128)))
        p.ph("gpsimd", lw)
        p.ph("sync", lambda p, s: p.dma(s.dma_start(out=baT[:], in_=W["battnT"])))

        for ci in range(NCH):
            p.ph("sync", lambda p, s, ci=ci: load_chunk(p, s, x, x2_f, KD, ci, T))
            p.ph("scalar", lambda p, A: p.inc(A.activation(out=sq[:], in_=x[:], func=AF.Square)))
            p.ph("tensor", lambda p, t: p.inc(sum_matmuls(p, t, psA, x[:], KD, T, C.ones_col)))
            p.ph("tensor", lambda p, t: p.inc(sum_matmuls(p, t, psB, sq[:], KD, T, C.ones_col)))
            def fstats(p, v):
                v.tensor_scalar(out=stats[0:1, 0:T], in0=psA[0:1, 0:T], scalar1=1.0 / D,
                                scalar2=None, op0=OP.mult)
                v.tensor_tensor(out=stats[0:1, T:2 * T], in0=stats[0:1, 0:T],
                                in1=stats[0:1, 0:T], op=OP.mult)
                p.inc(v.scalar_tensor_tensor(out=stats[0:1, T:2 * T], in0=psB[0:1, 0:T],
                                             scalar=1.0 / D, in1=stats[0:1, T:2 * T],
                                             op0=OP.mult, op1=OP.subtract))
            p.ph("vector", fstats)
            p.ph("scalar", lambda p, A: p.inc(
                rstd_from_ms(p, A, stats[0:1, 2 * T:3 * T], stats[0:1, T:2 * T], 1.0, EPS_LN)))
            p.ph("vector", lambda p, v: p.inc(
                v.tensor_tensor(out=stats[0:1, 3 * T:4 * T], in0=stats[0:1, 0:T],
                                in1=stats[0:1, 2 * T:3 * T], op=OP.mult)))
            def fb(p, t):
                bcast_matmul(t, psR, stats[0:1, 2 * T:3 * T], T, C.ones_row)
                p.inc(bcast_matmul(t, psM, stats[0:1, 3 * T:4 * T], T, C.ones_row))
            p.ph("tensor", fb)
            def fnorm(p, v):
                for a in range(KD):
                    sl = slice(a * T, (a + 1) * T)
                    v.tensor_tensor(out=sq[:, sl], in0=x[:, sl], in1=psR[:, 0:T], op=OP.mult)
                    v.tensor_tensor(out=sq[:, sl], in0=sq[:, sl], in1=psM[:, 0:T], op=OP.subtract)
                    inst = v.tensor_copy(xb[:, sl], sq[:, sl])
                p.inc(inst)
            p.ph("vector", fnorm)
            for mt in range(KQ):
                bank = psx[mt % 2]
                def fmm(p, t, mt=mt, bank=bank):
                    for kt in range(KD):
                        inst = t.matmul(bank[:, 0:T], wa[:, kt * 3 * D + mt * 128: kt * 3 * D + mt * 128 + 128],
                                        xb[:, kt * T:(kt + 1) * T], start=(kt == 0), stop=(kt == KD - 1))
                    p.inc(inst)
                p.ph("tensor", fmm)
                p.ph("scalar", lambda p, A, mt=mt, bank=bank: p.inc(
                    A.activation(out=qkv[:, mt * T:(mt + 1) * T], in_=bank[:, 0:T],
                                 func=AF.Identity, bias=baT[:, mt:mt + 1])))
            p.ph("sync", lambda p, s, ci=ci: p.dma(s.dma_start(
                out=qkv_dram.rearrange("(a p) t -> p a t", p=128)[:, :, ci * T:(ci + 1) * T],
                in_=sb3(qkv[:], KQ, T))))
            for tsub in range(T // 128):
                for mt in range(8):
                    p.ph("tensor", lambda p, t, mt=mt, tsub=tsub: p.inc(
                        t.transpose(psT[:], qkv[:, (16 + mt) * T + tsub * 128:
                                                 (16 + mt) * T + tsub * 128 + 128], C.ident_b[:])))
                    p.ph("vector", lambda p, v, mt=mt: p.inc(
                        v.tensor_copy(vt[:, mt * 128:(mt + 1) * 128], psT[:])))
                p.ph("sync", lambda p, s, ci=ci, tsub=tsub: p.dma(s.dma_start(
                    out=vtm_dram[ci * T + tsub * 128: ci * T + tsub * 128 + 128, :],
                    in_=vt[:])))


def stage_attn_core(p, C, W, qkv_dram, vtm_dram, x2_f, dst_f, T=512, name="ac"):
    nc = p.nc
    NCH = L // T
    NTK = L // 128
    scale = 1.0 / 8.0
    with ExitStack() as st:
        kT = st.enter_context(nc.sbuf_tensor(f"{name}_k", [64, L], BF16))
        qT = st.enter_context(nc.sbuf_tensor(f"{name}_q", [64, L], BF16))
        vtm = st.enter_context(nc.sbuf_tensor(f"{name}_v", [128, NTK * 65], BF16))
        esb = st.enter_context(nc.sbuf_tensor(f"{name}_e", [128, 2 * T], BF16))
        osb = st.enter_context(nc.sbuf_tensor(f"{name}_o", [64, T], F32))
        rden = st.enter_context(nc.sbuf_tensor(f"{name}_r", [1, T], F32))
        att = st.enter_context(nc.sbuf_tensor(f"{name}_at", [128, KD * L], BF16))

        S = Sched(p, name)
        with ExitStack() as ps_head:
            psS = [ps_head.enter_context(nc.psum_tensor(f"{name}_pS{i}", [128, T], F32)) for i in range(2)]
            psO = [ps_head.enter_context(nc.psum_tensor(f"{name}_pO{i}", [65, T], F32)) for i in range(2)]
            psD = ps_head.enter_context(nc.psum_tensor(f"{name}_pD", [64, T], F32))
            prev_ex = None
            prev_av = None
            prev_fn = None
            prev_fn2 = {}
            for h_ in range(NH):
                hw = [f"sl_{h_ - 1}"] if h_ > 0 else []
                S.add("sync", lambda s, h_=h_: s.dma_start(
                    out=qT[:], in_=qkv_dram[h_ * 64:(h_ + 1) * 64, :]),
                    waits=hw, tick="ld", dma=True)
                S.add("sync", lambda s, h_=h_: s.dma_start(
                    out=kT[:], in_=qkv_dram[D + h_ * 64: D + (h_ + 1) * 64, :]),
                    tick="ld", dma=True)
                vw = [f"avl_{h_ - 1}"] if h_ > 0 else []
                S.add("vector", lambda v: v.memset(vtm[:], 1.0), waits=vw,
                      tick="dv", event=f"vm_{h_}")
                S.add("sync", lambda s, h_=h_: s.dma_start(
                    out=vtm[:].rearrange("p (a d) -> p a d", d=65)[:, :, 0:64],
                    in_=vtm_dram.rearrange("(a p) d -> p a d", p=128)[:, :, h_ * 64:(h_ + 1) * 64]),
                    waits=[f"vm_{h_}"], tick="ld", dma=True, event=f"ldv_{h_}")
                for ci in range(NCH):
                    ntk = (ci + 1) * (T // 128)
                    unit_ex = []
                    unit_av = []
                    for tk in range(ntk + 1):
                        if tk < ntk:
                            sw = [f"ldv_{h_}"]
                            if tk >= 2:
                                sw.append(f"ex_{h_}_{ci}_{tk - 2}")
                            elif prev_ex is not None:
                                sw.append(prev_ex)
                            S.add("tensor", lambda t, h_=h_, ci=ci, tk=tk: t.matmul(
                                psS[tk % 2][:, 0:T], kT[:, tk * 128:(tk + 1) * 128],
                                qT[:, ci * T:(ci + 1) * T], start=True, stop=True),
                                waits=sw, tick="pe",
                                event=(f"sl_{h_}" if ci == NCH - 1 and tk == ntk - 1 else f"s_{h_}_{ci}_{tk}"))
                        if tk >= 1:
                            tkp = tk - 1
                            diag = tkp - ci * (T // 128)
                            sev = (f"sl_{h_}" if ci == NCH - 1 and tkp == ntk - 1 else f"s_{h_}_{ci}_{tkp}")
                            ew = [sev]
                            if tkp >= 2:
                                ew.append(unit_av[tkp - 2])
                            elif prev_av is not None:
                                ew.append(prev_av)
                            S.add("scalar", lambda A, tkp=tkp: A.activation(
                                out=esb[:, (tkp % 2) * T:(tkp % 2 + 1) * T], in_=psS[tkp % 2][:, 0:T],
                                func=AF.Exp, scale=scale),
                                waits=ew, tick="ex", event=f"ex_{h_}_{ci}_{tkp}")
                            unit_ex.append(f"ex_{h_}_{ci}_{tkp}")
                            mev = f"ex_{h_}_{ci}_{tkp}"
                            if diag >= 0:
                                S.add("vector", lambda v, tkp=tkp, diag=diag: v.tensor_tensor(
                                    out=esb[:, (tkp % 2) * T:(tkp % 2 + 1) * T],
                                    in0=esb[:, (tkp % 2) * T:(tkp % 2 + 1) * T],
                                    in1=C.masks[:, diag * 512: diag * 512 + T], op=OP.mult),
                                    waits=[mev], tick="dv", event=f"mk_{h_}_{ci}_{tkp}")
                                mev = f"mk_{h_}_{ci}_{tkp}"
                            aw = [mev, f"ldv_{h_}"]
                            if tkp == 0 and (ci % 2) in prev_fn2:
                                aw.append(prev_fn2[ci % 2])
                            avev = (f"avl_{h_}" if ci == NCH - 1 and tkp == ntk - 1 else f"av_{h_}_{ci}_{tkp}")
                            S.add("tensor", lambda t, h_=h_, ci=ci, tkp=tkp, ntk=ntk: t.matmul(
                                psO[ci % 2][:, 0:T], vtm[:, tkp * 65:(tkp + 1) * 65],
                                esb[:, (tkp % 2) * T:(tkp % 2 + 1) * T],
                                start=(tkp == 0), stop=(tkp == ntk - 1)),
                                waits=aw, tick="pe", event=avev)
                            unit_av.append(avev)
                    avev = unit_av[-1]
                    S.add("vector", lambda v, ci=ci: v.reciprocal(out=rden[:], in_=psO[ci % 2][64:65, 0:T]),
                          waits=[avev], tick="dv", event=f"rc_{h_}_{ci}")
                    dbw = [f"rc_{h_}_{ci}"] + ([prev_fn] if prev_fn else [])
                    S.add("tensor", lambda t: bcast_matmul(t, psD, rden[0:1, 0:T], T, C.ones_row, m=64),
                          waits=dbw, tick="pe", event=f"db_{h_}_{ci}")
                    obw = [avev] + ([prev_fn] if prev_fn else [])
                    S.add("scalar", lambda A, ci=ci: A.copy(osb[:], psO[ci % 2][0:64, 0:T]),
                          waits=obw, tick="ex", event=f"ob_{h_}_{ci}")
                    def fnr(v, h_=h_, ci=ci):
                        row0 = (h_ % 2) * 64
                        return v.tensor_tensor(
                            out=att[row0:row0 + 64, (h_ // 2) * L + ci * T: (h_ // 2) * L + ci * T + T],
                            in0=osb[:], in1=psD[:, 0:T], op=OP.mult)
                    S.add("vector", fnr, waits=[f"db_{h_}_{ci}", f"ob_{h_}_{ci}"],
                          tick="dv", event=f"fn_{h_}_{ci}")
                    prev_fn2[ci % 2] = f"fn_{h_}_{ci}"
                    prev_fn = f"fn_{h_}_{ci}"
                    prev_ex = unit_ex[-1]
                    prev_av = unit_av[-1]
            S.events["fn_prev"] = S.events[prev_fn]
        wp = st.enter_context(nc.sbuf_tensor(f"{name}_wp", [128, KD * D], BF16))
        bpT = st.enter_context(nc.sbuf_tensor(f"{name}_bp", [128, 8], F32))
        br = st.enter_context(nc.sbuf_tensor(f"{name}_br", [128, KD * T], F32))
        x2c = st.enter_context(nc.sbuf_tensor(f"{name}_x2", [128, KD * T], F32))
        sq = st.enter_context(nc.sbuf_tensor(f"{name}_sq", [128, KD * T], F32))
        stats = st.enter_context(nc.sbuf_tensor(f"{name}_st", [1, 4 * T], F32))
        with ExitStack() as ps2:
            pp = [ps2.enter_context(nc.psum_tensor(f"{name}_pp{i}", [128, T], F32)) for i in range(2)]
            psA = ps2.enter_context(nc.psum_tensor(f"{name}_pA", [1, T], F32))
            psB = ps2.enter_context(nc.psum_tensor(f"{name}_pB", [1, T], F32))
            psR = ps2.enter_context(nc.psum_tensor(f"{name}_pR", [128, T], F32))
            psM = ps2.enter_context(nc.psum_tensor(f"{name}_pM", [128, T], F32))
            S.add("gpsimd", lambda g: g.dma_start(out=wp[:].rearrange("p (a m) -> p a m", a=KD),
                                                  in_=W["W_proj"].rearrange("(a p) m -> p a m", p=128)),
                  tick="wl", dma=True, event="wp")
            S.add("sync", lambda s: s.dma_start(out=bpT[:], in_=W["bprojT"]),
                  tick="ld", dma=True, event="bp")
            lastb = [None, None]
            for ci in range(NCH):
                S.add("sync", lambda s, ci=ci: s.dma_start(
                    out=sb3(x2c[:], KD, T), in_=feat_slice(x2_f, KD, ci, T)),
                    waits=([f"fn3_{ci - 1}"] if ci > 0 else []), tick="ld", dma=True, event=f"x2_{ci}")
                for mt in range(KD):
                    b = mt % 2
                    bw = [f"avl_{NH - 1}", "fn_prev", "wp"] + ([lastb[b]] if lastb[b] else [])
                    def fmm(t, mt=mt, ci=ci, b=b):
                        for kt in range(KD):
                            inst = t.matmul(pp[b][:, 0:T], wp[:, kt * D + mt * 128: kt * D + mt * 128 + 128],
                                            att[:, kt * L + ci * T: kt * L + ci * T + T],
                                            start=(kt == 0), stop=(kt == KD - 1))
                        return inst
                    S.add("tensor", fmm, waits=bw, tick="pe", event=f"pj_{ci}_{mt}")
                    S.add("scalar", lambda A, mt=mt, b=b: A.activation(
                        out=br[:, mt * T:(mt + 1) * T], in_=pp[b][:, 0:T],
                        func=AF.Identity, bias=bpT[:, mt:mt + 1]),
                        waits=[f"pj_{ci}_{mt}", "bp"] + ([f"fn3_{ci - 1}"] if ci > 0 and mt == 0 else []),
                        tick="ex", event=f"bre_{ci}_{mt}")
                    lastb[b] = f"bre_{ci}_{mt}"
                S.add("scalar", lambda A: A.activation(out=sq[:], in_=br[:], func=AF.Square),
                      waits=[f"bre_{ci}_{KD - 1}"] + ([f"st3_{ci - 1}"] if ci > 0 else []),
                      tick="ex", event=f"sq2_{ci}")
                def fsums(t):
                    sum_matmuls(p, t, psA, br[:], KD, T, C.ones_col)
                    return sum_matmuls(p, t, psB, sq[:], KD, T, C.ones_col)
                S.add("tensor", fsums, waits=[f"sq2_{ci}"] + ([f"rst_{ci - 1}"] if ci > 0 else []),
                      tick="pe", event=f"sm_{ci}")
                def fstats(v):
                    v.tensor_scalar(out=stats[0:1, 0:T], in0=psA[0:1, 0:T], scalar1=1.0 / D,
                                    scalar2=None, op0=OP.mult)
                    v.tensor_tensor(out=stats[0:1, T:2 * T], in0=stats[0:1, 0:T],
                                    in1=stats[0:1, 0:T], op=OP.mult)
                    return v.scalar_tensor_tensor(out=stats[0:1, T:2 * T], in0=psB[0:1, 0:T],
                                                  scalar=1.0 / D, in1=stats[0:1, T:2 * T],
                                                  op0=OP.mult, op1=OP.subtract)
                S.add("vector", fstats, waits=[f"sm_{ci}"], tick="dv", event=f"vst_{ci}")
                S.add("scalar", lambda A: rstd_from_ms(p, A, stats[0:1, 2 * T:3 * T],
                                                       stats[0:1, T:2 * T], 1.0, EPS_LN),
                      waits=[f"vst_{ci}"], tick="ex", event=f"rst_{ci}")
                S.add("vector", lambda v: v.tensor_tensor(out=stats[0:1, 3 * T:4 * T],
                                                          in0=stats[0:1, 0:T],
                                                          in1=stats[0:1, 2 * T:3 * T], op=OP.mult),
                      waits=[f"rst_{ci}"], tick="dv", event=f"mrs_{ci}")
                def fbb(t):
                    bcast_matmul(t, psR, stats[0:1, 2 * T:3 * T], T, C.ones_row)
                    return bcast_matmul(t, psM, stats[0:1, 3 * T:4 * T], T, C.ones_row)
                S.add("tensor", fbb, waits=[f"mrs_{ci}"] + ([f"fn3_{ci - 1}"] if ci > 0 else []),
                      tick="pe", event=f"bb_{ci}")
                def fnorm(v):
                    for a in range(KD):
                        sl = slice(a * T, (a + 1) * T)
                        v.tensor_tensor(out=sq[:, sl], in0=br[:, sl], in1=psR[:, 0:T], op=OP.mult)
                        v.tensor_tensor(out=sq[:, sl], in0=sq[:, sl], in1=psM[:, 0:T], op=OP.subtract)
                        inst = v.tensor_tensor(out=sq[:, sl], in0=sq[:, sl], in1=x2c[:, sl], op=OP.add)
                    return inst
                S.add("vector", fnorm, waits=[f"bb_{ci}", f"x2_{ci}"], tick="dv", event=f"fn3_{ci}")
                S.add("gpsimd", lambda g, ci=ci: g.dma_start(
                    out=feat_slice(dst_f, KD, ci, T), in_=sb3(sq[:], KD, T)),
                    waits=[f"fn3_{ci}"], tick="wl", dma=True, event=f"st3_{ci}")
        S.run()

def stage_ffn(p, C, W, x3_f, y_out, T=256, name="ff"):
    nc = p.nc
    NCH = L // T
    with ExitStack() as st:
        w1 = st.enter_context(nc.sbuf_tensor(f"{name}_w1", [128, KD * FF], BF16))
        w2 = st.enter_context(nc.sbuf_tensor(f"{name}_w2", [128, KF * D], BF16))
        b1T = st.enter_context(nc.sbuf_tensor(f"{name}_b1", [128, KF], F32))
        hb2 = st.enter_context(nc.sbuf_tensor(f"{name}_b2", [128, 8], F32))
        gamT = st.enter_context(nc.sbuf_tensor(f"{name}_g", [128, 8], F32))
        betT = st.enter_context(nc.sbuf_tensor(f"{name}_be", [128, 8], F32))

        S = Sched(p, name)
        S.add("sync", lambda s: s.dma_start(out=b1T[:], in_=W["fc1bT"]), tick="ld", dma=True)
        S.add("sync", lambda s: s.dma_start(out=hb2[:], in_=W["fc2bT"]), tick="ld", dma=True)
        S.add("sync", lambda s: s.dma_start(out=gamT[:], in_=W["gamT"]), tick="ld", dma=True)
        S.add("sync", lambda s: s.dma_start(out=betT[:], in_=W["betT"]),
              tick="ld", dma=True, event="consts")
        S.add("vector", lambda v: v.tensor_scalar(out=hb2[:], in0=hb2[:], scalar1=0.5,
                                                  scalar2=None, op0=OP.mult),
              waits=["consts"], tick="dv", event="hb2")

        with ExitStack() as dq:
            qf = dq.enter_context(nc.sbuf_tensor(f"{name}_qf", [128, 2 * FF], F32))
            srow = dq.enter_context(nc.sbuf_tensor(f"{name}_sr", [1, 2 * FF], F32))
            psc = dq.enter_context(nc.psum_tensor(f"{name}_pc", [128, 512], F32))
            psc2 = dq.enter_context(nc.psum_tensor(f"{name}_pc2", [128, 512], F32))
            units = [("fc1_q", "fc1_scales", FF, 8) , ("fc2_q", "fc2_scales", D, 32)]
            gidx = 0
            for (qk, sk, N, NG) in units:
                for g in range(NG):
                    u = gidx; gidx += 1
                    par = u % 2
                    lw = [f"dqs_{u - 2}"] if u >= 2 else []
                    S.add("gpsimd", lambda gp, qk=qk, g=g, par=par, N=N: gp.dma_start(
                        out=qf[:, par * FF: par * FF + N], in_=W[qk][g * 128:(g + 1) * 128, :]),
                        waits=lw, tick="wl", dma=True, event=f"dqq_{u}")
                    S.add("sync", lambda s, sk=sk, g=g, par=par, N=N: s.dma_start(
                        out=srow[0:1, par * FF: par * FF + N], in_=W[sk][g:g + 1, :]),
                        waits=lw, tick="ld", dma=True, event=f"dqr_{u}")
                    for ns in range(N // 512):
                        bank = psc if ns % 2 == 0 else psc2
                        bw = [f"dqr_{u}"]
                        if ns >= 2:
                            bw.append(f"dqs_{u}_{ns - 2}")
                        elif u >= 1:
                            bw.append(f"dqs_{u - 1}")
                        S.add("tensor", lambda t, par=par, ns=ns, bank=bank: bcast_matmul(
                            t, bank, srow[0:1, par * FF + ns * 512: par * FF + (ns + 1) * 512],
                            512, C.ones_row),
                            waits=bw, tick="pe", event=f"dqb_{u}_{ns}")
                        wdst = w1 if qk == "fc1_q" else w2
                        S.add("vector", lambda v, qk=qk, g=g, ns=ns, par=par, N=N, wdst=wdst, bank=bank:
                              v.scalar_tensor_tensor(
                                  out=wdst[:, g * N + ns * 512: g * N + (ns + 1) * 512],
                                  in0=qf[:, par * FF + ns * 512: par * FF + (ns + 1) * 512],
                                  scalar=-128.0, in1=bank[:, 0:512], op0=OP.add, op1=OP.mult),
                              waits=[f"dqb_{u}_{ns}", f"dqq_{u}"], tick="dv",
                              event=(f"dqs_{u}_{ns}" if ns < N // 512 - 1 else f"dqs_{u}"))
            S.run()

        x = st.enter_context(nc.sbuf_tensor(f"{name}_x", [128, 2 * KD * T], F32))
        u4 = st.enter_context(nc.sbuf_tensor(f"{name}_u4", [128, KD * T], F32))
        hq = st.enter_context(nc.sbuf_tensor(f"{name}_hq", [128, KD * T], BF16))
        hs = st.enter_context(nc.sbuf_tensor(f"{name}_hs", [128, KF * T], BF16))
        vv = st.enter_context(nc.sbuf_tensor(f"{name}_vv", [128, KD * T], F32))
        sq = st.enter_context(nc.sbuf_tensor(f"{name}_sq", [128, KD * T], F32))
        stats = st.enter_context(nc.sbuf_tensor(f"{name}_st", [1, 4 * T], F32))
        S = Sched(p, name + "c")
        with ExitStack() as ps2:
            pp = [ps2.enter_context(nc.psum_tensor(f"{name}_pp{i}", [128, T], F32)) for i in range(2)]
            psA = ps2.enter_context(nc.psum_tensor(f"{name}_pA", [1, T], F32))
            psB = ps2.enter_context(nc.psum_tensor(f"{name}_pB", [1, T], F32))
            psR = ps2.enter_context(nc.psum_tensor(f"{name}_pR", [128, T], F32))
            psM = ps2.enter_context(nc.psum_tensor(f"{name}_pM", [128, T], F32))

            lnctr = [0]
            def ln_chain(ci, key, src, deps):
                k = lnctr[0]; lnctr[0] += 1
                sqw = list(deps) + ([f"lnsum_{k - 1}"] if k > 0 else [])
                S.add("scalar", lambda A, src=src: A.activation(out=sq[:], in_=src[:] if not isinstance(src, tuple) else x[:, src[1]:src[2]], func=AF.Square),
                      waits=sqw, tick="ex", event=f"lnsq_{k}")
                def fsums(t, src=src):
                    ssrc = src[:] if not isinstance(src, tuple) else x[:, src[1]:src[2]]
                    sum_matmuls(p, t, psA, ssrc, KD, T, C.ones_col)
                    return sum_matmuls(p, t, psB, sq[:], KD, T, C.ones_col)
                S.add("tensor", fsums, waits=[f"lnsq_{k}"] + ([f"lnrst_{k - 1}"] if k > 0 else []),
                      tick="pe", event=f"lnsum_{k}")
                def fstats(v):
                    v.tensor_scalar(out=stats[0:1, 0:T], in0=psA[0:1, 0:T], scalar1=1.0 / D,
                                    scalar2=None, op0=OP.mult)
                    v.tensor_tensor(out=stats[0:1, T:2 * T], in0=stats[0:1, 0:T],
                                    in1=stats[0:1, 0:T], op=OP.mult)
                    return v.scalar_tensor_tensor(out=stats[0:1, T:2 * T], in0=psB[0:1, 0:T],
                                                  scalar=1.0 / D, in1=stats[0:1, T:2 * T],
                                                  op0=OP.mult, op1=OP.subtract)
                S.add("vector", fstats, waits=[f"lnsum_{k}"], tick="dv", event=f"lnst_{k}")
                S.add("scalar", lambda A: rstd_from_ms(p, A, stats[0:1, 2 * T:3 * T],
                                                       stats[0:1, T:2 * T], 1.0, EPS_LN),
                      waits=[f"lnst_{k}"], tick="ex", event=f"lnrst_{k}")
                S.add("vector", lambda v: v.tensor_tensor(out=stats[0:1, 3 * T:4 * T],
                                                          in0=stats[0:1, 0:T],
                                                          in1=stats[0:1, 2 * T:3 * T], op=OP.mult),
                      waits=[f"lnrst_{k}"], tick="dv", event=f"lnmr_{k}")
                def fbb(t):
                    bcast_matmul(t, psR, stats[0:1, 2 * T:3 * T], T, C.ones_row)
                    return bcast_matmul(t, psM, stats[0:1, 3 * T:4 * T], T, C.ones_row)
                S.add("tensor", fbb, waits=[f"lnmr_{k}", f"lnbusy_{k}"] if f"lnbusy_{k}" in S.events
                      else [f"lnmr_{k}"], tick="pe", event=f"lnbb_{k}")
                return f"lnbb_{k}", k
            lastb = [None, None]
            for ci in range(NCH):
                xo = (ci % 2) * KD * T
                S.add("sync", lambda s, ci=ci, xo=xo: s.dma_start(
                    out=sb3(x[:, xo:xo + KD * T], KD, T), in_=feat_slice(x3_f, KD, ci, T)),
                    waits=([f"yst_{ci - 2}"] if ci >= 2 else []), tick="ld", dma=True, event=f"x_{ci}")
                e1, k1 = ln_chain(ci, "u4", ("x", xo, xo + KD * T), [f"x_{ci}"] +
                                  ([f"u4n_{ci - 1}", f"yst_{ci - 1}"] if ci > 0 else []))
                def fu4(v, xo=xo):
                    for a in range(KD):
                        sl = slice(a * T, (a + 1) * T)
                        v.tensor_tensor(out=u4[:, sl], in0=x[:, xo + a * T: xo + (a + 1) * T],
                                        in1=psR[:, 0:T], op=OP.mult)
                        inst = v.tensor_tensor(out=u4[:, sl], in0=u4[:, sl], in1=psM[:, 0:T],
                                               op=OP.subtract)
                    return inst
                u4w = [e1] + ([f"vvn_{ci - 1}"] if ci > 0 else [])
                S.add("vector", fu4, waits=u4w, tick="dv", event=f"u4n_{ci}")
                e2, k2 = ln_chain(ci, "hq", u4, [f"u4n_{ci}"])
                def fhq(v):
                    for a in range(KD):
                        sl = slice(a * T, (a + 1) * T)
                        v.tensor_tensor(out=sq[:, sl], in0=u4[:, sl], in1=psR[:, 0:T], op=OP.mult)
                        v.tensor_tensor(out=sq[:, sl], in0=sq[:, sl], in1=psM[:, 0:T], op=OP.subtract)
                        inst = v.tensor_scalar(out=hq[:, sl], in0=sq[:, sl], scalar1=gamT[:, a:a + 1],
                                               scalar2=betT[:, a:a + 1], op0=OP.mult, op1=OP.add)
                    return inst
                hqw = [e2] + ([f"fc1_{ci - 1}_{KF - 1}"] if ci > 0 else [])
                S.add("vector", fhq, waits=hqw, tick="dv", event=f"hq_{ci}")
                S.events[f"lnbusy_{k2 + 1}"] = S.events[f"hq_{ci}"]
                for et in range(KF):
                    b = et % 2
                    bw = [f"hq_{ci}", ("dqwall", None)] if False else [f"hq_{ci}"]
                    if lastb[b]:
                        bw.append(lastb[b])
                    def fmm(t, et=et, b=b):
                        for kt in range(KD):
                            inst = t.matmul(pp[b][:, 0:T], w1[:, kt * FF + et * 128: kt * FF + et * 128 + 128],
                                            hq[:, kt * T:(kt + 1) * T], start=(kt == 0), stop=(kt == KD - 1))
                        return inst
                    S.add("tensor", fmm, waits=bw, tick="pe", event=f"fc1_{ci}_{et}")
                    ew = [f"fc1_{ci}_{et}"] + ([f"fc2_{ci - 1}_{KD - 1}"] if ci > 0 and et == 0 else [])
                    S.add("scalar", lambda A, et=et, b=b: A.activation(
                        out=hs[:, et * T:(et + 1) * T], in_=pp[b][:, 0:T],
                        func=AF.Silu, bias=b1T[:, et:et + 1]),
                        waits=ew, tick="ex", event=f"hse_{ci}_{et}")
                    lastb[b] = f"hse_{ci}_{et}"
                for mt in range(KD):
                    b = mt % 2
                    bw = [f"hse_{ci}_{KF - 1}", lastb[b]]
                    def fmm(t, mt=mt, b=b):
                        for kt in range(KF):
                            inst = t.matmul(pp[b][:, 0:T], w2[:, kt * D + mt * 128: kt * D + mt * 128 + 128],
                                            hs[:, kt * T:(kt + 1) * T], start=(kt == 0), stop=(kt == KF - 1))
                        return inst
                    S.add("tensor", fmm, waits=bw, tick="pe", event=f"fc2_{ci}_{mt}")
                    def fv(v, mt=mt, b=b):
                        sl = slice(mt * T, (mt + 1) * T)
                        v.scalar_tensor_tensor(out=vv[:, sl], in0=pp[b][:, 0:T], scalar=0.5,
                                               in1=u4[:, sl], op0=OP.mult, op1=OP.add)
                        return v.tensor_scalar(out=vv[:, sl], in0=vv[:, sl], scalar1=hb2[:, mt:mt + 1],
                                               scalar2=None, op0=OP.add)
                    vw = [f"fc2_{ci}_{mt}"] + ([f"vvn_{ci - 1}"] if ci > 0 and mt == 0 else [])
                    S.add("vector", fv, waits=vw, tick="dv", event=f"vve_{ci}_{mt}")
                    lastb[b] = f"vve_{ci}_{mt}"
                e3, k3 = ln_chain(ci, "y", vv, [f"vve_{ci}_{KD - 1}"])
                def fy(v, xo=xo):
                    for a in range(KD):
                        sl = slice(a * T, (a + 1) * T)
                        v.tensor_tensor(out=sq[:, sl], in0=vv[:, sl], in1=psR[:, 0:T], op=OP.mult)
                        v.tensor_tensor(out=sq[:, sl], in0=sq[:, sl], in1=psM[:, 0:T], op=OP.subtract)
                        inst = v.tensor_tensor(out=sq[:, sl], in0=sq[:, sl],
                                               in1=x[:, xo + a * T: xo + (a + 1) * T], op=OP.add)
                    return inst
                S.add("vector", fy, waits=[e3] + ([f"yst_{ci - 1}"] if ci > 0 else []),
                      tick="dv", event=f"vvn_{ci}")
                S.add("gpsimd", lambda g, ci=ci: g.dma_start(
                    out=feat_slice(y_out, KD, ci, T), in_=sb3(sq[:], KD, T)),
                    waits=[f"vvn_{ci}"], tick="wl", dma=True, event=f"yst_{ci}")
            S.run()

def declare_inputs(nc, shapes):
    aps = {}
    for name, (shape, dt) in shapes.items():
        aps[name] = nc.dram_tensor(name, list(shape), dt, kind="ExternalInput").ap()
    return aps


def input_shapes():
    sh = {
        "xT": ((D, L), F32),
        "gate_w": ((D, NE), F32),
        "gate_b_rep": ((128, NE), F32),
        "moe_w1": ((NE, D, FF), F32),
        "moe_w2": ((NE, FF, D), F32),
        "moe_b1T": ((NE, 128, KF), F32),
        "moe_b2": ((NE, D), F32),
        "W_attn": ((D, 3 * D), F32),
        "battnT": ((128, 24), F32),
        "W_proj": ((D, D), F32),
        "bprojT": ((128, 8), F32),
        "gamT": ((128, 8), F32),
        "betT": ((128, 8), F32),
        "fc1_q": ((D, FF), I32),
        "fc1_scales": ((8, FF), F32),
        "fc1bT": ((128, KF), F32),
        "fc2_q": ((FF, D), I32),
        "fc2_scales": ((32, D), F32),
        "fc2bT": ((128, 8), F32),
        "mlp_w1T": ((D, DI), F32),
        "mlp_b1T": ((128, 16), F32),
        "mlp_w2T": ((DI, D), F32),
        "mlp_b2T": ((128, 8), F32),
        "ln_wT": ((128, 8), F32),
        "ln2_wT": ((128, 8), F32),
    }
    for i in range(NLAY):
        sh.update({
            f"m{i}_inT": ((D, 2 * DI), F32),
            f"m{i}_convw": ((128, 64), F32),
            f"m{i}_convb": ((128, 16), F32),
            f"m{i}_xpT": ((DI, 96), F32),
            f"m{i}_dtpT": ((DTR, DI), F32),
            f"m{i}_dtb": ((128, 16), F32),
            f"m{i}_alog": ((128, 256), F32),
            f"m{i}_dp": ((128, 16), F32),
            f"m{i}_outT": ((DI, D), F32),
        })
    return sh


def prepare_host_inputs(inputs):
    w = {}
    w["gate_w"] = np.ascontiguousarray(inputs["L0_ffn_gate_w"], np.float32)
    w["gate_b_rep"] = np.ascontiguousarray(
        np.broadcast_to(np.asarray(inputs["L0_ffn_gate_b"], np.float32), (128, NE)))
    w["moe_w1"] = np.ascontiguousarray(inputs["L0_ffn_w1"], np.float32)
    w["moe_w2"] = np.ascontiguousarray(inputs["L0_ffn_w2"], np.float32)
    w["moe_b1T"] = np.ascontiguousarray(
        np.asarray(inputs["L0_ffn_b1"], np.float32).reshape(NE, KF, 128).transpose(0, 2, 1))
    w["moe_b2"] = np.ascontiguousarray(inputs["L0_ffn_b2"], np.float32)
    w["W_attn"] = np.ascontiguousarray(inputs["L1_attn_W_attn"], np.float32)
    w["battnT"] = _cols(inputs["L1_attn_b_attn"], 24)
    w["W_proj"] = np.ascontiguousarray(inputs["L1_attn_W_proj"], np.float32)
    w["bprojT"] = _cols(inputs["L1_attn_b_proj"], 8)
    w["gamT"] = _cols(inputs["L1_ffn_ln_gamma"], 8)
    w["betT"] = _cols(inputs["L1_ffn_ln_beta"], 8)
    w["fc1_q"] = np.ascontiguousarray(inputs["L1_ffn_fc1_q"], np.int32)
    w["fc1_scales"] = np.ascontiguousarray(inputs["L1_ffn_fc1_scales"], np.float32)
    w["fc1bT"] = _cols(inputs["L1_ffn_fc1_bias"], KF)
    w["fc2_q"] = np.ascontiguousarray(inputs["L1_ffn_fc2_q"], np.int32)
    w["fc2_scales"] = np.ascontiguousarray(inputs["L1_ffn_fc2_scales"], np.float32)
    w["fc2bT"] = _cols(inputs["L1_ffn_fc2_bias"], 8)
    w["mlp_w1T"] = _t(inputs["L0_attn_mlp_w1"])
    w["mlp_b1T"] = _cols(inputs["L0_attn_mlp_b1"], 16)
    w["mlp_w2T"] = _t(inputs["L0_attn_mlp_w2"])
    w["mlp_b2T"] = _cols(inputs["L0_attn_mlp_b2"], 8)
    w["ln_wT"] = _cols(inputs["L0_attn_ln_w"], 8)
    w["ln2_wT"] = _cols(inputs["L0_attn_ln2_w"], 8)
    for i in range(NLAY):
        w[f"m{i}_inT"] = _t(inputs["L0_attn_in_proj_w"][i])
        w[f"m{i}_convw"] = np.ascontiguousarray(
            np.asarray(inputs["L0_attn_conv_w"][i], np.float32).reshape(16, 128, 4)
            .transpose(1, 0, 2).reshape(128, 64))
        w[f"m{i}_convb"] = _cols(inputs["L0_attn_conv_b"][i], 16)
        w[f"m{i}_xpT"] = _t(inputs["L0_attn_x_proj_w"][i])
        w[f"m{i}_dtpT"] = _t(inputs["L0_attn_dt_proj_w"][i])
        w[f"m{i}_dtb"] = _cols(inputs["L0_attn_dt_proj_b"][i], 16)
        w[f"m{i}_alog"] = np.ascontiguousarray(
            np.asarray(inputs["L0_attn_A_log"][i], np.float32).reshape(16, 128, 16)
            .transpose(1, 0, 2).reshape(128, 256))
        w[f"m{i}_dp"] = _cols(inputs["L0_attn_D_param"][i], 16)
        w[f"m{i}_outT"] = _t(inputs["L0_attn_out_proj_w"][i])
    xs = [np.ascontiguousarray(np.asarray(inputs["L0_attn_x"][b], np.float32).T)
          for b in range(8)]
    return w, xs


def mamba_W(aps, i):
    return {"inT": aps[f"m{i}_inT"], "convw": aps[f"m{i}_convw"], "convb": aps[f"m{i}_convb"],
            "xpT": aps[f"m{i}_xpT"], "dtpT": aps[f"m{i}_dtpT"], "dtb": aps[f"m{i}_dtb"],
            "alog": aps[f"m{i}_alog"], "dp": aps[f"m{i}_dp"], "outT": aps[f"m{i}_outT"],
            "m1T": aps["mlp_w1T"], "b1T": aps["mlp_b1T"], "m2T": aps["mlp_w2T"],
            "b2T": aps["mlp_b2T"], "lnwT": aps["ln_wT"]}


def build_program():
    nc = bass.Bass("TRN2", target_bir_lowering=False, debug=False)
    aps = declare_inputs(nc, input_shapes())
    y_out = nc.dram_tensor("y", [D, L], F32, kind="ExternalOutput").ap()

    def scr(name, shape, dt):
        return nc.dram_tensor(name, list(shape), dt, kind="Internal").ap()
    u1f = scr("u1f", [D, L], F32)
    u1b = scr("u1b", [D, L], BF16)
    ygd = scr("ygd", [DI, L], BF16)
    bcd = scr("bcd", [32, 256], BF16)
    xm_f = [scr(f"xm{j}f", [D, L], F32) for j in range(NLAY)]
    xm_b = [scr(f"xm{j}b", [D, L], BF16) for j in range(NLAY)]
    x1f = scr("x1f", [D, L], F32)
    u2b = scr("u2b", [D, L], BF16)
    combd = scr("combd", [NE * L], F32)
    accf = scr("accf", [D, L], F32)
    x2f = scr("x2f", [D, L], F32)
    qkvd = scr("qkvd", [3 * D, L], BF16)
    vtmd = scr("vtmd", [L, D], BF16)
    x3f = scr("x3f", [D, L], F32)

    ident_dram = nc.inline_tensor(np.eye(128, dtype=np.float32), name="identc")
    masks_dram = nc.inline_tensor(build_masks(), name="masksc")

    with ExitStack() as stack:
        tok = stack.enter_context(nc.semaphore("tok"))
        blk = stack.enter_context(nc.Block())
        p = Pr(nc, blk, tok)
        C = alloc_consts(p, stack, masks_dram, ident_dram)

        stage_ln(p, C, aps["xT"], dst_f=u1f, dst_b=u1b, name="ln1")
        src_f, src_b = u1f, u1b
        for i in range(NLAY):
            W = mamba_W(aps, i)
            stage_mamba_a(p, C, W, src_b, ygd, bcd, name=f"ma{i}")
            stage_mamba_b(p, C, W, ygd, src_f, xm_f[i], xm_b[i], name=f"mb{i}")
            src_f, src_b = xm_f[i], xm_b[i]
        stage_mamba_tail(p, C, aps["ln2_wT"], src_f, u1f, aps["xT"], x1f, name="mtl")
        stage_moe_pre(p, C, aps, x1f, u2b, combd, name="mp")
        stage_moe(p, C, aps, u2b, combd, accf, name="moex")
        stage_add_ln(p, C, accf, x1f, x2f, name="mo2")
        stage_attn_qkv(p, C, aps, x2f, qkvd, vtmd, name="aq")
        stage_attn_core(p, C, aps, qkvd, vtmd, x2f, x3f, name="ac")
        stage_ffn(p, C, aps, x3f, y_out, T=256, name="ff")

        p.ph("sync", lambda p, s: s.wait_ge(tok, p.n))
    return nc


def kernel(**inputs):
    w, xs = prepare_host_inputs(inputs)
    nc = build_program()
    in_maps = [dict(w, xT=xs[b]) for b in range(8)]
    res = run_bass_kernel_spmd(nc, in_maps, core_ids=list(range(8)))
    out = np.empty((8, L, D), dtype=np.float32)
    for b in range(8):
        out[b] = np.asarray(res.results[b]["y"]).T
    return out


# version 10 (final)
# speedup vs baseline: 1.0251x; 1.0251x over previous
import numpy as np
import ml_dtypes
from contextlib import ExitStack

import concourse.bass as bass
import concourse.mybir as mybir
from concourse.bass_utils import run_bass_kernel_spmd

F32 = mybir.dt.float32
BF16 = mybir.dt.bfloat16
I32 = mybir.dt.int32
AF = mybir.ActivationFunctionType
OP = mybir.AluOpType
AX = mybir.AxisListType

D = 1024
L = 4096
DI = 2048
DS = 16
DTR = 64
DC = 4
NLAY = 2
NE = 8
FF = 4096
NH = 16
HD = 64
EPS_LN = 1e-5
EPS_RMS = 1.1920929e-07

KD = D // 128
KI = DI // 128
KF = FF // 128


class Pr:
    def __init__(self, nc, blk, tok):
        self.nc = nc
        self.blk = blk
        self.tok = tok
        self.n = 0

    def ph(self, eng_name, fn):
        n0 = self.n
        def body(eng):
            eng.wait_ge(self.tok, n0)
            fn(self, eng)
        getattr(self.blk, eng_name)(body)

    def inc(self, inst, k=1):
        inst.then_inc(self.tok, k)
        self.n += k

    def dma(self, inst):
        self.inc(inst, 16)


class Ctr:
    def __init__(self, nc, name):
        self.sem = nc.semaphore(name).__enter__()
        self.name = name
        self.n = 0


class Sched:
    ENGINES = ("sync", "tensor", "scalar", "vector", "gpsimd")

    def __init__(self, p, name):
        self.p = p
        self.name = name
        self.ops = {e: [] for e in self.ENGINES}
        self.ctrs = {}
        self.events = {}

    def ctr(self, cname):
        if cname not in self.ctrs:
            self.ctrs[cname] = Ctr(self.p.nc, f"{self.name}_{cname}")
        return self.ctrs[cname]

    def add(self, eng, emit, waits=(), tick=None, dma=False, event=None):
        wv = []
        for w in waits:
            wv.append(self.events[w] if isinstance(w, str) else w)
        val = None
        if tick is not None:
            c = self.ctr(tick)
            c.n += 16 if dma else 1
            val = c.n
        self.ops[eng].append((emit, wv, tick, dma))
        if event is not None:
            assert tick is not None
            self.events[event] = (tick, val)

    def ev(self, key):
        return self.events[key]

    def run(self):
        entry = self.p.n
        used = [e for e in self.ENGINES if self.ops[e]]
        for eng in used:
            oplist = self.ops[eng]
            def body(E, oplist=oplist, eng=eng):
                E.wait_ge(self.p.tok, entry)
                seen = {}
                for emit, wv, tick, dma in oplist:
                    merged = {}
                    for cn, v in wv:
                        if v is not None and merged.get(cn, -1) < v:
                            merged[cn] = v
                    for cn, v in merged.items():
                        if seen.get(cn, -1) < v:
                            E.wait_ge(self.ctrs[cn].sem, v)
                            seen[cn] = v
                    inst = emit(E)
                    if tick is not None:
                        inst.then_inc(self.ctrs[tick].sem, 16 if dma else 1)
                own = None
                for emit, wv, tick, dma in reversed(oplist):
                    if tick is not None:
                        own = tick
                        break
                if own is not None:
                    E.wait_ge(self.ctrs[own].sem, self.ctrs[own].n)
                E.nop().then_inc(self.p.tok, 1)
            getattr(self.p.blk, eng)(body)
            self.p.n += 1


def feat_slice(dram, ktiles, c, T):
    return dram.rearrange("(a p) t -> p a t", p=128)[:, :, c * T:(c + 1) * T]


def sb3(sb_ap, ktiles, T):
    return sb_ap.rearrange("p (a t) -> p a t", a=ktiles)


def load_chunk(p, eng, sbuf, dram, ktiles, c, T):
    p.dma(eng.dma_start(out=sb3(sbuf[:], ktiles, T), in_=feat_slice(dram, ktiles, c, T)))


def store_chunk(p, eng, dram, sbuf, ktiles, c, T):
    p.dma(eng.dma_start(out=feat_slice(dram, ktiles, c, T), in_=sb3(sbuf[:], ktiles, T)))


def rstd_from_ms(p, A, ms_dst, ms_src_psum, scale, eps):
    A.activation(out=ms_dst, in_=ms_src_psum, func=AF.Ln, scale=scale, bias=eps)
    return A.activation(out=ms_dst, in_=ms_dst, func=AF.Exp, scale=-0.5)


def sum_matmuls(p, t, psum, src_sb, ktiles, T, ones_col):
    for a in range(ktiles):
        inst = t.matmul(psum[0:1, 0:T], ones_col[:], src_sb[:, a * T:(a + 1) * T],
                        start=(a == 0), stop=(a == ktiles - 1))
    return inst


def bcast_matmul(t, psum, row_ap, T, ones_row, m=128):
    return t.matmul(psum[0:m, 0:T], ones_row[0:1, 0:m], row_ap, start=True, stop=True)


class Consts:
    pass


def alloc_consts(p, stack, masks_dram, ident_dram):
    nc = p.nc
    c = Consts()
    c.ones_col = stack.enter_context(nc.sbuf_tensor("ones_col", [128, 1], F32))
    c.ones_row = stack.enter_context(nc.sbuf_tensor("ones_row", [1, 128], F32))
    c.ones_row_b = stack.enter_context(nc.sbuf_tensor("ones_row_b", [1, 128], BF16))
    c.ident_b = stack.enter_context(nc.sbuf_tensor("ident_b", [128, 128], BF16))
    c.ident_f = stack.enter_context(nc.sbuf_tensor("ident_f", [128, 128], F32))
    c.masks = stack.enter_context(nc.sbuf_tensor("masks", [128, 4 * 512], BF16))
    c.cst = stack.enter_context(nc.sbuf_tensor("cst", [128, 16], F32))

    cvals = [EPS_LN, EPS_RMS, -1.0, -2.0, 0.5, -128.0, -0.5, 1.0 / D]
    def f(p, v):
        p.inc(v.memset(c.ones_col[:], 1.0))
        p.inc(v.memset(c.ones_row[:], 1.0))
        p.inc(v.memset(c.ones_row_b[:], 1.0))
        for i, val in enumerate(cvals):
            p.inc(v.memset(c.cst[:, i:i + 1], val))
            try:
                nc.const_aps.register(F32, val, c.cst[:, i:i + 1])
            except Exception:
                pass
    p.ph("vector", f)

    def g(p, s):
        p.dma(s.dma_start(out=c.masks[:], in_=masks_dram.ap()[:]))
        p.dma(s.dma_start(out=c.ident_f[:], in_=ident_dram.ap()[:]))
    p.ph("sync", g)

    def h(p, gp):
        p.dma(gp.dma_start(out=c.ident_b[:], in_=ident_dram.ap()[:]))
    p.ph("gpsimd", h)
    return c


def stage_ln(p, C, src, dst_f=None, dst_b=None, T=512, name="ln"):
    nc = p.nc
    NCH = L // T
    with ExitStack() as st:
        x = st.enter_context(nc.sbuf_tensor(f"{name}_x", [128, KD * T], F32))
        sq = st.enter_context(nc.sbuf_tensor(f"{name}_sq", [128, KD * T], F32))
        stats = st.enter_context(nc.sbuf_tensor(f"{name}_st", [1, 4 * T], F32))
        psA = st.enter_context(nc.psum_tensor(f"{name}_pA", [1, T], F32))
        psB = st.enter_context(nc.psum_tensor(f"{name}_pB", [1, T], F32))
        psR = st.enter_context(nc.psum_tensor(f"{name}_pR", [128, T], F32))
        psM = st.enter_context(nc.psum_tensor(f"{name}_pM", [128, T], F32))
        for ci in range(NCH):
            p.ph("sync", lambda p, s, ci=ci: load_chunk(p, s, x, src, KD, ci, T))
            p.ph("scalar", lambda p, A: p.inc(A.activation(out=sq[:], in_=x[:], func=AF.Square)))
            p.ph("tensor", lambda p, t: p.inc(sum_matmuls(p, t, psA, x[:], KD, T, C.ones_col)))
            p.ph("tensor", lambda p, t: p.inc(sum_matmuls(p, t, psB, sq[:], KD, T, C.ones_col)))
            def fstats(p, v):
                v.tensor_scalar(out=stats[0:1, 0:T], in0=psA[0:1, 0:T], scalar1=1.0 / D,
                                scalar2=None, op0=OP.mult)
                v.tensor_tensor(out=stats[0:1, T:2 * T], in0=stats[0:1, 0:T],
                                in1=stats[0:1, 0:T], op=OP.mult)
                p.inc(v.scalar_tensor_tensor(out=stats[0:1, T:2 * T], in0=psB[0:1, 0:T],
                                             scalar=1.0 / D, in1=stats[0:1, T:2 * T],
                                             op0=OP.mult, op1=OP.subtract))
            p.ph("vector", fstats)
            p.ph("scalar", lambda p, A: p.inc(
                rstd_from_ms(p, A, stats[0:1, 2 * T:3 * T], stats[0:1, T:2 * T], 1.0, EPS_LN)))
            p.ph("vector", lambda p, v: p.inc(
                v.tensor_tensor(out=stats[0:1, 3 * T:4 * T], in0=stats[0:1, 0:T],
                                in1=stats[0:1, 2 * T:3 * T], op=OP.mult)))
            def fb(p, t):
                bcast_matmul(t, psR, stats[0:1, 2 * T:3 * T], T, C.ones_row)
                p.inc(bcast_matmul(t, psM, stats[0:1, 3 * T:4 * T], T, C.ones_row))
            p.ph("tensor", fb)
            def fnorm(p, v):
                for a in range(KD):
                    sl = slice(a * T, (a + 1) * T)
                    v.tensor_tensor(out=sq[:, sl], in0=x[:, sl], in1=psR[:, 0:T], op=OP.mult)
                    last = v.tensor_tensor(out=sq[:, sl], in0=sq[:, sl], in1=psM[:, 0:T],
                                           op=OP.subtract)
                p.inc(last)
            p.ph("vector", fnorm)
            if dst_f is not None:
                p.ph("sync", lambda p, s, ci=ci: store_chunk(p, s, dst_f, sq, KD, ci, T))
            if dst_b is not None:
                p.ph("gpsimd", lambda p, g, ci=ci: p.dma(
                    g.dma_start(out=feat_slice(dst_b, KD, ci, T), in_=sb3(sq[:], KD, T))))


def stage_mamba_a(p, C, W, src_b, yg_dram, bc_dram, T=256, name="ma"):
    nc = p.nc
    NCH = L // T
    SEG = 3 + T
    with ExitStack() as st:
        w_in = st.enter_context(nc.sbuf_tensor(f"{name}_win", [128, KD * (2 * DI)], BF16))
        w_xp = st.enter_context(nc.sbuf_tensor(f"{name}_wxp", [128, KI * 96], BF16))
        w_dtp = st.enter_context(nc.sbuf_tensor(f"{name}_wdtp", [64, DI], BF16))
        convw = st.enter_context(nc.sbuf_tensor(f"{name}_cw", [128, 64], F32))
        convb = st.enter_context(nc.sbuf_tensor(f"{name}_cb", [128, 16], F32))
        dtb = st.enter_context(nc.sbuf_tensor(f"{name}_dtb", [128, 16], F32))
        alogf = st.enter_context(nc.sbuf_tensor(f"{name}_alf", [128, 256], F32))
        aneg = st.enter_context(nc.sbuf_tensor(f"{name}_an", [128, 256], BF16))
        dpar = st.enter_context(nc.sbuf_tensor(f"{name}_dp", [128, 16], F32))
        hcarry = st.enter_context(nc.sbuf_tensor(f"{name}_hc", [128, 256], BF16))
        hist = st.enter_context(nc.sbuf_tensor(f"{name}_hist", [128, 16 * 3], BF16))

        xin_b = st.enter_context(nc.sbuf_tensor(f"{name}_xb", [128, KD * T], BF16))
        x_in = st.enter_context(nc.sbuf_tensor(f"{name}_xin", [128, 2 * KI * SEG], BF16))
        sz = st.enter_context(nc.sbuf_tensor(f"{name}_sz", [128, 2 * KI * T], BF16))
        xc = st.enter_context(nc.sbuf_tensor(f"{name}_xc", [128, 2 * KI * T], BF16))
        ctmp = st.enter_context(nc.sbuf_tensor(f"{name}_ctp", [128, 2 * T], F32))
        dt = st.enter_context(nc.sbuf_tensor(f"{name}_dt", [128, KI * T], BF16))
        wdt = st.enter_context(nc.sbuf_tensor(f"{name}_w", [128, KI * T], BF16))
        xdbl_b = st.enter_context(nc.sbuf_tensor(f"{name}_xdb", [64, T], BF16))
        bc_sb = st.enter_context(nc.sbuf_tensor(f"{name}_bcs", [32, T], BF16))
        bcflat = st.enter_context(nc.sbuf_tensor(f"{name}_bc", [1, 32 * T], BF16))
        brc = st.enter_context(nc.sbuf_tensor(f"{name}_brc", [128, 16 * 2 * T], BF16))
        argq = st.enter_context(nc.sbuf_tensor(f"{name}_ag", [128, 4 * 4 * (T + 2)], BF16))
        WS = T + 2
        h8q = st.enter_context(nc.sbuf_tensor(f"{name}_h8", [128, 4 * WS], BF16))
        dbu = st.enter_context(nc.sbuf_tensor(f"{name}_db", [128, 2 * 4 * WS], BF16))
        hct = st.enter_context(nc.sbuf_tensor(f"{name}_hct", [128, 2 * 4 * T], BF16))
        ytmp = st.enter_context(nc.sbuf_tensor(f"{name}_yt", [128, T], F32))
        ygq = st.enter_context(nc.sbuf_tensor(f"{name}_ygq", [128, 2 * 4 * T], BF16))

        pp = [st.enter_context(nc.psum_tensor(f"{name}_pp{i}", [128, 512], F32)) for i in range(2)]
        pbc = [st.enter_context(nc.psum_tensor(f"{name}_pb{i}", [128, 512], F32)) for i in range(2)]
        py = [st.enter_context(nc.psum_tensor(f"{name}_py{i}", [128, 1024], F32)) for i in range(2)]

        S = Sched(p, name)

        S.add("gpsimd", lambda g: g.dma_start(out=w_in[:].rearrange("p (a m) -> p a m", a=KD),
                                              in_=W["inT"].rearrange("(a p) m -> p a m", p=128)),
              tick="wl", dma=True, event="win")
        S.add("gpsimd", lambda g: g.dma_start(out=w_xp[:].rearrange("p (a m) -> p a m", a=KI),
                                              in_=W["xpT"].rearrange("(a p) m -> p a m", p=128)),
              tick="wl", dma=True, event="wxp")
        S.add("gpsimd", lambda g: g.dma_start(out=w_dtp[:], in_=W["dtpT"]),
              tick="wl", dma=True, event="wdtp")
        S.add("sync", lambda s: s.dma_start(out=convw[:], in_=W["convw"]), tick="ld", dma=True)
        S.add("sync", lambda s: s.dma_start(out=convb[:], in_=W["convb"]), tick="ld", dma=True)
        S.add("sync", lambda s: s.dma_start(out=dtb[:], in_=W["dtb"]), tick="ld", dma=True)
        S.add("sync", lambda s: s.dma_start(out=dpar[:], in_=W["dp"]), tick="ld", dma=True)
        S.add("sync", lambda s: s.dma_start(out=alogf[:], in_=W["alog"]),
              tick="ld", dma=True, event="consts")
        S.add("scalar", lambda A: A.activation(out=alogf[:], in_=alogf[:], func=AF.Exp),
              waits=["consts"], tick="ex", event="aexp")
        def zi(v):
            v.tensor_scalar(out=aneg[:], in0=alogf[:], scalar1=-1.0, scalar2=None, op0=OP.mult)
            v.memset(argq[:], 0.0)
            v.memset(argq[:].rearrange("p (g w) -> p g w", w=WS)[:, :, 1], 1.0)
            v.memset(dbu[:], 0.0)
            v.memset(hcarry[:], 0.0)
            return v.memset(hist[:], 0.0)
        S.add("vector", zi, waits=["aexp"], tick="dv", event="aneg")

        aneg3 = aneg[:].rearrange("p (g s2) -> p g s2", s2=16)

        def emit_inproj(ci, lo, hi):
            cur = ci % 2
            xof = cur * KI * SEG
            cof = cur * KI * T
            if lo == 0:
                S.add("sync", lambda s, ci=ci: s.dma_start(
                    out=sb3(xin_b[:], KD, T), in_=feat_slice(src_b, KD, ci, T)),
                    waits=([f"ipmm_{ci - 1}_{2 * KI - 1}"] if ci > 0 else []),
                    tick="ld", dma=True, event=f"xld_{ci}")
            for et in range(lo, hi):
                bank = pp[et % 2]
                bw = [f"xld_{ci}", "win"]
                if et >= 2:
                    bw.append(f"ipev_{ci}_{et - 2}")
                elif ci > 0:
                    bw.append(f"sp_{ci - 1}_{14 + et}")
                def fmm(t, et=et, bank=bank):
                    for kt in range(KD):
                        inst = t.matmul(bank[:, 0:T],
                                        w_in[:, kt * 2 * DI + et * 128: kt * 2 * DI + et * 128 + 128],
                                        xin_b[:, kt * T:(kt + 1) * T],
                                        start=(kt == 0), stop=(kt == KD - 1))
                    return inst
                S.add("tensor", fmm, waits=bw, tick="pe", event=f"ipmm_{ci}_{et}")
                if et < KI:
                    ew = [f"ipmm_{ci}_{et}"]
                    if ci >= 2:
                        ew.append(f"cv_{ci - 2}_{KI - 1}")
                    S.add("scalar", lambda A, et=et, bank=bank, xof=xof: A.copy(
                        x_in[:, xof + et * SEG + 3: xof + et * SEG + 3 + T], bank[:, 0:T]),
                        waits=ew, tick="ex", event=f"ipev_{ci}_{et}")
                else:
                    g = et - KI
                    ew = [f"ipmm_{ci}_{et}"]
                    if ci >= 2:
                        ew.append(f"yq_{ci - 2}_3")
                    S.add("scalar", lambda A, g=g, bank=bank, cof=cof: A.activation(
                        out=sz[:, cof + g * T: cof + (g + 1) * T], in_=bank[:, 0:T], func=AF.Silu),
                        waits=ew, tick="ex", event=f"ipev_{ci}_{et}")

        emit_inproj(0, 0, 2 * KI)
        for ci in range(NCH):
            cur = ci % 2
            xof = cur * KI * SEG
            cof = cur * KI * T
            for g in range(KI):
                base = xof + g * SEG
                def fconv(v, g=g, base=base):
                    v.tensor_copy(x_in[:, base: base + 3], hist[:, g * 3: g * 3 + 3])
                    v.tensor_scalar(out=ctmp[:, (g % 2) * T:(g % 2) * T + T],
                                    in0=x_in[:, base: base + T],
                                    scalar1=convw[:, 4 * g: 4 * g + 1], scalar2=None, op0=OP.mult)
                    for j in (1, 2, 3):
                        inst = v.scalar_tensor_tensor(
                            out=ctmp[:, (g % 2) * T:(g % 2) * T + T],
                            in0=x_in[:, base + j: base + j + T],
                            scalar=convw[:, 4 * g + j: 4 * g + j + 1],
                            in1=ctmp[:, (g % 2) * T:(g % 2) * T + T],
                            op0=OP.mult, op1=OP.add)
                    inst = v.tensor_copy(hist[:, g * 3: g * 3 + 3], x_in[:, base + T: base + T + 3])
                    return inst
                cw = [f"ipev_{ci}_{g}", "consts", "aneg"]
                if g >= 2:
                    cw.append(f"xc_{ci}_{g - 2}")
                S.add("vector", fconv, waits=cw, tick="dv", event=f"cv_{ci}_{g}")
                sw = [f"cv_{ci}_{g}"]
                if ci >= 2:
                    sw.append(f"yq_{ci - 2}_3")
                S.add("scalar", lambda A, g=g, cof=cof: A.activation(
                    out=xc[:, cof + g * T: cof + (g + 1) * T],
                    in_=ctmp[:, (g % 2) * T:(g % 2) * T + T],
                    func=AF.Silu, bias=convb[:, g: g + 1]),
                    waits=sw, tick="ex", event=f"xc_{ci}_{g}")
            def fxp(t, cof=cof):
                for kt in range(KI):
                    inst = t.matmul(pp[0][0:96, 0:T], w_xp[:, kt * 96:(kt + 1) * 96],
                                    xc[:, cof + kt * T: cof + kt * T + T],
                                    start=(kt == 0), stop=(kt == KI - 1))
                return inst
            S.add("tensor", fxp, waits=[f"xc_{ci}_{KI - 1}", "wxp", f"ipev_{ci}_{2 * KI - 2}"],
                  tick="pe", event=f"xpmm_{ci}")
            def fxe(v):
                v.tensor_copy(xdbl_b[:], pp[0][0:64, 0:T])
                return v.tensor_copy(bc_sb[:], pp[0][64:96, 0:T])
            S.add("vector", fxe, waits=[f"xpmm_{ci}"], tick="dv", event=f"xde_{ci}")
            S.add("scalar", lambda A: A.dma_start(out=bc_dram[0:32, 0:T], in_=bc_sb[:]),
                  waits=[f"xde_{ci}"], tick="sd", dma=True, event=f"bcst_{ci}")
            bw = [f"bcst_{ci}"]
            if ci > 0:
                bw.append(f"brc_{ci - 1}_15")
            S.add("sync", lambda s: s.dma_start(
                out=bcflat[:], in_=bc_dram[0:32, 0:T].rearrange("r t -> (r t)").unsqueeze(0)),
                waits=bw, tick="ld", dma=True, event=f"bcf_{ci}")
            for mt in range(KI):
                bank = pp[mt % 2]
                bw = [f"xde_{ci}", "wdtp"]
                if mt >= 2:
                    bw.append(f"sp_{ci}_{mt - 2}")
                elif mt == 0:
                    bw.append(f"xde_{ci}")
                else:
                    bw.append(f"ipev_{ci}_{2 * KI - 1}")
                S.add("tensor", lambda t, mt=mt, bank=bank: t.matmul(
                    bank[:, 0:T], w_dtp[:, mt * 128:(mt + 1) * 128], xdbl_b[:],
                    start=True, stop=True),
                    waits=bw, tick="pe", event=f"dtmm_{ci}_{mt}")
                spw = [f"dtmm_{ci}_{mt}", f"cv_{ci}_{KI - 1}"]
                if ci > 0:
                    spw.append(f"ag_{ci - 1}_63")
                def fsp(A, mt=mt, bank=bank):
                    A.activation(out=ctmp[:, (mt % 2) * T:(mt % 2) * T + T], in_=bank[:, 0:T],
                                 func=AF.Exp, bias=dtb[:, mt:mt + 1])
                    return A.activation(out=dt[:, mt * T:(mt + 1) * T],
                                        in_=ctmp[:, (mt % 2) * T:(mt % 2) * T + T],
                                        func=AF.Ln, bias=1.0)
                S.add("scalar", fsp, waits=spw, tick="ex", event=f"sp_{ci}_{mt}")
            for s_ in range(DS):
                sp2 = s_ % 2
                bw = [f"bcf_{ci}"]
                if s_ >= 2:
                    bw.append(f"bev_{ci}_{s_ - 2}")
                elif ci > 0:
                    bw.append(f"bev_{ci - 1}_{14 + s_}")
                def fbc(t, s_=s_, sp2=sp2):
                    bcast_matmul(t, pbc[sp2][:, 0:T], bcflat[0:1, s_ * T:(s_ + 1) * T], T, C.ones_row_b)
                    return t.matmul(pbc[sp2][:, T:2 * T], C.ones_row_b[0:1, 0:128],
                                    bcflat[0:1, (16 + s_) * T:(17 + s_) * T], start=True, stop=True)
                S.add("tensor", fbc, waits=bw, tick="pe", event=f"brc_{ci}_{s_}")
                ev = [f"brc_{ci}_{s_}"]
                if ci > 0:
                    ev.append(f"hc_{ci - 1}_{48 + s_}")
                S.add("scalar", lambda A, s_=s_, sp2=sp2: A.copy(
                    brc[:, s_ * 2 * T:(s_ + 1) * 2 * T], pbc[sp2][:, 0:2 * T]),
                    waits=ev, tick="ex", event=f"bev_{ci}_{s_}")
            if ci + 1 < NCH:
                emit_inproj(ci + 1, 0, 4)

            for q in range(4):
                S.add("vector", lambda v, q=q, cof=cof: v.tensor_tensor(
                    out=wdt[:, q * 4 * T:(q + 1) * 4 * T],
                    in0=dt[:, q * 4 * T:(q + 1) * 4 * T],
                    in1=xc[:, cof + q * 4 * T: cof + (q + 1) * 4 * T], op=OP.mult),
                    waits=[f"sp_{ci}_{q * 4 + 3}", f"xc_{ci}_{q * 4 + 3}"],
                    tick="dv", event=f"w_{ci}_{q}")
                for s_ in range(DS):
                    j = q * 16 + s_
                    pr = j % 4
                    if s_ in (0, 8) and not (q == 0 and s_ == 0) and ci + 1 < NCH:
                        k = 2 * q + (1 if s_ == 8 else 0)
                        emit_inproj(ci + 1, 4 * k, 4 * k + 4)
                    aw = [f"sp_{ci}_{q * 4 + 3}", "aneg"]
                    if j >= 4:
                        aw.append(f"scn_{ci}_{j - 4}")
                    elif ci > 0:
                        aw.append(f"scn_{ci - 1}_{60 + j}")
                    S.add("gpsimd", lambda g_, q=q, s_=s_, pr=pr: g_.tensor_tensor(
                        out=argq[:, pr * 4 * WS:(pr + 1) * 4 * WS]
                            .rearrange("p (g w) -> p g w", g=4)[:, :, 2:WS],
                        in0=dt[:, q * 4 * T:(q + 1) * 4 * T].rearrange("p (g t) -> p g t", g=4),
                        in1=aneg3[:, q * 4:(q + 1) * 4, s_:s_ + 1].broadcast_to([128, 4, T]),
                        op=OP.mult),
                        waits=aw, tick="pl", event=f"ag_{ci}_{j}")
                    S.add("scalar", lambda A, pr=pr: A.activation(
                        out=argq[:, pr * 4 * WS:(pr + 1) * 4 * WS]
                            .rearrange("p (g w) -> p g w", g=4)[:, :, 2:WS],
                        in_=argq[:, pr * 4 * WS:(pr + 1) * 4 * WS]
                            .rearrange("p (g w) -> p g w", g=4)[:, :, 2:WS], func=AF.Exp),
                        waits=[f"ag_{ci}_{j}"], tick="ex", event=f"dA_{ci}_{j}")
                    dr = j % 2
                    def fdbu(v, q=q, s_=s_, dr=dr):
                        v.tensor_copy(
                            dbu[:, dr * 4 * WS:(dr + 1) * 4 * WS]
                                .rearrange("p (g w) -> p g w", g=4)[:, :, 0],
                            hcarry[:, s_ * 16 + q * 4: s_ * 16 + q * 4 + 4])
                        return v.tensor_tensor(
                            out=dbu[:, dr * 4 * WS:(dr + 1) * 4 * WS]
                                .rearrange("p (g w) -> p g w", g=4)[:, :, 2:WS],
                            in0=wdt[:, q * 4 * T:(q + 1) * 4 * T].rearrange("p (g t) -> p g t", g=4),
                            in1=brc[:, s_ * 2 * T: s_ * 2 * T + T].unsqueeze(1).broadcast_to([128, 4, T]),
                            op=OP.mult)
                    S.add("vector", fdbu, waits=[f"bev_{ci}_{s_}"], tick="dv")
                    S.add("vector", lambda v, pr=pr, j=j: v.tensor_tensor_scan(
                        out=h8q[:],
                        data0=argq[:, pr * 4 * WS:(pr + 1) * 4 * WS],
                        data1=dbu[:, (j % 2) * 4 * WS:(j % 2 + 1) * 4 * WS],
                        initial=0.0, op0=OP.mult, op1=OP.add),
                        waits=[f"dA_{ci}_{j}"], tick="dv", event=f"scn_{ci}_{j}")
                    hr = j % 2
                    hw = []
                    if j >= 2:
                        hw.append(f"am_{ci}_{j - 2}")
                    elif ci > 0:
                        hw.append(f"am_{ci - 1}_{62 + j}")
                    def fschc(v, q=q, s_=s_, hr=hr):
                        v.tensor_copy(
                            hcarry[:, s_ * 16 + q * 4: s_ * 16 + q * 4 + 4],
                            h8q[:].rearrange("p (g w) -> p g w", g=4)[:, :, WS - 1])
                        return v.tensor_tensor(
                            out=hct[:, hr * 4 * T:(hr + 1) * 4 * T].rearrange("p (g t) -> p g t", g=4),
                            in0=h8q[:].rearrange("p (g w) -> p g w", g=4)[:, :, 2:WS],
                            in1=brc[:, s_ * 2 * T + T: s_ * 2 * T + 2 * T].unsqueeze(1)
                                .broadcast_to([128, 4, T]),
                            op=OP.mult)
                    S.add("vector", fschc, waits=hw, tick="dv", event=f"hc_{ci}_{j}")
                    amw = [f"hc_{ci}_{j}"]
                    if s_ == 0:
                        if ci * 4 + q >= 2:
                            pq = (ci * 4 + q) - 2
                            amw.append(f"yq_{pq // 4}_{pq % 4}")
                    def fam(t, q=q, s_=s_, hr=hr):
                        t.matmul(py[q % 2][:, 0:512], C.ident_b[:],
                                 hct[:, hr * 4 * T: hr * 4 * T + 512],
                                 start=(s_ == 0), stop=(s_ == DS - 1))
                        return t.matmul(py[q % 2][:, 512:1024], C.ident_b[:],
                                        hct[:, hr * 4 * T + 512: hr * 4 * T + 1024],
                                        start=(s_ == 0), stop=(s_ == DS - 1))
                    S.add("tensor", fam, waits=amw, tick="pe", event=f"am_{ci}_{j}")
                for gg in range(4):
                    g = q * 4 + gg
                    yw = [f"am_{ci}_{q * 16 + 15}", f"ipev_{ci}_{2 * KI - 1}"]
                    if ci * 4 + q >= 2:
                        pq = (ci * 4 + q) - 2
                        yw.append(f"ygst_{pq // 4}_{pq % 4}")
                    def fyq(v, q=q, g=g, gg=gg, cof=cof):
                        v.scalar_tensor_tensor(
                            out=ytmp[:], in0=xc[:, cof + g * T: cof + (g + 1) * T],
                            scalar=dpar[:, g:g + 1],
                            in1=py[q % 2][:, gg * T:(gg + 1) * T],
                            op0=OP.mult, op1=OP.add)
                        return v.tensor_tensor(
                            out=ygq[:, (q % 2) * 4 * T + gg * T: (q % 2) * 4 * T + (gg + 1) * T],
                            in0=ytmp[:], in1=sz[:, cof + g * T: cof + (g + 1) * T], op=OP.mult)
                    S.add("vector", fyq, waits=yw, tick="dv",
                          event=(f"yq_{ci}_{q}" if gg == 3 else None))
                S.add("sync", lambda s, ci=ci, q=q: s.dma_start(
                    out=yg_dram.rearrange("(a p) t -> p a t", p=128)[:, q * 4:(q + 1) * 4,
                                                                    ci * T:(ci + 1) * T],
                    in_=ygq[:, (q % 2) * 4 * T:(q % 2 + 1) * 4 * T].rearrange("p (g t) -> p g t", g=4)),
                    waits=[f"yq_{ci}_{q}"], tick="ld", dma=True, event=f"ygst_{ci}_{q}")
        S.run()

def stage_mamba_b(p, C, W, yg_dram, resid_f, dst_f, dst_b, T=512, name="mb"):
    nc = p.nc
    NCH = L // T
    with ExitStack() as st:
        w_out = st.enter_context(nc.sbuf_tensor(f"{name}_wo", [128, KI * D], BF16))
        w_m1 = st.enter_context(nc.sbuf_tensor(f"{name}_w1", [128, KD * DI], BF16))
        w_m2 = st.enter_context(nc.sbuf_tensor(f"{name}_w2", [128, KI * D], BF16))
        b1T = st.enter_context(nc.sbuf_tensor(f"{name}_b1", [128, 16], F32))
        b2T = st.enter_context(nc.sbuf_tensor(f"{name}_b2", [128, 8], F32))
        lnwT = st.enter_context(nc.sbuf_tensor(f"{name}_lw", [128, 8], F32))
        yg = st.enter_context(nc.sbuf_tensor(f"{name}_yg", [128, KI * T], BF16))
        mo = st.enter_context(nc.sbuf_tensor(f"{name}_mo", [128, KD * T], BF16))
        h1 = st.enter_context(nc.sbuf_tensor(f"{name}_h1", [128, KI * T], BF16))
        mout = st.enter_context(nc.sbuf_tensor(f"{name}_mt", [128, KD * T], F32))
        resid = st.enter_context(nc.sbuf_tensor(f"{name}_rs", [128, KD * T], F32))
        sq = st.enter_context(nc.sbuf_tensor(f"{name}_sq", [128, KD * T], F32))
        stats = st.enter_context(nc.sbuf_tensor(f"{name}_st", [1, T], F32))
        pp = [st.enter_context(nc.psum_tensor(f"{name}_p{i}", [128, T], F32)) for i in range(2)]
        psS = st.enter_context(nc.psum_tensor(f"{name}_ps", [1, T], F32))
        psR = st.enter_context(nc.psum_tensor(f"{name}_pr", [128, T], F32))

        S = Sched(p, name)
        S.add("gpsimd", lambda g: g.dma_start(out=w_out[:].rearrange("p (a m) -> p a m", a=KI),
                                              in_=W["outT"].rearrange("(a p) m -> p a m", p=128)),
              tick="wl", dma=True, event="wo")
        S.add("gpsimd", lambda g: g.dma_start(out=w_m1[:].rearrange("p (a m) -> p a m", a=KD),
                                              in_=W["m1T"].rearrange("(a p) m -> p a m", p=128)),
              tick="wl", dma=True, event="w1")
        S.add("gpsimd", lambda g: g.dma_start(out=w_m2[:].rearrange("p (a m) -> p a m", a=KI),
                                              in_=W["m2T"].rearrange("(a p) m -> p a m", p=128)),
              tick="wl", dma=True, event="w2")
        S.add("sync", lambda s: s.dma_start(out=b1T[:], in_=W["b1T"]), tick="ld", dma=True)
        S.add("sync", lambda s: s.dma_start(out=b2T[:], in_=W["b2T"]), tick="ld", dma=True)
        S.add("sync", lambda s: s.dma_start(out=lnwT[:], in_=W["lnwT"]),
              tick="ld", dma=True, event="consts")

        last_bank_ev = [None, None]
        for ci in range(NCH):
            S.add("sync", lambda s, ci=ci: s.dma_start(
                out=sb3(yg[:], KI, T), in_=feat_slice(yg_dram, KI, ci, T)),
                waits=([f"opmm_{ci - 1}_{KD - 1}"] if ci > 0 else []),
                tick="ld", dma=True, event=f"ygld_{ci}")
            S.add("sync", lambda s, ci=ci: s.dma_start(
                out=sb3(resid[:], KD, T), in_=feat_slice(resid_f, KD, ci, T)),
                waits=([f"add_{ci - 1}"] if ci > 0 else []),
                tick="ld", dma=True, event=f"rsld_{ci}")
            for mt in range(KD):
                b = mt % 2
                bw = [f"ygld_{ci}", "wo"] + ([last_bank_ev[b]] if last_bank_ev[b] else [])
                def fmm(t, mt=mt, b=b):
                    for kt in range(KI):
                        inst = t.matmul(pp[b][:, 0:T],
                                        w_out[:, kt * D + mt * 128: kt * D + mt * 128 + 128],
                                        yg[:, kt * T:(kt + 1) * T], start=(kt == 0), stop=(kt == KI - 1))
                    return inst
                S.add("tensor", fmm, waits=bw, tick="pe", event=f"opmm_{ci}_{mt}")
                ew = [f"opmm_{ci}_{mt}"] + ([f"m1_{ci - 1}_{KI - 1}"] if ci > 0 else [])
                S.add("vector", lambda v, mt=mt, b=b: v.tensor_copy(
                    mo[:, mt * T:(mt + 1) * T], pp[b][:, 0:T]),
                    waits=ew, tick="dv", event=f"moev_{ci}_{mt}")
                last_bank_ev[b] = f"moev_{ci}_{mt}"
            for et in range(KI):
                b = et % 2
                bw = [f"moev_{ci}_{KD - 1}", "w1", last_bank_ev[b]]
                def fmm(t, et=et, b=b):
                    for kt in range(KD):
                        inst = t.matmul(pp[b][:, 0:T],
                                        w_m1[:, kt * DI + et * 128: kt * DI + et * 128 + 128],
                                        mo[:, kt * T:(kt + 1) * T], start=(kt == 0), stop=(kt == KD - 1))
                    return inst
                S.add("tensor", fmm, waits=bw, tick="pe", event=f"m1_{ci}_{et}")
                ew = [f"m1_{ci}_{et}", "consts"] + ([f"m2_{ci - 1}_{KD - 1}"] if ci > 0 and et < 2 else [])
                S.add("scalar", lambda A, et=et, b=b: A.activation(
                    out=h1[:, et * T:(et + 1) * T], in_=pp[b][:, 0:T],
                    func=AF.Gelu, bias=b1T[:, et:et + 1]),
                    waits=ew, tick="ex", event=f"h1_{ci}_{et}")
                last_bank_ev[b] = f"h1_{ci}_{et}"
            for mt in range(KD):
                b = mt % 2
                bw = [f"h1_{ci}_{KI - 1}", "w2", last_bank_ev[b]]
                def fmm(t, mt=mt, b=b):
                    for kt in range(KI):
                        inst = t.matmul(pp[b][:, 0:T],
                                        w_m2[:, kt * D + mt * 128: kt * D + mt * 128 + 128],
                                        h1[:, kt * T:(kt + 1) * T], start=(kt == 0), stop=(kt == KI - 1))
                    return inst
                S.add("tensor", fmm, waits=bw, tick="pe", event=f"m2_{ci}_{mt}")
                ew = [f"m2_{ci}_{mt}"] + ([f"nrm_{ci - 1}"] if ci > 0 and mt < 2 else [])
                S.add("scalar", lambda A, mt=mt, b=b: A.activation(
                    out=mout[:, mt * T:(mt + 1) * T], in_=pp[b][:, 0:T],
                    func=AF.Gelu, bias=b2T[:, mt:mt + 1]),
                    waits=ew, tick="ex", event=f"mt_{ci}_{mt}")
                last_bank_ev[b] = f"mt_{ci}_{mt}"
            S.add("vector", lambda v: v.tensor_tensor(out=mout[:], in0=mout[:], in1=resid[:], op=OP.add),
                  waits=[f"mt_{ci}_{KD - 1}", f"rsld_{ci}"], tick="dv", event=f"add_{ci}")
            sqw = [f"add_{ci}"] + ([f"sums_{ci - 1}", f"stf_{ci - 1}", f"stb_{ci - 1}"] if ci > 0 else [])
            S.add("scalar", lambda A: A.activation(out=sq[:], in_=mout[:], func=AF.Square),
                  waits=sqw, tick="ex", event=f"sq_{ci}")
            S.add("tensor", lambda t: sum_matmuls(p, t, psS, sq[:], KD, T, C.ones_col),
                  waits=[f"sq_{ci}"] + ([f"rstd_{ci - 1}"] if ci > 0 else []),
                  tick="pe", event=f"sums_{ci}")
            S.add("scalar", lambda A: rstd_from_ms(p, A, stats[0:1, 0:T], psS[0:1, 0:T], 1.0 / D, EPS_RMS),
                  waits=[f"sums_{ci}"], tick="ex", event=f"rstd_{ci}")
            S.add("tensor", lambda t: bcast_matmul(t, psR, stats[0:1, 0:T], T, C.ones_row),
                  waits=[f"rstd_{ci}"] + ([f"nrm_{ci - 1}"] if ci > 0 else []),
                  tick="pe", event=f"bc_{ci}")
            def fnorm(v):
                for a in range(KD):
                    sl = slice(a * T, (a + 1) * T)
                    inst = v.scalar_tensor_tensor(out=sq[:, sl], in0=mout[:, sl],
                                                  scalar=lnwT[:, a:a + 1], in1=psR[:, 0:T],
                                                  op0=OP.mult, op1=OP.mult)
                return inst
            S.add("vector", fnorm, waits=[f"bc_{ci}"] +
                  ([f"stf_{ci - 1}", f"stb_{ci - 1}"] if ci > 0 else []),
                  tick="dv", event=f"nrm_{ci}")
            if dst_f is not None:
                S.add("gpsimd", lambda g, ci=ci: g.dma_start(
                    out=feat_slice(dst_f, KD, ci, T), in_=sb3(sq[:], KD, T)),
                    waits=[f"nrm_{ci}"], tick="wl", dma=True, event=f"stf_{ci}")
            else:
                S.events[f"stf_{ci}"] = S.events[f"nrm_{ci}"]
            if dst_b is not None:
                S.add("gpsimd", lambda g, ci=ci: g.dma_start(
                    out=feat_slice(dst_b, KD, ci, T), in_=sb3(sq[:], KD, T)),
                    waits=[f"nrm_{ci}"], tick="wl", dma=True, event=f"stb_{ci}")
            else:
                S.events[f"stb_{ci}"] = S.events[f"nrm_{ci}"]
        S.run()

def stage_mamba_tail(p, C, ln2wT_dram, xcur_f, u1_f, x0_f, dst_f, T=512, name="mt"):
    nc = p.nc
    NCH = L // T
    with ExitStack() as st:
        lnwT = st.enter_context(nc.sbuf_tensor(f"{name}_lw", [128, 8], F32))
        a1 = st.enter_context(nc.sbuf_tensor(f"{name}_a", [128, KD * T], F32))
        a2 = st.enter_context(nc.sbuf_tensor(f"{name}_b", [128, KD * T], F32))
        sq = st.enter_context(nc.sbuf_tensor(f"{name}_sq", [128, KD * T], F32))
        stats = st.enter_context(nc.sbuf_tensor(f"{name}_st", [1, 4 * T], F32))
        psA = st.enter_context(nc.psum_tensor(f"{name}_pA", [1, T], F32))
        psB = st.enter_context(nc.psum_tensor(f"{name}_pB", [1, T], F32))
        psR = st.enter_context(nc.psum_tensor(f"{name}_pR", [128, T], F32))
        psM = st.enter_context(nc.psum_tensor(f"{name}_pM", [128, T], F32))
        p.ph("sync", lambda p, s: p.dma(s.dma_start(out=lnwT[:], in_=ln2wT_dram)))
        for ci in range(NCH):
            def ld(p, s, ci=ci):
                load_chunk(p, s, a1, xcur_f, KD, ci, T)
                load_chunk(p, s, a2, u1_f, KD, ci, T)
            p.ph("sync", ld)
            p.ph("vector", lambda p, v: p.inc(v.tensor_tensor(out=a1[:], in0=a1[:], in1=a2[:], op=OP.add)))
            p.ph("scalar", lambda p, A: p.inc(A.activation(out=sq[:], in_=a1[:], func=AF.Square)))
            p.ph("tensor", lambda p, t: p.inc(sum_matmuls(p, t, psA, sq[:], KD, T, C.ones_col)))
            p.ph("scalar", lambda p, A: p.inc(rstd_from_ms(p, A, stats[0:1, 0:T], psA[0:1, 0:T], 1.0 / D, EPS_RMS)))
            p.ph("tensor", lambda p, t: p.inc(bcast_matmul(t, psR, stats[0:1, 0:T], T, C.ones_row)))
            def fbr(p, v):
                for a in range(KD):
                    sl = slice(a * T, (a + 1) * T)
                    inst = v.scalar_tensor_tensor(out=a1[:, sl], in0=a1[:, sl],
                                                  scalar=lnwT[:, a:a + 1], in1=psR[:, 0:T],
                                                  op0=OP.mult, op1=OP.mult)
                p.inc(inst)
            p.ph("vector", fbr)
            p.ph("scalar", lambda p, A: p.inc(A.activation(out=sq[:], in_=a1[:], func=AF.Square)))
            p.ph("tensor", lambda p, t: p.inc(sum_matmuls(p, t, psA, a1[:], KD, T, C.ones_col)))
            p.ph("tensor", lambda p, t: p.inc(sum_matmuls(p, t, psB, sq[:], KD, T, C.ones_col)))
            def fstats(p, v):
                v.tensor_scalar(out=stats[0:1, 0:T], in0=psA[0:1, 0:T], scalar1=1.0 / D,
                                scalar2=None, op0=OP.mult)
                v.tensor_tensor(out=stats[0:1, T:2 * T], in0=stats[0:1, 0:T],
                                in1=stats[0:1, 0:T], op=OP.mult)
                p.inc(v.scalar_tensor_tensor(out=stats[0:1, T:2 * T], in0=psB[0:1, 0:T],
                                             scalar=1.0 / D, in1=stats[0:1, T:2 * T],
                                             op0=OP.mult, op1=OP.subtract))
            p.ph("vector", fstats)
            p.ph("scalar", lambda p, A: p.inc(
                rstd_from_ms(p, A, stats[0:1, 2 * T:3 * T], stats[0:1, T:2 * T], 1.0, EPS_LN)))
            p.ph("vector", lambda p, v: p.inc(
                v.tensor_tensor(out=stats[0:1, 3 * T:4 * T], in0=stats[0:1, 0:T],
                                in1=stats[0:1, 2 * T:3 * T], op=OP.mult)))
            def fb(p, t):
                bcast_matmul(t, psR, stats[0:1, 2 * T:3 * T], T, C.ones_row)
                p.inc(bcast_matmul(t, psM, stats[0:1, 3 * T:4 * T], T, C.ones_row))
            p.ph("tensor", fb)
            p.ph("sync", lambda p, s, ci=ci: load_chunk(p, s, a2, x0_f, KD, ci, T))
            def fnorm(p, v):
                for a in range(KD):
                    sl = slice(a * T, (a + 1) * T)
                    v.tensor_tensor(out=sq[:, sl], in0=a1[:, sl], in1=psR[:, 0:T], op=OP.mult)
                    v.tensor_tensor(out=sq[:, sl], in0=sq[:, sl], in1=psM[:, 0:T], op=OP.subtract)
                    inst = v.tensor_tensor(out=sq[:, sl], in0=sq[:, sl], in1=a2[:, sl], op=OP.add)
                p.inc(inst)
            p.ph("vector", fnorm)
            p.ph("sync", lambda p, s, ci=ci: store_chunk(p, s, dst_f, sq, KD, ci, T))


def _t(a):
    return np.ascontiguousarray(np.asarray(a).T)

def _cols(a, k):
    return np.ascontiguousarray(np.asarray(a).reshape(k, 128).T)


def build_masks():
    m = np.zeros((128, 4 * 512), dtype=ml_dtypes.bfloat16)
    for j in range(4):
        pp = np.arange(128)[:, None] + 128 * j
        ff = np.arange(512)[None, :]
        m[:, j * 512:(j + 1) * 512] = (pp <= ff).astype(np.float32)
    return m


def stage_moe_pre(p, C, W, x1_f, u2_b, comb_dram, T=512, name="mp"):
    nc = p.nc
    NCH = L // T
    with ExitStack() as st:
        gw = st.enter_context(nc.sbuf_tensor(f"{name}_gw", [128, KD * 8], BF16))
        gbr = st.enter_context(nc.sbuf_tensor(f"{name}_gb", [128, 8], F32))
        x = st.enter_context(nc.sbuf_tensor(f"{name}_x", [128, KD * T], F32))
        xb = st.enter_context(nc.sbuf_tensor(f"{name}_xb", [128, KD * T], BF16))
        sq = st.enter_context(nc.sbuf_tensor(f"{name}_sq", [128, KD * T], F32))
        stats = st.enter_context(nc.sbuf_tensor(f"{name}_st", [1, 4 * T], F32))
        gsc = st.enter_context(nc.sbuf_tensor(f"{name}_gs", [128, 64], F32))
        comb = st.enter_context(nc.sbuf_tensor(f"{name}_cb", [8, T], F32))
        psA = st.enter_context(nc.psum_tensor(f"{name}_pA", [1, T], F32))
        psB = st.enter_context(nc.psum_tensor(f"{name}_pB", [1, T], F32))
        psR = st.enter_context(nc.psum_tensor(f"{name}_pR", [128, T], F32))
        psM = st.enter_context(nc.psum_tensor(f"{name}_pM", [128, T], F32))
        psG = st.enter_context(nc.psum_tensor(f"{name}_pG", [128, 8], F32))
        psT = st.enter_context(nc.psum_tensor(f"{name}_pT", [8, 128], F32))

        def lw(p, g):
            p.dma(g.dma_start(out=gw[:].rearrange("p (a m) -> p a m", a=KD),
                              in_=W["gate_w"].rearrange("(a p) m -> p a m", p=128)))
        p.ph("gpsimd", lw)
        p.ph("sync", lambda p, s: p.dma(s.dma_start(out=gbr[:], in_=W["gate_b_rep"])))

        for ci in range(NCH):
            p.ph("sync", lambda p, s, ci=ci: load_chunk(p, s, x, x1_f, KD, ci, T))
            p.ph("scalar", lambda p, A: p.inc(A.activation(out=sq[:], in_=x[:], func=AF.Square)))
            p.ph("tensor", lambda p, t: p.inc(sum_matmuls(p, t, psA, x[:], KD, T, C.ones_col)))
            p.ph("tensor", lambda p, t: p.inc(sum_matmuls(p, t, psB, sq[:], KD, T, C.ones_col)))
            def fstats(p, v):
                v.tensor_scalar(out=stats[0:1, 0:T], in0=psA[0:1, 0:T], scalar1=1.0 / D,
                                scalar2=None, op0=OP.mult)
                v.tensor_tensor(out=stats[0:1, T:2 * T], in0=stats[0:1, 0:T],
                                in1=stats[0:1, 0:T], op=OP.mult)
                p.inc(v.scalar_tensor_tensor(out=stats[0:1, T:2 * T], in0=psB[0:1, 0:T],
                                             scalar=1.0 / D, in1=stats[0:1, T:2 * T],
                                             op0=OP.mult, op1=OP.subtract))
            p.ph("vector", fstats)
            p.ph("scalar", lambda p, A: p.inc(
                rstd_from_ms(p, A, stats[0:1, 2 * T:3 * T], stats[0:1, T:2 * T], 1.0, EPS_LN)))
            p.ph("vector", lambda p, v: p.inc(
                v.tensor_tensor(out=stats[0:1, 3 * T:4 * T], in0=stats[0:1, 0:T],
                                in1=stats[0:1, 2 * T:3 * T], op=OP.mult)))
            def fb(p, t):
                bcast_matmul(t, psR, stats[0:1, 2 * T:3 * T], T, C.ones_row)
                p.inc(bcast_matmul(t, psM, stats[0:1, 3 * T:4 * T], T, C.ones_row))
            p.ph("tensor", fb)
            def fnorm(p, v):
                for a in range(KD):
                    sl = slice(a * T, (a + 1) * T)
                    v.tensor_tensor(out=sq[:, sl], in0=x[:, sl], in1=psR[:, 0:T], op=OP.mult)
                    v.tensor_tensor(out=sq[:, sl], in0=sq[:, sl], in1=psM[:, 0:T], op=OP.subtract)
                    inst = v.tensor_copy(xb[:, sl], sq[:, sl])
                p.inc(inst)
            p.ph("vector", fnorm)
            p.ph("gpsimd", lambda p, g, ci=ci: p.dma(
                g.dma_start(out=feat_slice(u2_b, KD, ci, T), in_=sb3(sq[:], KD, T))))

            for ts in range(T // 128):
                def fg(p, t, ts=ts):
                    for kt in range(KD):
                        inst = t.matmul(psG[:, 0:8], xb[:, kt * T + ts * 128: kt * T + ts * 128 + 128],
                                        gw[:, kt * 8:(kt + 1) * 8], start=(kt == 0), stop=(kt == KD - 1))
                    p.inc(inst)
                p.ph("tensor", fg)
                g0 = gsc[:, 0:8]; m1 = gsc[:, 8:16]; m2 = gsc[:, 16:24]; t0 = gsc[:, 24:32]
                mx = gsc[:, 32:33]; v2 = gsc[:, 33:34]; den = gsc[:, 34:35]
                nv2 = gsc[:, 35:36]; r = gsc[:, 36:37]; c2 = gsc[:, 37:38]
                p.ph("vector", lambda p, v: p.inc(
                    v.tensor_tensor(out=g0, in0=psG[:, 0:8], in1=gbr[:], op=OP.add)))
                p.ph("vector", lambda p, v: p.inc(v.reduce_max(mx, g0, axis=AX.X, negate=True)))
                p.ph("scalar", lambda p, A: p.inc(
                    A.activation(out=g0, in_=g0, func=AF.Exp, bias=mx)))
                p.ph("scalar", lambda p, A: p.inc(
                    A.activation(out=m1, in_=g0, func=AF.Sign, bias=-1.0)))
                p.ph("vector", lambda p, v: p.inc(
                    v.tensor_scalar(out=m1, in0=m1, scalar1=1.0, scalar2=None, op0=OP.add)))
                p.ph("vector", lambda p, v: p.inc(
                    v.scalar_tensor_tensor(out=t0, in0=m1, scalar=-2.0, in1=g0,
                                           op0=OP.mult, op1=OP.add)))
                def fred(p, v):
                    v.reduce_max(v2, t0, axis=AX.X)
                    p.inc(v.reduce_max(nv2, t0, axis=AX.X, negate=True))
                p.ph("vector", fred)
                p.ph("scalar", lambda p, A: p.inc(
                    A.activation(out=m2, in_=t0, func=AF.Sign, bias=nv2)))
                def fcw1(p, v):
                    v.tensor_scalar(out=m2, in0=m2, scalar1=1.0, scalar2=None, op0=OP.add)
                    p.inc(v.tensor_scalar(out=den, in0=v2, scalar1=1.0, scalar2=None, op0=OP.add))
                p.ph("vector", fcw1)
                p.ph("vector", lambda p, v: p.inc(v.reciprocal(out=r, in_=den)))
                def fcw2(p, v):
                    v.tensor_tensor(out=c2, in0=v2, in1=r, op=OP.mult)
                    p.inc(v.tensor_tensor(out=m1, in0=m1, in1=r.broadcast_to([128, 8]), op=OP.mult))
                p.ph("vector", fcw2)
                p.ph("vector", lambda p, v: p.inc(
                    v.tensor_tensor(out=m2, in0=m2, in1=c2.broadcast_to([128, 8]), op=OP.mult)))
                p.ph("vector", lambda p, v: p.inc(
                    v.tensor_tensor(out=gsc[:, 40:48], in0=m1, in1=m2, op=OP.add)))
                p.ph("tensor", lambda p, t, ts=ts: p.inc(
                    t.transpose(psT[:], gsc[:, 40:48], C.ident_f[:])))
                p.ph("vector", lambda p, v, ts=ts: p.inc(
                    v.tensor_copy(comb[:, ts * 128:(ts + 1) * 128], psT[0:8, 0:128])))
            p.ph("sync", lambda p, s, ci=ci: p.dma(s.dma_start(
                out=comb_dram[ci * 8 * T:(ci + 1) * 8 * T].rearrange("(r t) -> r t", r=8),
                in_=comb[:])))


def stage_moe(p, C, W, u2_b, comb_dram, acc_f, T=512, name="moe"):
    nc = p.nc
    NCH = L // T
    with ExitStack() as st:
        w1 = st.enter_context(nc.sbuf_tensor(f"{name}_w1", [128, KD * FF], BF16))
        w2 = st.enter_context(nc.sbuf_tensor(f"{name}_w2", [128, KF * D], BF16))
        b1T = st.enter_context(nc.sbuf_tensor(f"{name}_b1", [128, NE * KF], F32))
        b2m = st.enter_context(nc.sbuf_tensor(f"{name}_b2", [8, D], BF16))
        xb = st.enter_context(nc.sbuf_tensor(f"{name}_xb", [128, 2 * KD * T], BF16))
        h = st.enter_context(nc.sbuf_tensor(f"{name}_h", [128, KF * T], BF16))
        cb8 = st.enter_context(nc.sbuf_tensor(f"{name}_c8", [8, T], BF16))
        cbe = st.enter_context(nc.sbuf_tensor(f"{name}_cf", [1, 2 * T], F32))
        ceb = st.enter_context(nc.sbuf_tensor(f"{name}_ce", [128, T], F32))
        contrib = st.enter_context(nc.sbuf_tensor(f"{name}_ct", [128, KD * T], F32))
        pm = [st.enter_context(nc.psum_tensor(f"{name}_p{i}", [128, T], F32)) for i in range(4)]
        psc = st.enter_context(nc.psum_tensor(f"{name}_pc", [128, T], F32))

        S = Sched(p, name)
        S.add("sync", lambda s: s.dma_start(out=b1T[:].rearrange("p (e m) -> p e m", e=NE),
                                            in_=W["moe_b1T"]), tick="ld", dma=True)
        S.add("gpsimd", lambda g_: g_.dma_start(out=b2m[:], in_=W["moe_b2"]),
              tick="wl", dma=True, event="b2m")

        def wload(e, which):
            def f(g, e=e, which=which):
                if which == 0:
                    return g.dma_start(out=w1[:].rearrange("p (a m) -> p a m", a=KD),
                                       in_=W["moe_w1"][e].rearrange("(a p) m -> p a m", p=128))
                return g.dma_start(out=w2[:].rearrange("p (a m) -> p a m", a=KF),
                                   in_=W["moe_w2"][e].rearrange("(a p) m -> p a m", p=128))
            return f
        S.add("gpsimd", wload(0, 0), tick="wl", dma=True, event="w1_0")
        S.add("gpsimd", wload(0, 1), tick="wl", dma=True, event="w2_0")

        for ci in range(NCH):
            S.add("gpsimd", lambda g_, ci=ci: g_.dma_start(
                out=cb8[:], in_=comb_dram[ci * 8 * T:(ci + 1) * 8 * T].rearrange("(r t) -> r t", r=8)),
                waits=([("st", 16 * ci)] if ci > 0 else []),
                tick="wl", dma=True, event=f"cb8_{ci}")
            for mt in range(KD):
                g = ci * KD + mt
                bank = pm[mt % 2]
                S.add("tensor", lambda t, mt=mt, bank=bank: t.matmul(
                    bank[:, 0:T], b2m[:, mt * 128:(mt + 1) * 128], cb8[:], start=True, stop=True),
                    waits=[f"cb8_{ci}", "b2m"] + ([("dv", g - 1)] if g >= 2 else []),
                    tick="pe", event=f"bmm_{ci}_{mt}")
                S.add("vector", lambda v, mt=mt, bank=bank: v.tensor_copy(
                    contrib[:, mt * T:(mt + 1) * T], bank[:, 0:T]),
                    waits=[f"bmm_{ci}_{mt}"] + ([("st", 16 * ci)] if ci > 0 else []),
                    tick="dv", event=f"bcp_{ci}_{mt}")
            S.add("gpsimd", lambda g_, ci=ci: g_.dma_start(
                out=feat_slice(acc_f, KD, ci, T), in_=sb3(contrib[:], KD, T)),
                waits=[f"bcp_{ci}_{KD - 1}"], tick="st", dma=True, event=f"bst_{ci}")
        bias_dv = S.ctr("dv").n

        for e in range(NE):
            for ci in range(NCH):
                u = e * NCH + ci
                xs = (u % 2) * KD * T
                cs = (u % 2) * T
                lw = [f"mm1_{u - 2}_{KF - 1}"] if u >= 2 else []
                S.add("sync", lambda s, ci=ci, xs=xs: s.dma_start(
                    out=sb3(xb[:, xs:xs + KD * T], KD, T), in_=feat_slice(u2_b, KD, ci, T)),
                    waits=lw, tick="ld", dma=True)
                S.add("sync", lambda s, ci=ci, e=e, cs=cs: s.dma_start(
                    out=cbe[0:1, cs:cs + T],
                    in_=comb_dram[(ci * 8 + e) * T:(ci * 8 + e + 1) * T].unsqueeze(0)),
                    tick="ld", dma=True, event=f"ld_{u}")
                S.add("tensor", lambda t, cs=cs: bcast_matmul(t, psc, cbe[0:1, cs:cs + T], T, C.ones_row),
                      waits=[f"ld_{u}"] + ([f"cecp_{u - 1}"] if u > 0 else []),
                      tick="pe", event=f"cemm_{u}")
                S.add("vector", lambda v: v.tensor_copy(ceb[:], psc[:, 0:T]),
                      waits=[f"cemm_{u}"] + ([f"ct_{u - 1}_{KD - 1}"] if u > 0 else []),
                      tick="dv", event=f"cecp_{u}")
                for et in range(KF):
                    bank = pm[et % 2]
                    def f_mm1(t, et=et, bank=bank, xs=xs):
                        for kt in range(KD):
                            inst = t.matmul(bank[:, 0:T],
                                            w1[:, kt * FF + et * 128: kt * FF + et * 128 + 128],
                                            xb[:, xs + kt * T: xs + kt * T + T],
                                            start=(kt == 0), stop=(kt == KD - 1))
                        return inst
                    wts = [f"ld_{u}", f"w1_{e}"]
                    if et >= 2:
                        wts.append(f"h_{u}_{et - 2}")
                    elif u > 0:
                        wts.append(f"h_{u - 1}_{KF - 2 + et}")
                    else:
                        wts.append(("dv", bias_dv - 2 + et + 1))
                    S.add("tensor", f_mm1, waits=wts, tick="pe", event=f"mm1_{u}_{et}")
                    rw = [f"mm1_{u}_{et}"]
                    if u > 0:
                        rw.append(f"mm2_{u - 1}_{KD - 1}")
                    S.add("scalar", lambda A, et=et, e=e, bank=bank: A.activation(
                        out=h[:, et * T:(et + 1) * T], in_=bank[:, 0:T],
                        func=AF.Relu, bias=b1T[:, e * KF + et: e * KF + et + 1]),
                        waits=rw, tick="ex", event=f"h_{u}_{et}")
                if ci == NCH - 1 and e < NE - 1:
                    S.add("gpsimd", wload(e + 1, 0), waits=[f"mm1_{u}_{KF - 1}"],
                          tick="wl", dma=True, event=f"w1_{e + 1}")
                for mt in range(KD):
                    bank = pm[2 + mt % 2]
                    def f_mm2(t, mt=mt, bank=bank):
                        for kt in range(KF):
                            inst = t.matmul(bank[:, 0:T],
                                            w2[:, kt * D + mt * 128: kt * D + mt * 128 + 128],
                                            h[:, kt * T:(kt + 1) * T],
                                            start=(kt == 0), stop=(kt == KF - 1))
                        return inst
                    wts = [f"h_{u}_{KF - 1}", f"w2_{e}"]
                    if mt >= 2:
                        wts.append(f"ct_{u}_{mt - 2}")
                    elif u > 0:
                        wts.append(f"ct_{u - 1}_{KD - 2 + mt}")
                    S.add("tensor", f_mm2, waits=wts, tick="pe", event=f"mm2_{u}_{mt}")
                    cw = [f"mm2_{u}_{mt}", f"cecp_{u}"]
                    if mt == 0:
                        cw.append(f"acc_{u - 1}" if u > 0 else f"bst_{NCH - 1}")
                    S.add("vector", lambda v, mt=mt: v.tensor_tensor(
                        out=contrib[:, mt * T:(mt + 1) * T], in0=pm[2 + mt % 2][:, 0:T],
                        in1=ceb[:], op=OP.mult),
                        waits=cw, tick="dv", event=f"ct_{u}_{mt}")
                S.add("gpsimd", lambda g_, ci=ci: g_.dma_start(
                    out=feat_slice(acc_f, KD, ci, T), in_=sb3(contrib[:], KD, T), accum_op=OP.add),
                    waits=[f"ct_{u}_{KD - 1}"], tick="st", dma=True, event=f"acc_{u}")
                if ci == NCH - 1 and e < NE - 1:
                    S.add("gpsimd", wload(e + 1, 1), waits=[f"mm2_{u}_{KD - 1}"],
                          tick="wl", dma=True, event=f"w2_{e + 1}")
        S.run()


def stage_moe_old(p, C, W, u2_b, comb_dram, acc_f, T=512, name="moeo"):
    nc = p.nc
    NCH = L // T
    with ExitStack() as st:
        w1 = st.enter_context(nc.sbuf_tensor(f"{name}_w1", [128, KD * FF], BF16))
        w2 = st.enter_context(nc.sbuf_tensor(f"{name}_w2", [128, KF * D], BF16))
        b1T = st.enter_context(nc.sbuf_tensor(f"{name}_b1", [128, NE * KF], F32))
        b2m = st.enter_context(nc.sbuf_tensor(f"{name}_b2", [8, D], F32))
        xb = st.enter_context(nc.sbuf_tensor(f"{name}_xb", [128, KD * T], BF16))
        h = st.enter_context(nc.sbuf_tensor(f"{name}_h", [128, KF * T], BF16))
        cb8 = st.enter_context(nc.sbuf_tensor(f"{name}_c8", [8, T], F32))
        cbe = st.enter_context(nc.sbuf_tensor(f"{name}_cf", [1, T], F32))
        ceb = st.enter_context(nc.sbuf_tensor(f"{name}_ce", [128, T], F32))
        contrib = st.enter_context(nc.sbuf_tensor(f"{name}_ct", [128, KD * T], F32))
        ps0 = st.enter_context(nc.psum_tensor(f"{name}_p0", [128, T], F32))
        ps1 = st.enter_context(nc.psum_tensor(f"{name}_p1", [128, T], F32))
        psc = st.enter_context(nc.psum_tensor(f"{name}_pc", [128, T], F32))
        psx = [ps0, ps1]

        def lb(p, s):
            p.dma(s.dma_start(out=b1T[:].rearrange("p (e m) -> p e m", e=NE),
                              in_=W["moe_b1T"]))
            p.dma(s.dma_start(out=b2m[:], in_=W["moe_b2"]))
        p.ph("sync", lb)

        for ci in range(NCH):
            def lcm(p, s, ci=ci):
                p.dma(s.dma_start(out=cb8[:], in_=comb_dram[ci * 8 * T:(ci + 1) * 8 * T]
                                  .rearrange("(r t) -> r t", r=8)))
            p.ph("sync", lcm)
            for mt in range(KD):
                bank = psx[mt % 2]
                p.ph("tensor", lambda p, t, mt=mt, bank=bank: p.inc(
                    t.matmul(bank[:, 0:T], b2m[:, mt * 128:(mt + 1) * 128], cb8[:],
                             start=True, stop=True)))
                p.ph("vector", lambda p, v, mt=mt, bank=bank: p.inc(
                    v.tensor_copy(contrib[:, mt * T:(mt + 1) * T], bank[:, 0:T])))
            p.ph("sync", lambda p, s, ci=ci: store_chunk(p, s, acc_f, contrib, KD, ci, T))

        for e in range(NE):
            def lw(p, g, e=e):
                p.dma(g.dma_start(out=w1[:].rearrange("p (a m) -> p a m", a=KD),
                                  in_=W["moe_w1"][e].rearrange("(a p) m -> p a m", p=128)))
                p.dma(g.dma_start(out=w2[:].rearrange("p (a m) -> p a m", a=KF),
                                  in_=W["moe_w2"][e].rearrange("(a p) m -> p a m", p=128)))
            p.ph("gpsimd", lw)
            for ci in range(NCH):
                def lx(p, s, ci=ci, e=e):
                    load_chunk(p, s, xb, u2_b, KD, ci, T)
                    p.dma(s.dma_start(out=cbe[:], in_=comb_dram[(ci * 8 + e) * T:(ci * 8 + e + 1) * T]
                                      .unsqueeze(0)))
                p.ph("sync", lx)
                p.ph("tensor", lambda p, t: p.inc(
                    bcast_matmul(t, psc, cbe[0:1, 0:T], T, C.ones_row)))
                p.ph("vector", lambda p, v: p.inc(v.tensor_copy(ceb[:], psc[:, 0:T])))
                for et in range(KF):
                    bank = psx[et % 2]
                    def fmm(p, t, et=et, bank=bank):
                        for kt in range(KD):
                            inst = t.matmul(bank[:, 0:T],
                                            w1[:, kt * FF + et * 128: kt * FF + et * 128 + 128],
                                            xb[:, kt * T:(kt + 1) * T],
                                            start=(kt == 0), stop=(kt == KD - 1))
                        p.inc(inst)
                    p.ph("tensor", fmm)
                    p.ph("scalar", lambda p, A, et=et, e=e, bank=bank: p.inc(
                        A.activation(out=h[:, et * T:(et + 1) * T], in_=bank[:, 0:T],
                                     func=AF.Relu, bias=b1T[:, e * KF + et: e * KF + et + 1])))
                for mt in range(KD):
                    bank = psx[mt % 2]
                    def fmm(p, t, mt=mt, bank=bank):
                        for kt in range(KF):
                            inst = t.matmul(bank[:, 0:T],
                                            w2[:, kt * D + mt * 128: kt * D + mt * 128 + 128],
                                            h[:, kt * T:(kt + 1) * T],
                                            start=(kt == 0), stop=(kt == KF - 1))
                        p.inc(inst)
                    p.ph("tensor", fmm)
                    p.ph("vector", lambda p, v, mt=mt, bank=bank: p.inc(
                        v.tensor_tensor(out=contrib[:, mt * T:(mt + 1) * T], in0=bank[:, 0:T],
                                        in1=ceb[:], op=OP.mult)))
                p.ph("gpsimd", lambda p, g, ci=ci: p.dma(
                    g.dma_start(out=feat_slice(acc_f, KD, ci, T), in_=sb3(contrib[:], KD, T),
                                accum_op=OP.add)))


def stage_add_ln(p, C, src_f, base_f, dst_f, T=512, name="al"):
    nc = p.nc
    NCH = L // T
    with ExitStack() as st:
        a1 = st.enter_context(nc.sbuf_tensor(f"{name}_a", [128, KD * T], F32))
        a2 = st.enter_context(nc.sbuf_tensor(f"{name}_b", [128, KD * T], F32))
        sq = st.enter_context(nc.sbuf_tensor(f"{name}_sq", [128, KD * T], F32))
        stats = st.enter_context(nc.sbuf_tensor(f"{name}_st", [1, 4 * T], F32))
        psA = st.enter_context(nc.psum_tensor(f"{name}_pA", [1, T], F32))
        psB = st.enter_context(nc.psum_tensor(f"{name}_pB", [1, T], F32))
        psR = st.enter_context(nc.psum_tensor(f"{name}_pR", [128, T], F32))
        psM = st.enter_context(nc.psum_tensor(f"{name}_pM", [128, T], F32))
        for ci in range(NCH):
            p.ph("sync", lambda p, s, ci=ci: load_chunk(p, s, a1, src_f, KD, ci, T))
            p.ph("scalar", lambda p, A: p.inc(A.activation(out=sq[:], in_=a1[:], func=AF.Square)))
            p.ph("tensor", lambda p, t: p.inc(sum_matmuls(p, t, psA, a1[:], KD, T, C.ones_col)))
            p.ph("tensor", lambda p, t: p.inc(sum_matmuls(p, t, psB, sq[:], KD, T, C.ones_col)))
            def fstats(p, v):
                v.tensor_scalar(out=stats[0:1, 0:T], in0=psA[0:1, 0:T], scalar1=1.0 / D,
                                scalar2=None, op0=OP.mult)
                v.tensor_tensor(out=stats[0:1, T:2 * T], in0=stats[0:1, 0:T],
                                in1=stats[0:1, 0:T], op=OP.mult)
                p.inc(v.scalar_tensor_tensor(out=stats[0:1, T:2 * T], in0=psB[0:1, 0:T],
                                             scalar=1.0 / D, in1=stats[0:1, T:2 * T],
                                             op0=OP.mult, op1=OP.subtract))
            p.ph("vector", fstats)
            p.ph("scalar", lambda p, A: p.inc(
                rstd_from_ms(p, A, stats[0:1, 2 * T:3 * T], stats[0:1, T:2 * T], 1.0, EPS_LN)))
            p.ph("vector", lambda p, v: p.inc(
                v.tensor_tensor(out=stats[0:1, 3 * T:4 * T], in0=stats[0:1, 0:T],
                                in1=stats[0:1, 2 * T:3 * T], op=OP.mult)))
            def fb(p, t):
                bcast_matmul(t, psR, stats[0:1, 2 * T:3 * T], T, C.ones_row)
                p.inc(bcast_matmul(t, psM, stats[0:1, 3 * T:4 * T], T, C.ones_row))
            p.ph("tensor", fb)
            p.ph("sync", lambda p, s, ci=ci: load_chunk(p, s, a2, base_f, KD, ci, T))
            def fnorm(p, v):
                for a in range(KD):
                    sl = slice(a * T, (a + 1) * T)
                    v.tensor_tensor(out=sq[:, sl], in0=a1[:, sl], in1=psR[:, 0:T], op=OP.mult)
                    v.tensor_tensor(out=sq[:, sl], in0=sq[:, sl], in1=psM[:, 0:T], op=OP.subtract)
                    inst = v.tensor_tensor(out=sq[:, sl], in0=sq[:, sl], in1=a2[:, sl], op=OP.add)
                p.inc(inst)
            p.ph("vector", fnorm)
            p.ph("sync", lambda p, s, ci=ci: store_chunk(p, s, dst_f, sq, KD, ci, T))


def stage_attn_qkv(p, C, W, x2_f, qkv_dram, vtm_dram, T=512, name="aq"):
    nc = p.nc
    NCH = L // T
    KQ = 3 * D // 128
    with ExitStack() as st:
        wa = st.enter_context(nc.sbuf_tensor(f"{name}_wa", [128, KD * 3 * D], BF16))
        baT = st.enter_context(nc.sbuf_tensor(f"{name}_ba", [128, KQ], F32))
        x = st.enter_context(nc.sbuf_tensor(f"{name}_x", [128, KD * T], F32))
        xb = st.enter_context(nc.sbuf_tensor(f"{name}_xb", [128, KD * T], BF16))
        sq = st.enter_context(nc.sbuf_tensor(f"{name}_sq", [128, KD * T], F32))
        stats = st.enter_context(nc.sbuf_tensor(f"{name}_st", [1, 4 * T], F32))
        qkv = st.enter_context(nc.sbuf_tensor(f"{name}_q", [128, KQ * T], BF16))
        vt = st.enter_context(nc.sbuf_tensor(f"{name}_vt", [128, 2 * 8 * 128], BF16))
        psA = st.enter_context(nc.psum_tensor(f"{name}_pA", [1, T], F32))
        psB = st.enter_context(nc.psum_tensor(f"{name}_pB", [1, T], F32))
        psR = st.enter_context(nc.psum_tensor(f"{name}_pR", [128, T], F32))
        psM = st.enter_context(nc.psum_tensor(f"{name}_pM", [128, T], F32))
        ps0 = st.enter_context(nc.psum_tensor(f"{name}_p0", [128, T], F32))
        ps1 = st.enter_context(nc.psum_tensor(f"{name}_p1", [128, T], F32))
        psT = [st.enter_context(nc.psum_tensor(f"{name}_pT{i}", [128, 128], BF16)) for i in range(2)]
        psx = [ps0, ps1]

        S = Sched(p, name)
        S.add("gpsimd", lambda g: g.dma_start(out=wa[:].rearrange("p (a m) -> p a m", a=KD),
                                              in_=W["W_attn"].rearrange("(a p) m -> p a m", p=128)),
              tick="wl", dma=True, event="wa")
        S.add("sync", lambda s: s.dma_start(out=baT[:], in_=W["battnT"]),
              tick="ld", dma=True, event="ba")

        for ci in range(NCH):
            S.add("sync", lambda s, ci=ci: s.dma_start(
                out=sb3(x[:], KD, T), in_=feat_slice(x2_f, KD, ci, T)),
                waits=([f"fn_{ci - 1}"] if ci > 0 else []), tick="ld", dma=True, event=f"xl_{ci}")
            S.add("scalar", lambda A: A.activation(out=sq[:], in_=x[:], func=AF.Square),
                  waits=[f"xl_{ci}"] + ([f"fn_{ci - 1}"] if ci > 0 else []),
                  tick="ex", event=f"sqv_{ci}")
            def fsums(t):
                sum_matmuls(p, t, psA, x[:], KD, T, C.ones_col)
                return sum_matmuls(p, t, psB, sq[:], KD, T, C.ones_col)
            S.add("tensor", fsums, waits=[f"sqv_{ci}"] + ([f"vst_{ci - 1}"] if ci > 0 else []),
                  tick="pe", event=f"sm_{ci}")
            def fstats(v):
                v.tensor_scalar(out=stats[0:1, 0:T], in0=psA[0:1, 0:T], scalar1=1.0 / D,
                                scalar2=None, op0=OP.mult)
                v.tensor_tensor(out=stats[0:1, T:2 * T], in0=stats[0:1, 0:T],
                                in1=stats[0:1, 0:T], op=OP.mult)
                return v.scalar_tensor_tensor(out=stats[0:1, T:2 * T], in0=psB[0:1, 0:T],
                                              scalar=1.0 / D, in1=stats[0:1, T:2 * T],
                                              op0=OP.mult, op1=OP.subtract)
            S.add("vector", fstats, waits=[f"sm_{ci}"], tick="dv", event=f"vst_{ci}")
            S.add("scalar", lambda A: rstd_from_ms(p, A, stats[0:1, 2 * T:3 * T],
                                                   stats[0:1, T:2 * T], 1.0, EPS_LN),
                  waits=[f"vst_{ci}"] + ([f"bb_{ci - 1}"] if ci > 0 else []),
                  tick="ex", event=f"rst_{ci}")
            S.add("vector", lambda v: v.tensor_tensor(out=stats[0:1, 3 * T:4 * T],
                                                      in0=stats[0:1, 0:T],
                                                      in1=stats[0:1, 2 * T:3 * T], op=OP.mult),
                  waits=[f"rst_{ci}"], tick="dv", event=f"mrs_{ci}")
            def fb(t):
                bcast_matmul(t, psR, stats[0:1, 2 * T:3 * T], T, C.ones_row)
                return bcast_matmul(t, psM, stats[0:1, 3 * T:4 * T], T, C.ones_row)
            S.add("tensor", fb, waits=[f"mrs_{ci}"] + ([f"fn_{ci - 1}"] if ci > 0 else []),
                  tick="pe", event=f"bb_{ci}")
            def fnorm(v):
                for a in range(KD):
                    sl = slice(a * T, (a + 1) * T)
                    v.tensor_tensor(out=sq[:, sl], in0=x[:, sl], in1=psR[:, 0:T], op=OP.mult)
                    v.tensor_tensor(out=sq[:, sl], in0=sq[:, sl], in1=psM[:, 0:T], op=OP.subtract)
                    inst = v.tensor_copy(xb[:, sl], sq[:, sl])
                return inst
            S.add("vector", fnorm, waits=[f"bb_{ci}"] + ([f"qm_{ci - 1}_{KQ - 1}"] if ci > 0 else []),
                  tick="dv", event=f"fn_{ci}")
            for mt in range(KQ):
                bank = psx[mt % 2]
                bw = [f"fn_{ci}", "wa"]
                if mt >= 2:
                    bw.append(f"qb_{ci}_{mt - 2}")
                elif ci > 0:
                    bw.append(f"qb_{ci - 1}_{22 + mt}")
                def fmm(t, mt=mt, bank=bank):
                    for kt in range(KD):
                        inst = t.matmul(bank[:, 0:T], wa[:, kt * 3 * D + mt * 128: kt * 3 * D + mt * 128 + 128],
                                        xb[:, kt * T:(kt + 1) * T], start=(kt == 0), stop=(kt == KD - 1))
                    return inst
                S.add("tensor", fmm, waits=bw, tick="pe", event=f"qm_{ci}_{mt}")
                ew = [f"qm_{ci}_{mt}", "ba"]
                if ci > 0:
                    ew.append(f"qst_{ci - 1}")
                    if mt >= 16:
                        ew.append(f"tcp_{ci - 1}_{24 + (mt - 16)}")
                S.add("scalar", lambda A, mt=mt, bank=bank: A.activation(
                    out=qkv[:, mt * T:(mt + 1) * T], in_=bank[:, 0:T],
                    func=AF.Identity, bias=baT[:, mt:mt + 1]),
                    waits=ew, tick="ex", event=f"qb_{ci}_{mt}")
            S.add("sync", lambda s, ci=ci: s.dma_start(
                out=qkv_dram.rearrange("(a p) t -> p a t", p=128)[:, :, ci * T:(ci + 1) * T],
                in_=sb3(qkv[:], KQ, T)),
                waits=[f"qb_{ci}_{KQ - 1}"], tick="ld", dma=True, event=f"qst_{ci}")
            for tsub in range(T // 128):
                vof = (tsub % 2) * 8 * 128
                for mt in range(8):
                    k = tsub * 8 + mt
                    b = k % 2
                    tw = [f"qb_{ci}_{16 + mt}"]
                    if k >= 2:
                        tw.append(f"tcp_{ci}_{k - 2}")
                    elif ci > 0:
                        tw.append(f"tcp_{ci - 1}_{30 + k}")
                    S.add("tensor", lambda t, mt=mt, tsub=tsub, b=b: t.transpose(
                        psT[b][:], qkv[:, (16 + mt) * T + tsub * 128:
                                        (16 + mt) * T + tsub * 128 + 128], C.ident_b[:]),
                        waits=tw, tick="pe", event=f"tp_{ci}_{k}")
                    cw = [f"tp_{ci}_{k}"]
                    if mt == 0:
                        cidx = ci * 4 + tsub - 2
                        if cidx >= 0:
                            cw.append(f"vtst_{cidx // 4}_{cidx % 4}")
                    S.add("vector", lambda v, mt=mt, b=b, vof=vof: v.tensor_copy(
                        vt[:, vof + mt * 128: vof + (mt + 1) * 128], psT[b][:]),
                        waits=cw, tick="dv", event=f"tcp_{ci}_{k}")
                S.add("sync", lambda s, ci=ci, tsub=tsub, vof=vof: s.dma_start(
                    out=vtm_dram[ci * T + tsub * 128: ci * T + tsub * 128 + 128, :],
                    in_=vt[:, vof: vof + 8 * 128]),
                    waits=[f"tcp_{ci}_{tsub * 8 + 7}"], tick="ld", dma=True,
                    event=f"vtst_{ci}_{tsub}")
        S.run()


def stage_attn_core(p, C, W, qkv_dram, vtm_dram, x2_f, dst_f, T=512, name="ac"):
    nc = p.nc
    NCH = L // T
    NTK = L // 128
    scale = 1.0 / 8.0
    with ExitStack() as st:
        kT = st.enter_context(nc.sbuf_tensor(f"{name}_k", [64, L], BF16))
        qT = st.enter_context(nc.sbuf_tensor(f"{name}_q", [64, L], BF16))
        vtm = st.enter_context(nc.sbuf_tensor(f"{name}_v", [128, NTK * 65], BF16))
        esb = st.enter_context(nc.sbuf_tensor(f"{name}_e", [128, 2 * T], BF16))
        osb = st.enter_context(nc.sbuf_tensor(f"{name}_o", [64, T], F32))
        rden = st.enter_context(nc.sbuf_tensor(f"{name}_r", [1, T], F32))
        att = st.enter_context(nc.sbuf_tensor(f"{name}_at", [128, KD * L], BF16))

        S = Sched(p, name)
        with ExitStack() as ps_head:
            psS = [ps_head.enter_context(nc.psum_tensor(f"{name}_pS{i}", [128, T], F32)) for i in range(2)]
            psO = [ps_head.enter_context(nc.psum_tensor(f"{name}_pO{i}", [65, T], F32)) for i in range(2)]
            psD = ps_head.enter_context(nc.psum_tensor(f"{name}_pD", [64, T], F32))
            prev_ex = None
            prev_av = None
            prev_fn = None
            prev_fn2 = {}
            for h_ in range(NH):
                hw = [f"sl_{h_ - 1}"] if h_ > 0 else []
                S.add("sync", lambda s, h_=h_: s.dma_start(
                    out=qT[:], in_=qkv_dram[h_ * 64:(h_ + 1) * 64, :]),
                    waits=hw, tick="ld", dma=True)
                S.add("sync", lambda s, h_=h_: s.dma_start(
                    out=kT[:], in_=qkv_dram[D + h_ * 64: D + (h_ + 1) * 64, :]),
                    tick="ld", dma=True)
                vw = [f"avl_{h_ - 1}"] if h_ > 0 else []
                S.add("vector", lambda v: v.memset(vtm[:], 1.0), waits=vw,
                      tick="dv", event=f"vm_{h_}")
                S.add("sync", lambda s, h_=h_: s.dma_start(
                    out=vtm[:].rearrange("p (a d) -> p a d", d=65)[:, :, 0:64],
                    in_=vtm_dram.rearrange("(a p) d -> p a d", p=128)[:, :, h_ * 64:(h_ + 1) * 64]),
                    waits=[f"vm_{h_}"], tick="ld", dma=True, event=f"ldv_{h_}")
                for ci in range(NCH):
                    ntk = (ci + 1) * (T // 128)
                    unit_ex = []
                    unit_av = []
                    for tk in range(ntk + 1):
                        if tk < ntk:
                            sw = [f"ldv_{h_}"]
                            if tk >= 2:
                                sw.append(f"ex_{h_}_{ci}_{tk - 2}")
                            elif prev_ex is not None:
                                sw.append(prev_ex)
                            S.add("tensor", lambda t, h_=h_, ci=ci, tk=tk: t.matmul(
                                psS[tk % 2][:, 0:T], kT[:, tk * 128:(tk + 1) * 128],
                                qT[:, ci * T:(ci + 1) * T], start=True, stop=True),
                                waits=sw, tick="pe",
                                event=(f"sl_{h_}" if ci == NCH - 1 and tk == ntk - 1 else f"s_{h_}_{ci}_{tk}"))
                        if tk >= 1:
                            tkp = tk - 1
                            diag = tkp - ci * (T // 128)
                            sev = (f"sl_{h_}" if ci == NCH - 1 and tkp == ntk - 1 else f"s_{h_}_{ci}_{tkp}")
                            ew = [sev]
                            if tkp >= 2:
                                ew.append(unit_av[tkp - 2])
                            elif prev_av is not None:
                                ew.append(prev_av)
                            S.add("scalar", lambda A, tkp=tkp: A.activation(
                                out=esb[:, (tkp % 2) * T:(tkp % 2 + 1) * T], in_=psS[tkp % 2][:, 0:T],
                                func=AF.Exp, scale=scale),
                                waits=ew, tick="ex", event=f"ex_{h_}_{ci}_{tkp}")
                            unit_ex.append(f"ex_{h_}_{ci}_{tkp}")
                            mev = f"ex_{h_}_{ci}_{tkp}"
                            if diag >= 0:
                                S.add("vector", lambda v, tkp=tkp, diag=diag: v.tensor_tensor(
                                    out=esb[:, (tkp % 2) * T:(tkp % 2 + 1) * T],
                                    in0=esb[:, (tkp % 2) * T:(tkp % 2 + 1) * T],
                                    in1=C.masks[:, diag * 512: diag * 512 + T], op=OP.mult),
                                    waits=[mev], tick="dv", event=f"mk_{h_}_{ci}_{tkp}")
                                mev = f"mk_{h_}_{ci}_{tkp}"
                            aw = [mev, f"ldv_{h_}"]
                            if tkp == 0 and (ci % 2) in prev_fn2:
                                aw.append(prev_fn2[ci % 2])
                            avev = (f"avl_{h_}" if ci == NCH - 1 and tkp == ntk - 1 else f"av_{h_}_{ci}_{tkp}")
                            S.add("tensor", lambda t, h_=h_, ci=ci, tkp=tkp, ntk=ntk: t.matmul(
                                psO[ci % 2][:, 0:T], vtm[:, tkp * 65:(tkp + 1) * 65],
                                esb[:, (tkp % 2) * T:(tkp % 2 + 1) * T],
                                start=(tkp == 0), stop=(tkp == ntk - 1)),
                                waits=aw, tick="pe", event=avev)
                            unit_av.append(avev)
                    avev = unit_av[-1]
                    S.add("vector", lambda v, ci=ci: v.reciprocal(out=rden[:], in_=psO[ci % 2][64:65, 0:T]),
                          waits=[avev], tick="dv", event=f"rc_{h_}_{ci}")
                    dbw = [f"rc_{h_}_{ci}"] + ([prev_fn] if prev_fn else [])
                    S.add("tensor", lambda t: bcast_matmul(t, psD, rden[0:1, 0:T], T, C.ones_row, m=64),
                          waits=dbw, tick="pe", event=f"db_{h_}_{ci}")
                    obw = [avev] + ([prev_fn] if prev_fn else [])
                    S.add("scalar", lambda A, ci=ci: A.copy(osb[:], psO[ci % 2][0:64, 0:T]),
                          waits=obw, tick="ex", event=f"ob_{h_}_{ci}")
                    def fnr(v, h_=h_, ci=ci):
                        row0 = (h_ % 2) * 64
                        return v.tensor_tensor(
                            out=att[row0:row0 + 64, (h_ // 2) * L + ci * T: (h_ // 2) * L + ci * T + T],
                            in0=osb[:], in1=psD[:, 0:T], op=OP.mult)
                    S.add("vector", fnr, waits=[f"db_{h_}_{ci}", f"ob_{h_}_{ci}"],
                          tick="dv", event=f"fn_{h_}_{ci}")
                    prev_fn2[ci % 2] = f"fn_{h_}_{ci}"
                    prev_fn = f"fn_{h_}_{ci}"
                    prev_ex = unit_ex[-1]
                    prev_av = unit_av[-1]
            S.events["fn_prev"] = S.events[prev_fn]
        wp = st.enter_context(nc.sbuf_tensor(f"{name}_wp", [128, KD * D], BF16))
        bpT = st.enter_context(nc.sbuf_tensor(f"{name}_bp", [128, 8], F32))
        br = st.enter_context(nc.sbuf_tensor(f"{name}_br", [128, KD * T], F32))
        x2c = st.enter_context(nc.sbuf_tensor(f"{name}_x2", [128, KD * T], F32))
        sq = st.enter_context(nc.sbuf_tensor(f"{name}_sq", [128, KD * T], F32))
        stats = st.enter_context(nc.sbuf_tensor(f"{name}_st", [1, 4 * T], F32))
        with ExitStack() as ps2:
            pp = [ps2.enter_context(nc.psum_tensor(f"{name}_pp{i}", [128, T], F32)) for i in range(2)]
            psA = ps2.enter_context(nc.psum_tensor(f"{name}_pA", [1, T], F32))
            psB = ps2.enter_context(nc.psum_tensor(f"{name}_pB", [1, T], F32))
            psR = ps2.enter_context(nc.psum_tensor(f"{name}_pR", [128, T], F32))
            psM = ps2.enter_context(nc.psum_tensor(f"{name}_pM", [128, T], F32))
            S.add("gpsimd", lambda g: g.dma_start(out=wp[:].rearrange("p (a m) -> p a m", a=KD),
                                                  in_=W["W_proj"].rearrange("(a p) m -> p a m", p=128)),
                  tick="wl", dma=True, event="wp")
            S.add("sync", lambda s: s.dma_start(out=bpT[:], in_=W["bprojT"]),
                  tick="ld", dma=True, event="bp")
            lastb = [None, None]
            for ci in range(NCH):
                S.add("sync", lambda s, ci=ci: s.dma_start(
                    out=sb3(x2c[:], KD, T), in_=feat_slice(x2_f, KD, ci, T)),
                    waits=([f"fn3_{ci - 1}"] if ci > 0 else []), tick="ld", dma=True, event=f"x2_{ci}")
                for mt in range(KD):
                    b = mt % 2
                    bw = [f"avl_{NH - 1}", "fn_prev", "wp"] + ([lastb[b]] if lastb[b] else [])
                    def fmm(t, mt=mt, ci=ci, b=b):
                        for kt in range(KD):
                            inst = t.matmul(pp[b][:, 0:T], wp[:, kt * D + mt * 128: kt * D + mt * 128 + 128],
                                            att[:, kt * L + ci * T: kt * L + ci * T + T],
                                            start=(kt == 0), stop=(kt == KD - 1))
                        return inst
                    S.add("tensor", fmm, waits=bw, tick="pe", event=f"pj_{ci}_{mt}")
                    S.add("scalar", lambda A, mt=mt, b=b: A.activation(
                        out=br[:, mt * T:(mt + 1) * T], in_=pp[b][:, 0:T],
                        func=AF.Identity, bias=bpT[:, mt:mt + 1]),
                        waits=[f"pj_{ci}_{mt}", "bp"] + ([f"fn3_{ci - 1}"] if ci > 0 and mt == 0 else []),
                        tick="ex", event=f"bre_{ci}_{mt}")
                    lastb[b] = f"bre_{ci}_{mt}"
                S.add("scalar", lambda A: A.activation(out=sq[:], in_=br[:], func=AF.Square),
                      waits=[f"bre_{ci}_{KD - 1}"] + ([f"st3_{ci - 1}"] if ci > 0 else []),
                      tick="ex", event=f"sq2_{ci}")
                def fsums(t):
                    sum_matmuls(p, t, psA, br[:], KD, T, C.ones_col)
                    return sum_matmuls(p, t, psB, sq[:], KD, T, C.ones_col)
                S.add("tensor", fsums, waits=[f"sq2_{ci}"] + ([f"rst_{ci - 1}"] if ci > 0 else []),
                      tick="pe", event=f"sm_{ci}")
                def fstats(v):
                    v.tensor_scalar(out=stats[0:1, 0:T], in0=psA[0:1, 0:T], scalar1=1.0 / D,
                                    scalar2=None, op0=OP.mult)
                    v.tensor_tensor(out=stats[0:1, T:2 * T], in0=stats[0:1, 0:T],
                                    in1=stats[0:1, 0:T], op=OP.mult)
                    return v.scalar_tensor_tensor(out=stats[0:1, T:2 * T], in0=psB[0:1, 0:T],
                                                  scalar=1.0 / D, in1=stats[0:1, T:2 * T],
                                                  op0=OP.mult, op1=OP.subtract)
                S.add("vector", fstats, waits=[f"sm_{ci}"], tick="dv", event=f"vst_{ci}")
                S.add("scalar", lambda A: rstd_from_ms(p, A, stats[0:1, 2 * T:3 * T],
                                                       stats[0:1, T:2 * T], 1.0, EPS_LN),
                      waits=[f"vst_{ci}"], tick="ex", event=f"rst_{ci}")
                S.add("vector", lambda v: v.tensor_tensor(out=stats[0:1, 3 * T:4 * T],
                                                          in0=stats[0:1, 0:T],
                                                          in1=stats[0:1, 2 * T:3 * T], op=OP.mult),
                      waits=[f"rst_{ci}"], tick="dv", event=f"mrs_{ci}")
                def fbb(t):
                    bcast_matmul(t, psR, stats[0:1, 2 * T:3 * T], T, C.ones_row)
                    return bcast_matmul(t, psM, stats[0:1, 3 * T:4 * T], T, C.ones_row)
                S.add("tensor", fbb, waits=[f"mrs_{ci}"] + ([f"fn3_{ci - 1}"] if ci > 0 else []),
                      tick="pe", event=f"bb_{ci}")
                def fnorm(v):
                    for a in range(KD):
                        sl = slice(a * T, (a + 1) * T)
                        v.tensor_tensor(out=sq[:, sl], in0=br[:, sl], in1=psR[:, 0:T], op=OP.mult)
                        v.tensor_tensor(out=sq[:, sl], in0=sq[:, sl], in1=psM[:, 0:T], op=OP.subtract)
                        inst = v.tensor_tensor(out=sq[:, sl], in0=sq[:, sl], in1=x2c[:, sl], op=OP.add)
                    return inst
                S.add("vector", fnorm, waits=[f"bb_{ci}", f"x2_{ci}"], tick="dv", event=f"fn3_{ci}")
                S.add("gpsimd", lambda g, ci=ci: g.dma_start(
                    out=feat_slice(dst_f, KD, ci, T), in_=sb3(sq[:], KD, T)),
                    waits=[f"fn3_{ci}"], tick="wl", dma=True, event=f"st3_{ci}")
        S.run()

def stage_ffn(p, C, W, x3_f, y_out, T=256, name="ff"):
    nc = p.nc
    NCH = L // T
    with ExitStack() as st:
        w1 = st.enter_context(nc.sbuf_tensor(f"{name}_w1", [128, KD * FF], BF16))
        w2 = st.enter_context(nc.sbuf_tensor(f"{name}_w2", [128, KF * D], BF16))
        b1T = st.enter_context(nc.sbuf_tensor(f"{name}_b1", [128, KF], F32))
        hb2 = st.enter_context(nc.sbuf_tensor(f"{name}_b2", [128, 8], F32))
        gamT = st.enter_context(nc.sbuf_tensor(f"{name}_g", [128, 8], F32))
        betT = st.enter_context(nc.sbuf_tensor(f"{name}_be", [128, 8], F32))

        S = Sched(p, name)
        S.add("sync", lambda s: s.dma_start(out=b1T[:], in_=W["fc1bT"]), tick="ld", dma=True)
        S.add("sync", lambda s: s.dma_start(out=hb2[:], in_=W["fc2bT"]), tick="ld", dma=True)
        S.add("sync", lambda s: s.dma_start(out=gamT[:], in_=W["gamT"]), tick="ld", dma=True)
        S.add("sync", lambda s: s.dma_start(out=betT[:], in_=W["betT"]),
              tick="ld", dma=True, event="consts")
        S.add("vector", lambda v: v.tensor_scalar(out=hb2[:], in0=hb2[:], scalar1=0.5,
                                                  scalar2=None, op0=OP.mult),
              waits=["consts"], tick="dv", event="hb2")

        with ExitStack() as dq:
            qf = dq.enter_context(nc.sbuf_tensor(f"{name}_qf", [128, 2 * FF], F32))
            srow = dq.enter_context(nc.sbuf_tensor(f"{name}_sr", [1, 2 * FF], F32))
            psc = dq.enter_context(nc.psum_tensor(f"{name}_pc", [128, 512], F32))
            psc2 = dq.enter_context(nc.psum_tensor(f"{name}_pc2", [128, 512], F32))
            units = [("fc1_q", "fc1_scales", FF, 8) , ("fc2_q", "fc2_scales", D, 32)]
            gidx = 0
            for (qk, sk, N, NG) in units:
                for g in range(NG):
                    u = gidx; gidx += 1
                    par = u % 2
                    lw = [f"dqs_{u - 2}"] if u >= 2 else []
                    S.add("gpsimd", lambda gp, qk=qk, g=g, par=par, N=N: gp.dma_start(
                        out=qf[:, par * FF: par * FF + N], in_=W[qk][g * 128:(g + 1) * 128, :]),
                        waits=lw, tick="wl", dma=True, event=f"dqq_{u}")
                    S.add("sync", lambda s, sk=sk, g=g, par=par, N=N: s.dma_start(
                        out=srow[0:1, par * FF: par * FF + N], in_=W[sk][g:g + 1, :]),
                        waits=lw, tick="ld", dma=True, event=f"dqr_{u}")
                    for ns in range(N // 512):
                        bank = psc if ns % 2 == 0 else psc2
                        bw = [f"dqr_{u}"]
                        if ns >= 2:
                            bw.append(f"dqs_{u}_{ns - 2}")
                        elif u >= 1:
                            bw.append(f"dqs_{u - 1}")
                        S.add("tensor", lambda t, par=par, ns=ns, bank=bank: bcast_matmul(
                            t, bank, srow[0:1, par * FF + ns * 512: par * FF + (ns + 1) * 512],
                            512, C.ones_row),
                            waits=bw, tick="pe", event=f"dqb_{u}_{ns}")
                        wdst = w1 if qk == "fc1_q" else w2
                        S.add("vector", lambda v, qk=qk, g=g, ns=ns, par=par, N=N, wdst=wdst, bank=bank:
                              v.scalar_tensor_tensor(
                                  out=wdst[:, g * N + ns * 512: g * N + (ns + 1) * 512],
                                  in0=qf[:, par * FF + ns * 512: par * FF + (ns + 1) * 512],
                                  scalar=-128.0, in1=bank[:, 0:512], op0=OP.add, op1=OP.mult),
                              waits=[f"dqb_{u}_{ns}", f"dqq_{u}"], tick="dv",
                              event=(f"dqs_{u}_{ns}" if ns < N // 512 - 1 else f"dqs_{u}"))
            S.run()

        x = st.enter_context(nc.sbuf_tensor(f"{name}_x", [128, 2 * KD * T], F32))
        u4 = st.enter_context(nc.sbuf_tensor(f"{name}_u4", [128, KD * T], F32))
        hq = st.enter_context(nc.sbuf_tensor(f"{name}_hq", [128, KD * T], BF16))
        hs = st.enter_context(nc.sbuf_tensor(f"{name}_hs", [128, KF * T], BF16))
        vv = st.enter_context(nc.sbuf_tensor(f"{name}_vv", [128, KD * T], F32))
        sq = st.enter_context(nc.sbuf_tensor(f"{name}_sq", [128, KD * T], F32))
        stats = st.enter_context(nc.sbuf_tensor(f"{name}_st", [1, 4 * T], F32))
        S = Sched(p, name + "c")
        with ExitStack() as ps2:
            pp = [ps2.enter_context(nc.psum_tensor(f"{name}_pp{i}", [128, T], F32)) for i in range(2)]
            psA = ps2.enter_context(nc.psum_tensor(f"{name}_pA", [1, T], F32))
            psB = ps2.enter_context(nc.psum_tensor(f"{name}_pB", [1, T], F32))
            psR = ps2.enter_context(nc.psum_tensor(f"{name}_pR", [128, T], F32))
            psM = ps2.enter_context(nc.psum_tensor(f"{name}_pM", [128, T], F32))

            lnctr = [0]
            def ln_chain(ci, key, src, deps):
                k = lnctr[0]; lnctr[0] += 1
                sqw = list(deps) + ([f"lnsum_{k - 1}"] if k > 0 else [])
                S.add("scalar", lambda A, src=src: A.activation(out=sq[:], in_=src[:] if not isinstance(src, tuple) else x[:, src[1]:src[2]], func=AF.Square),
                      waits=sqw, tick="ex", event=f"lnsq_{k}")
                def fsums(t, src=src):
                    ssrc = src[:] if not isinstance(src, tuple) else x[:, src[1]:src[2]]
                    sum_matmuls(p, t, psA, ssrc, KD, T, C.ones_col)
                    return sum_matmuls(p, t, psB, sq[:], KD, T, C.ones_col)
                S.add("tensor", fsums, waits=[f"lnsq_{k}"] + ([f"lnrst_{k - 1}"] if k > 0 else []),
                      tick="pe", event=f"lnsum_{k}")
                def fstats(v):
                    v.tensor_scalar(out=stats[0:1, 0:T], in0=psA[0:1, 0:T], scalar1=1.0 / D,
                                    scalar2=None, op0=OP.mult)
                    v.tensor_tensor(out=stats[0:1, T:2 * T], in0=stats[0:1, 0:T],
                                    in1=stats[0:1, 0:T], op=OP.mult)
                    return v.scalar_tensor_tensor(out=stats[0:1, T:2 * T], in0=psB[0:1, 0:T],
                                                  scalar=1.0 / D, in1=stats[0:1, T:2 * T],
                                                  op0=OP.mult, op1=OP.subtract)
                S.add("vector", fstats, waits=[f"lnsum_{k}"], tick="dv", event=f"lnst_{k}")
                S.add("scalar", lambda A: rstd_from_ms(p, A, stats[0:1, 2 * T:3 * T],
                                                       stats[0:1, T:2 * T], 1.0, EPS_LN),
                      waits=[f"lnst_{k}"], tick="ex", event=f"lnrst_{k}")
                S.add("vector", lambda v: v.tensor_tensor(out=stats[0:1, 3 * T:4 * T],
                                                          in0=stats[0:1, 0:T],
                                                          in1=stats[0:1, 2 * T:3 * T], op=OP.mult),
                      waits=[f"lnrst_{k}"], tick="dv", event=f"lnmr_{k}")
                def fbb(t):
                    bcast_matmul(t, psR, stats[0:1, 2 * T:3 * T], T, C.ones_row)
                    return bcast_matmul(t, psM, stats[0:1, 3 * T:4 * T], T, C.ones_row)
                S.add("tensor", fbb, waits=[f"lnmr_{k}", f"lnbusy_{k}"] if f"lnbusy_{k}" in S.events
                      else [f"lnmr_{k}"], tick="pe", event=f"lnbb_{k}")
                return f"lnbb_{k}", k
            lastb = [None, None]
            for ci in range(NCH):
                xo = (ci % 2) * KD * T
                S.add("sync", lambda s, ci=ci, xo=xo: s.dma_start(
                    out=sb3(x[:, xo:xo + KD * T], KD, T), in_=feat_slice(x3_f, KD, ci, T)),
                    waits=([f"yst_{ci - 2}"] if ci >= 2 else []), tick="ld", dma=True, event=f"x_{ci}")
                e1, k1 = ln_chain(ci, "u4", ("x", xo, xo + KD * T), [f"x_{ci}"] +
                                  ([f"u4n_{ci - 1}", f"yst_{ci - 1}"] if ci > 0 else []))
                def fu4(v, xo=xo):
                    for a in range(KD):
                        sl = slice(a * T, (a + 1) * T)
                        v.tensor_tensor(out=u4[:, sl], in0=x[:, xo + a * T: xo + (a + 1) * T],
                                        in1=psR[:, 0:T], op=OP.mult)
                        inst = v.tensor_tensor(out=u4[:, sl], in0=u4[:, sl], in1=psM[:, 0:T],
                                               op=OP.subtract)
                    return inst
                u4w = [e1] + ([f"vvn_{ci - 1}"] if ci > 0 else [])
                S.add("vector", fu4, waits=u4w, tick="dv", event=f"u4n_{ci}")
                e2, k2 = ln_chain(ci, "hq", u4, [f"u4n_{ci}"])
                def fhq(v):
                    for a in range(KD):
                        sl = slice(a * T, (a + 1) * T)
                        v.tensor_tensor(out=sq[:, sl], in0=u4[:, sl], in1=psR[:, 0:T], op=OP.mult)
                        v.tensor_tensor(out=sq[:, sl], in0=sq[:, sl], in1=psM[:, 0:T], op=OP.subtract)
                        inst = v.tensor_scalar(out=hq[:, sl], in0=sq[:, sl], scalar1=gamT[:, a:a + 1],
                                               scalar2=betT[:, a:a + 1], op0=OP.mult, op1=OP.add)
                    return inst
                hqw = [e2] + ([f"fc1_{ci - 1}_{KF - 1}"] if ci > 0 else [])
                S.add("vector", fhq, waits=hqw, tick="dv", event=f"hq_{ci}")
                S.events[f"lnbusy_{k2 + 1}"] = S.events[f"hq_{ci}"]
                for et in range(KF):
                    b = et % 2
                    bw = [f"hq_{ci}", ("dqwall", None)] if False else [f"hq_{ci}"]
                    if lastb[b]:
                        bw.append(lastb[b])
                    def fmm(t, et=et, b=b):
                        for kt in range(KD):
                            inst = t.matmul(pp[b][:, 0:T], w1[:, kt * FF + et * 128: kt * FF + et * 128 + 128],
                                            hq[:, kt * T:(kt + 1) * T], start=(kt == 0), stop=(kt == KD - 1))
                        return inst
                    S.add("tensor", fmm, waits=bw, tick="pe", event=f"fc1_{ci}_{et}")
                    ew = [f"fc1_{ci}_{et}"] + ([f"fc2_{ci - 1}_{KD - 1}"] if ci > 0 and et == 0 else [])
                    S.add("scalar", lambda A, et=et, b=b: A.activation(
                        out=hs[:, et * T:(et + 1) * T], in_=pp[b][:, 0:T],
                        func=AF.Silu, bias=b1T[:, et:et + 1]),
                        waits=ew, tick="ex", event=f"hse_{ci}_{et}")
                    lastb[b] = f"hse_{ci}_{et}"
                for mt in range(KD):
                    b = mt % 2
                    bw = [f"hse_{ci}_{KF - 1}", lastb[b]]
                    def fmm(t, mt=mt, b=b):
                        for kt in range(KF):
                            inst = t.matmul(pp[b][:, 0:T], w2[:, kt * D + mt * 128: kt * D + mt * 128 + 128],
                                            hs[:, kt * T:(kt + 1) * T], start=(kt == 0), stop=(kt == KF - 1))
                        return inst
                    S.add("tensor", fmm, waits=bw, tick="pe", event=f"fc2_{ci}_{mt}")
                    def fv(v, mt=mt, b=b):
                        sl = slice(mt * T, (mt + 1) * T)
                        v.scalar_tensor_tensor(out=vv[:, sl], in0=pp[b][:, 0:T], scalar=0.5,
                                               in1=u4[:, sl], op0=OP.mult, op1=OP.add)
                        return v.tensor_scalar(out=vv[:, sl], in0=vv[:, sl], scalar1=hb2[:, mt:mt + 1],
                                               scalar2=None, op0=OP.add)
                    vw = [f"fc2_{ci}_{mt}"] + ([f"vvn_{ci - 1}"] if ci > 0 and mt == 0 else [])
                    S.add("vector", fv, waits=vw, tick="dv", event=f"vve_{ci}_{mt}")
                    lastb[b] = f"vve_{ci}_{mt}"
                e3, k3 = ln_chain(ci, "y", vv, [f"vve_{ci}_{KD - 1}"])
                def fy(v, xo=xo):
                    for a in range(KD):
                        sl = slice(a * T, (a + 1) * T)
                        v.tensor_tensor(out=sq[:, sl], in0=vv[:, sl], in1=psR[:, 0:T], op=OP.mult)
                        v.tensor_tensor(out=sq[:, sl], in0=sq[:, sl], in1=psM[:, 0:T], op=OP.subtract)
                        inst = v.tensor_tensor(out=sq[:, sl], in0=sq[:, sl],
                                               in1=x[:, xo + a * T: xo + (a + 1) * T], op=OP.add)
                    return inst
                S.add("vector", fy, waits=[e3] + ([f"yst_{ci - 1}"] if ci > 0 else []),
                      tick="dv", event=f"vvn_{ci}")
                S.add("gpsimd", lambda g, ci=ci: g.dma_start(
                    out=feat_slice(y_out, KD, ci, T), in_=sb3(sq[:], KD, T)),
                    waits=[f"vvn_{ci}"], tick="wl", dma=True, event=f"yst_{ci}")
            S.run()

def declare_inputs(nc, shapes):
    aps = {}
    for name, (shape, dt) in shapes.items():
        aps[name] = nc.dram_tensor(name, list(shape), dt, kind="ExternalInput").ap()
    return aps


def input_shapes():
    sh = {
        "xT": ((D, L), F32),
        "gate_w": ((D, NE), F32),
        "gate_b_rep": ((128, NE), F32),
        "moe_w1": ((NE, D, FF), F32),
        "moe_w2": ((NE, FF, D), F32),
        "moe_b1T": ((NE, 128, KF), F32),
        "moe_b2": ((NE, D), F32),
        "W_attn": ((D, 3 * D), F32),
        "battnT": ((128, 24), F32),
        "W_proj": ((D, D), F32),
        "bprojT": ((128, 8), F32),
        "gamT": ((128, 8), F32),
        "betT": ((128, 8), F32),
        "fc1_q": ((D, FF), I32),
        "fc1_scales": ((8, FF), F32),
        "fc1bT": ((128, KF), F32),
        "fc2_q": ((FF, D), I32),
        "fc2_scales": ((32, D), F32),
        "fc2bT": ((128, 8), F32),
        "mlp_w1T": ((D, DI), F32),
        "mlp_b1T": ((128, 16), F32),
        "mlp_w2T": ((DI, D), F32),
        "mlp_b2T": ((128, 8), F32),
        "ln_wT": ((128, 8), F32),
        "ln2_wT": ((128, 8), F32),
    }
    for i in range(NLAY):
        sh.update({
            f"m{i}_inT": ((D, 2 * DI), F32),
            f"m{i}_convw": ((128, 64), F32),
            f"m{i}_convb": ((128, 16), F32),
            f"m{i}_xpT": ((DI, 96), F32),
            f"m{i}_dtpT": ((DTR, DI), F32),
            f"m{i}_dtb": ((128, 16), F32),
            f"m{i}_alog": ((128, 256), F32),
            f"m{i}_dp": ((128, 16), F32),
            f"m{i}_outT": ((DI, D), F32),
        })
    return sh


def prepare_host_inputs(inputs):
    w = {}
    w["gate_w"] = np.ascontiguousarray(inputs["L0_ffn_gate_w"], np.float32)
    w["gate_b_rep"] = np.ascontiguousarray(
        np.broadcast_to(np.asarray(inputs["L0_ffn_gate_b"], np.float32), (128, NE)))
    w["moe_w1"] = np.ascontiguousarray(inputs["L0_ffn_w1"], np.float32)
    w["moe_w2"] = np.ascontiguousarray(inputs["L0_ffn_w2"], np.float32)
    w["moe_b1T"] = np.ascontiguousarray(
        np.asarray(inputs["L0_ffn_b1"], np.float32).reshape(NE, KF, 128).transpose(0, 2, 1))
    w["moe_b2"] = np.ascontiguousarray(inputs["L0_ffn_b2"], np.float32)
    w["W_attn"] = np.ascontiguousarray(inputs["L1_attn_W_attn"], np.float32)
    w["battnT"] = _cols(inputs["L1_attn_b_attn"], 24)
    w["W_proj"] = np.ascontiguousarray(inputs["L1_attn_W_proj"], np.float32)
    w["bprojT"] = _cols(inputs["L1_attn_b_proj"], 8)
    w["gamT"] = _cols(inputs["L1_ffn_ln_gamma"], 8)
    w["betT"] = _cols(inputs["L1_ffn_ln_beta"], 8)
    w["fc1_q"] = np.ascontiguousarray(inputs["L1_ffn_fc1_q"], np.int32)
    w["fc1_scales"] = np.ascontiguousarray(inputs["L1_ffn_fc1_scales"], np.float32)
    w["fc1bT"] = _cols(inputs["L1_ffn_fc1_bias"], KF)
    w["fc2_q"] = np.ascontiguousarray(inputs["L1_ffn_fc2_q"], np.int32)
    w["fc2_scales"] = np.ascontiguousarray(inputs["L1_ffn_fc2_scales"], np.float32)
    w["fc2bT"] = _cols(inputs["L1_ffn_fc2_bias"], 8)
    w["mlp_w1T"] = _t(inputs["L0_attn_mlp_w1"])
    w["mlp_b1T"] = _cols(inputs["L0_attn_mlp_b1"], 16)
    w["mlp_w2T"] = _t(inputs["L0_attn_mlp_w2"])
    w["mlp_b2T"] = _cols(inputs["L0_attn_mlp_b2"], 8)
    w["ln_wT"] = _cols(inputs["L0_attn_ln_w"], 8)
    w["ln2_wT"] = _cols(inputs["L0_attn_ln2_w"], 8)
    for i in range(NLAY):
        w[f"m{i}_inT"] = _t(inputs["L0_attn_in_proj_w"][i])
        w[f"m{i}_convw"] = np.ascontiguousarray(
            np.asarray(inputs["L0_attn_conv_w"][i], np.float32).reshape(16, 128, 4)
            .transpose(1, 0, 2).reshape(128, 64))
        w[f"m{i}_convb"] = _cols(inputs["L0_attn_conv_b"][i], 16)
        w[f"m{i}_xpT"] = _t(inputs["L0_attn_x_proj_w"][i])
        w[f"m{i}_dtpT"] = _t(inputs["L0_attn_dt_proj_w"][i])
        w[f"m{i}_dtb"] = _cols(inputs["L0_attn_dt_proj_b"][i], 16)
        w[f"m{i}_alog"] = np.ascontiguousarray(
            np.asarray(inputs["L0_attn_A_log"][i], np.float32).reshape(16, 128, 16)
            .transpose(1, 0, 2).reshape(128, 256))
        w[f"m{i}_dp"] = _cols(inputs["L0_attn_D_param"][i], 16)
        w[f"m{i}_outT"] = _t(inputs["L0_attn_out_proj_w"][i])
    xs = [np.ascontiguousarray(np.asarray(inputs["L0_attn_x"][b], np.float32).T)
          for b in range(8)]
    return w, xs


def mamba_W(aps, i):
    return {"inT": aps[f"m{i}_inT"], "convw": aps[f"m{i}_convw"], "convb": aps[f"m{i}_convb"],
            "xpT": aps[f"m{i}_xpT"], "dtpT": aps[f"m{i}_dtpT"], "dtb": aps[f"m{i}_dtb"],
            "alog": aps[f"m{i}_alog"], "dp": aps[f"m{i}_dp"], "outT": aps[f"m{i}_outT"],
            "m1T": aps["mlp_w1T"], "b1T": aps["mlp_b1T"], "m2T": aps["mlp_w2T"],
            "b2T": aps["mlp_b2T"], "lnwT": aps["ln_wT"]}


def build_program():
    nc = bass.Bass("TRN2", target_bir_lowering=False, debug=False)
    aps = declare_inputs(nc, input_shapes())
    y_out = nc.dram_tensor("y", [D, L], F32, kind="ExternalOutput").ap()

    def scr(name, shape, dt):
        return nc.dram_tensor(name, list(shape), dt, kind="Internal").ap()
    u1f = scr("u1f", [D, L], F32)
    u1b = scr("u1b", [D, L], BF16)
    ygd = scr("ygd", [DI, L], BF16)
    bcd = scr("bcd", [32, 256], BF16)
    xm_f = [scr(f"xm{j}f", [D, L], F32) for j in range(NLAY)]
    xm_b = [scr(f"xm{j}b", [D, L], BF16) for j in range(NLAY)]
    x1f = scr("x1f", [D, L], F32)
    u2b = scr("u2b", [D, L], BF16)
    combd = scr("combd", [NE * L], F32)
    accf = scr("accf", [D, L], F32)
    x2f = scr("x2f", [D, L], F32)
    qkvd = scr("qkvd", [3 * D, L], BF16)
    vtmd = scr("vtmd", [L, D], BF16)
    x3f = scr("x3f", [D, L], F32)

    ident_dram = nc.inline_tensor(np.eye(128, dtype=np.float32), name="identc")
    masks_dram = nc.inline_tensor(build_masks(), name="masksc")

    with ExitStack() as stack:
        tok = stack.enter_context(nc.semaphore("tok"))
        blk = stack.enter_context(nc.Block())
        p = Pr(nc, blk, tok)
        C = alloc_consts(p, stack, masks_dram, ident_dram)

        stage_ln(p, C, aps["xT"], dst_f=u1f, dst_b=u1b, name="ln1")
        src_f, src_b = u1f, u1b
        for i in range(NLAY):
            W = mamba_W(aps, i)
            stage_mamba_a(p, C, W, src_b, ygd, bcd, name=f"ma{i}")
            stage_mamba_b(p, C, W, ygd, src_f, xm_f[i], xm_b[i], name=f"mb{i}")
            src_f, src_b = xm_f[i], xm_b[i]
        stage_mamba_tail(p, C, aps["ln2_wT"], src_f, u1f, aps["xT"], x1f, name="mtl")
        stage_moe_pre(p, C, aps, x1f, u2b, combd, name="mp")
        stage_moe(p, C, aps, u2b, combd, accf, name="moex")
        stage_add_ln(p, C, accf, x1f, x2f, name="mo2")
        stage_attn_qkv(p, C, aps, x2f, qkvd, vtmd, name="aq")
        stage_attn_core(p, C, aps, qkvd, vtmd, x2f, x3f, name="ac")
        stage_ffn(p, C, aps, x3f, y_out, T=256, name="ff")

        p.ph("sync", lambda p, s: s.wait_ge(tok, p.n))
    return nc


def kernel(**inputs):
    w, xs = prepare_host_inputs(inputs)
    nc = build_program()
    in_maps = [dict(w, xT=xs[b]) for b in range(8)]
    res = run_bass_kernel_spmd(nc, in_maps, core_ids=list(range(8)))
    out = np.empty((8, L, D), dtype=np.float32)
    for b in range(8):
        out[b] = np.asarray(res.results[b]["y"]).T
    return out


# version 13
# speedup vs baseline: 1.0497x; 1.0240x over previous
import numpy as np
import ml_dtypes
from contextlib import ExitStack

import concourse.bass as bass
import concourse.mybir as mybir
from concourse.bass_utils import run_bass_kernel_spmd

F32 = mybir.dt.float32
BF16 = mybir.dt.bfloat16
I32 = mybir.dt.int32
AF = mybir.ActivationFunctionType
OP = mybir.AluOpType
AX = mybir.AxisListType

D = 1024
L = 4096
DI = 2048
DS = 16
DTR = 64
DC = 4
NLAY = 2
NE = 8
FF = 4096
NH = 16
HD = 64
EPS_LN = 1e-5
EPS_RMS = 1.1920929e-07

KD = D // 128
KI = DI // 128
KF = FF // 128


class Pr:
    def __init__(self, nc, blk, tok):
        self.nc = nc
        self.blk = blk
        self.tok = tok
        self.n = 0

    def ph(self, eng_name, fn):
        n0 = self.n
        def body(eng):
            eng.wait_ge(self.tok, n0)
            fn(self, eng)
        getattr(self.blk, eng_name)(body)

    def inc(self, inst, k=1):
        inst.then_inc(self.tok, k)
        self.n += k

    def dma(self, inst):
        self.inc(inst, 16)


class Ctr:
    def __init__(self, nc, name):
        self.sem = nc.semaphore(name).__enter__()
        self.name = name
        self.n = 0


class Sched:
    ENGINES = ("sync", "tensor", "scalar", "vector", "gpsimd")

    def __init__(self, p, name):
        self.p = p
        self.name = name
        self.ops = {e: [] for e in self.ENGINES}
        self.ctrs = {}
        self.events = {}

    def ctr(self, cname):
        if cname not in self.ctrs:
            self.ctrs[cname] = Ctr(self.p.nc, f"{self.name}_{cname}")
        return self.ctrs[cname]

    def add(self, eng, emit, waits=(), tick=None, dma=False, event=None):
        wv = []
        for w in waits:
            wv.append(self.events[w] if isinstance(w, str) else w)
        val = None
        if tick is not None:
            c = self.ctr(tick)
            c.n += 16 if dma else 1
            val = c.n
        self.ops[eng].append((emit, wv, tick, dma))
        if event is not None:
            assert tick is not None
            self.events[event] = (tick, val)

    def ev(self, key):
        return self.events[key]

    def run(self):
        entry = self.p.n
        used = [e for e in self.ENGINES if self.ops[e]]
        for eng in used:
            oplist = self.ops[eng]
            def body(E, oplist=oplist, eng=eng):
                E.wait_ge(self.p.tok, entry)
                seen = {}
                for emit, wv, tick, dma in oplist:
                    merged = {}
                    for cn, v in wv:
                        if v is not None and merged.get(cn, -1) < v:
                            merged[cn] = v
                    for cn, v in merged.items():
                        if seen.get(cn, -1) < v:
                            E.wait_ge(self.ctrs[cn].sem, v)
                            seen[cn] = v
                    inst = emit(E)
                    if tick is not None:
                        inst.then_inc(self.ctrs[tick].sem, 16 if dma else 1)
                own = None
                for emit, wv, tick, dma in reversed(oplist):
                    if tick is not None:
                        own = tick
                        break
                if own is not None:
                    E.wait_ge(self.ctrs[own].sem, self.ctrs[own].n)
                E.nop().then_inc(self.p.tok, 1)
            getattr(self.p.blk, eng)(body)
            self.p.n += 1


def feat_slice(dram, ktiles, c, T):
    return dram.rearrange("(a p) t -> p a t", p=128)[:, :, c * T:(c + 1) * T]


def sb3(sb_ap, ktiles, T):
    return sb_ap.rearrange("p (a t) -> p a t", a=ktiles)


def load_chunk(p, eng, sbuf, dram, ktiles, c, T):
    p.dma(eng.dma_start(out=sb3(sbuf[:], ktiles, T), in_=feat_slice(dram, ktiles, c, T)))


def store_chunk(p, eng, dram, sbuf, ktiles, c, T):
    p.dma(eng.dma_start(out=feat_slice(dram, ktiles, c, T), in_=sb3(sbuf[:], ktiles, T)))


def rstd_from_ms(p, A, ms_dst, ms_src_psum, scale, eps):
    A.activation(out=ms_dst, in_=ms_src_psum, func=AF.Ln, scale=scale, bias=eps)
    return A.activation(out=ms_dst, in_=ms_dst, func=AF.Exp, scale=-0.5)


def sum_matmuls(p, t, psum, src_sb, ktiles, T, ones_col):
    for a in range(ktiles):
        inst = t.matmul(psum[0:1, 0:T], ones_col[:], src_sb[:, a * T:(a + 1) * T],
                        start=(a == 0), stop=(a == ktiles - 1))
    return inst


def bcast_matmul(t, psum, row_ap, T, ones_row, m=128):
    return t.matmul(psum[0:m, 0:T], ones_row[0:1, 0:m], row_ap, start=True, stop=True)


class Consts:
    pass


def alloc_consts(p, stack, masks_dram, ident_dram):
    nc = p.nc
    c = Consts()
    c.ones_col = stack.enter_context(nc.sbuf_tensor("ones_col", [128, 1], F32))
    c.ones_row = stack.enter_context(nc.sbuf_tensor("ones_row", [1, 128], F32))
    c.ones_row_b = stack.enter_context(nc.sbuf_tensor("ones_row_b", [1, 128], BF16))
    c.ident_b = stack.enter_context(nc.sbuf_tensor("ident_b", [128, 128], BF16))
    c.ident_f = stack.enter_context(nc.sbuf_tensor("ident_f", [128, 128], F32))
    c.masks = stack.enter_context(nc.sbuf_tensor("masks", [128, 4 * 512], BF16))
    c.cst = stack.enter_context(nc.sbuf_tensor("cst", [128, 16], F32))

    cvals = [EPS_LN, EPS_RMS, -1.0, -2.0, 0.5, -128.0, -0.5, 1.0 / D]
    def f(p, v):
        p.inc(v.memset(c.ones_col[:], 1.0))
        p.inc(v.memset(c.ones_row[:], 1.0))
        p.inc(v.memset(c.ones_row_b[:], 1.0))
        for i, val in enumerate(cvals):
            p.inc(v.memset(c.cst[:, i:i + 1], val))
            try:
                nc.const_aps.register(F32, val, c.cst[:, i:i + 1])
            except Exception:
                pass
    p.ph("vector", f)

    def g(p, s):
        p.dma(s.dma_start(out=c.masks[:], in_=masks_dram.ap()[:]))
        p.dma(s.dma_start(out=c.ident_f[:], in_=ident_dram.ap()[:]))
    p.ph("sync", g)

    def h(p, gp):
        p.dma(gp.dma_start(out=c.ident_b[:], in_=ident_dram.ap()[:]))
    p.ph("gpsimd", h)
    return c


def stage_ln(p, C, src, dst_f=None, dst_b=None, T=512, name="ln"):
    nc = p.nc
    NCH = L // T
    with ExitStack() as st:
        x = st.enter_context(nc.sbuf_tensor(f"{name}_x", [128, KD * T], F32))
        sq = st.enter_context(nc.sbuf_tensor(f"{name}_sq", [128, KD * T], F32))
        stats = st.enter_context(nc.sbuf_tensor(f"{name}_st", [1, 4 * T], F32))
        psA = st.enter_context(nc.psum_tensor(f"{name}_pA", [1, T], F32))
        psB = st.enter_context(nc.psum_tensor(f"{name}_pB", [1, T], F32))
        psR = st.enter_context(nc.psum_tensor(f"{name}_pR", [128, T], F32))
        psM = st.enter_context(nc.psum_tensor(f"{name}_pM", [128, T], F32))
        for ci in range(NCH):
            p.ph("sync", lambda p, s, ci=ci: load_chunk(p, s, x, src, KD, ci, T))
            p.ph("scalar", lambda p, A: p.inc(A.activation(out=sq[:], in_=x[:], func=AF.Square)))
            p.ph("tensor", lambda p, t: p.inc(sum_matmuls(p, t, psA, x[:], KD, T, C.ones_col)))
            p.ph("tensor", lambda p, t: p.inc(sum_matmuls(p, t, psB, sq[:], KD, T, C.ones_col)))
            def fstats(p, v):
                v.tensor_scalar(out=stats[0:1, 0:T], in0=psA[0:1, 0:T], scalar1=1.0 / D,
                                scalar2=None, op0=OP.mult)
                v.tensor_tensor(out=stats[0:1, T:2 * T], in0=stats[0:1, 0:T],
                                in1=stats[0:1, 0:T], op=OP.mult)
                p.inc(v.scalar_tensor_tensor(out=stats[0:1, T:2 * T], in0=psB[0:1, 0:T],
                                             scalar=1.0 / D, in1=stats[0:1, T:2 * T],
                                             op0=OP.mult, op1=OP.subtract))
            p.ph("vector", fstats)
            p.ph("scalar", lambda p, A: p.inc(
                rstd_from_ms(p, A, stats[0:1, 2 * T:3 * T], stats[0:1, T:2 * T], 1.0, EPS_LN)))
            p.ph("vector", lambda p, v: p.inc(
                v.tensor_tensor(out=stats[0:1, 3 * T:4 * T], in0=stats[0:1, 0:T],
                                in1=stats[0:1, 2 * T:3 * T], op=OP.mult)))
            def fb(p, t):
                bcast_matmul(t, psR, stats[0:1, 2 * T:3 * T], T, C.ones_row)
                p.inc(bcast_matmul(t, psM, stats[0:1, 3 * T:4 * T], T, C.ones_row))
            p.ph("tensor", fb)
            def fnorm(p, v):
                for a in range(KD):
                    sl = slice(a * T, (a + 1) * T)
                    v.tensor_tensor(out=sq[:, sl], in0=x[:, sl], in1=psR[:, 0:T], op=OP.mult)
                    last = v.tensor_tensor(out=sq[:, sl], in0=sq[:, sl], in1=psM[:, 0:T],
                                           op=OP.subtract)
                p.inc(last)
            p.ph("vector", fnorm)
            if dst_f is not None:
                p.ph("sync", lambda p, s, ci=ci: store_chunk(p, s, dst_f, sq, KD, ci, T))
            if dst_b is not None:
                p.ph("gpsimd", lambda p, g, ci=ci: p.dma(
                    g.dma_start(out=feat_slice(dst_b, KD, ci, T), in_=sb3(sq[:], KD, T))))


def stage_mamba_a(p, C, W, src_b, yg_dram, bc_dram, T=256, name="ma"):
    nc = p.nc
    NCH = L // T
    SEG = 3 + T
    with ExitStack() as st:
        w_in = st.enter_context(nc.sbuf_tensor(f"{name}_win", [128, KD * (2 * DI)], BF16))
        w_xp = st.enter_context(nc.sbuf_tensor(f"{name}_wxp", [128, KI * 96], BF16))
        w_dtp = st.enter_context(nc.sbuf_tensor(f"{name}_wdtp", [64, DI], BF16))
        convw = st.enter_context(nc.sbuf_tensor(f"{name}_cw", [128, 64], F32))
        convb = st.enter_context(nc.sbuf_tensor(f"{name}_cb", [128, 16], F32))
        dtb = st.enter_context(nc.sbuf_tensor(f"{name}_dtb", [128, 16], F32))
        alogf = st.enter_context(nc.sbuf_tensor(f"{name}_alf", [128, 256], F32))
        aneg = st.enter_context(nc.sbuf_tensor(f"{name}_an", [128, 256], BF16))
        dpar = st.enter_context(nc.sbuf_tensor(f"{name}_dp", [128, 16], F32))
        hcarry = st.enter_context(nc.sbuf_tensor(f"{name}_hc", [128, 256], BF16))
        hist = st.enter_context(nc.sbuf_tensor(f"{name}_hist", [128, 16 * 3], BF16))

        xin_b = st.enter_context(nc.sbuf_tensor(f"{name}_xb", [128, KD * T], BF16))
        x_in = st.enter_context(nc.sbuf_tensor(f"{name}_xin", [128, 2 * KI * SEG], BF16))
        sz = st.enter_context(nc.sbuf_tensor(f"{name}_sz", [128, 2 * KI * T], BF16))
        xc = st.enter_context(nc.sbuf_tensor(f"{name}_xc", [128, 2 * KI * T], BF16))
        ctmp = st.enter_context(nc.sbuf_tensor(f"{name}_ctp", [128, 2 * T], F32))
        dt = st.enter_context(nc.sbuf_tensor(f"{name}_dt", [128, KI * T], BF16))
        wdt = st.enter_context(nc.sbuf_tensor(f"{name}_w", [128, KI * T], BF16))
        xdbl_b = st.enter_context(nc.sbuf_tensor(f"{name}_xdb", [64, T], BF16))
        bc_sb = st.enter_context(nc.sbuf_tensor(f"{name}_bcs", [32, T], BF16))
        bcflat = st.enter_context(nc.sbuf_tensor(f"{name}_bc", [1, 32 * T], BF16))
        brc = st.enter_context(nc.sbuf_tensor(f"{name}_brc", [128, 16 * 2 * T], BF16))
        argq = st.enter_context(nc.sbuf_tensor(f"{name}_ag", [128, 4 * 4 * (T + 2)], BF16))
        WS = T + 2
        h8q = st.enter_context(nc.sbuf_tensor(f"{name}_h8", [128, 4 * WS], BF16))
        dbu = st.enter_context(nc.sbuf_tensor(f"{name}_db", [128, 2 * 4 * WS], BF16))
        hct = st.enter_context(nc.sbuf_tensor(f"{name}_hct", [128, 2 * 4 * T], BF16))
        ytmp = st.enter_context(nc.sbuf_tensor(f"{name}_yt", [128, T], F32))
        ygq = st.enter_context(nc.sbuf_tensor(f"{name}_ygq", [128, 2 * 4 * T], BF16))

        pp = [st.enter_context(nc.psum_tensor(f"{name}_pp{i}", [128, 512], F32)) for i in range(2)]
        pbc = [st.enter_context(nc.psum_tensor(f"{name}_pb{i}", [128, 512], F32)) for i in range(2)]
        py = [st.enter_context(nc.psum_tensor(f"{name}_py{i}", [128, 1024], F32)) for i in range(2)]

        S = Sched(p, name)

        S.add("gpsimd", lambda g: g.dma_start(out=w_in[:].rearrange("p (a m) -> p a m", a=KD),
                                              in_=W["inT"].rearrange("(a p) m -> p a m", p=128)),
              tick="wl", dma=True, event="win")
        S.add("gpsimd", lambda g: g.dma_start(out=w_xp[:].rearrange("p (a m) -> p a m", a=KI),
                                              in_=W["xpT"].rearrange("(a p) m -> p a m", p=128)),
              tick="wl", dma=True, event="wxp")
        S.add("gpsimd", lambda g: g.dma_start(out=w_dtp[:], in_=W["dtpT"]),
              tick="wl", dma=True, event="wdtp")
        S.add("sync", lambda s: s.dma_start(out=convw[:], in_=W["convw"]), tick="ld", dma=True)
        S.add("sync", lambda s: s.dma_start(out=convb[:], in_=W["convb"]), tick="ld", dma=True)
        S.add("sync", lambda s: s.dma_start(out=dtb[:], in_=W["dtb"]), tick="ld", dma=True)
        S.add("sync", lambda s: s.dma_start(out=dpar[:], in_=W["dp"]), tick="ld", dma=True)
        S.add("sync", lambda s: s.dma_start(out=alogf[:], in_=W["alog"]),
              tick="ld", dma=True, event="consts")
        S.add("scalar", lambda A: A.activation(out=alogf[:], in_=alogf[:], func=AF.Exp),
              waits=["consts"], tick="ex", event="aexp")
        def zi(v):
            v.tensor_scalar(out=aneg[:], in0=alogf[:], scalar1=-1.0, scalar2=None, op0=OP.mult)
            v.memset(argq[:], 0.0)
            v.memset(argq[:].rearrange("p (g w) -> p g w", w=WS)[:, :, 1], 1.0)
            v.memset(dbu[:], 0.0)
            v.memset(hcarry[:], 0.0)
            return v.memset(hist[:], 0.0)
        S.add("vector", zi, waits=["aexp"], tick="dv", event="aneg")

        aneg3 = aneg[:].rearrange("p (g s2) -> p g s2", s2=16)

        def emit_inproj(ci, lo, hi):
            cur = ci % 2
            xof = cur * KI * SEG
            cof = cur * KI * T
            if lo == 0:
                S.add("sync", lambda s, ci=ci: s.dma_start(
                    out=sb3(xin_b[:], KD, T), in_=feat_slice(src_b, KD, ci, T)),
                    waits=([f"ipmm_{ci - 1}_{2 * KI - 1}"] if ci > 0 else []),
                    tick="ld", dma=True, event=f"xld_{ci}")
            for et in range(lo, hi):
                bank = pp[et % 2]
                bw = [f"xld_{ci}", "win"]
                if et >= 2:
                    bw.append(f"ipev_{ci}_{et - 2}")
                elif ci > 0:
                    bw.append(f"sp_{ci - 1}_{14 + et}")
                def fmm(t, et=et, bank=bank):
                    for kt in range(KD):
                        inst = t.matmul(bank[:, 0:T],
                                        w_in[:, kt * 2 * DI + et * 128: kt * 2 * DI + et * 128 + 128],
                                        xin_b[:, kt * T:(kt + 1) * T],
                                        start=(kt == 0), stop=(kt == KD - 1))
                    return inst
                S.add("tensor", fmm, waits=bw, tick="pe", event=f"ipmm_{ci}_{et}")
                if et < KI:
                    ew = [f"ipmm_{ci}_{et}"]
                    if ci >= 2:
                        ew.append(f"cv_{ci - 2}_{KI - 1}")
                    S.add("scalar", lambda A, et=et, bank=bank, xof=xof: A.copy(
                        x_in[:, xof + et * SEG + 3: xof + et * SEG + 3 + T], bank[:, 0:T]),
                        waits=ew, tick="ex", event=f"ipev_{ci}_{et}")
                else:
                    g = et - KI
                    ew = [f"ipmm_{ci}_{et}"]
                    if ci >= 2:
                        ew.append(f"yq_{ci - 2}_3")
                    S.add("scalar", lambda A, g=g, bank=bank, cof=cof: A.activation(
                        out=sz[:, cof + g * T: cof + (g + 1) * T], in_=bank[:, 0:T], func=AF.Silu),
                        waits=ew, tick="ex", event=f"ipev_{ci}_{et}")

        emit_inproj(0, 0, 2 * KI)
        for ci in range(NCH):
            cur = ci % 2
            xof = cur * KI * SEG
            cof = cur * KI * T
            for g in range(KI):
                base = xof + g * SEG
                def fconv(v, g=g, base=base):
                    v.tensor_copy(x_in[:, base: base + 3], hist[:, g * 3: g * 3 + 3])
                    v.tensor_scalar(out=ctmp[:, (g % 2) * T:(g % 2) * T + T],
                                    in0=x_in[:, base: base + T],
                                    scalar1=convw[:, 4 * g: 4 * g + 1], scalar2=None, op0=OP.mult)
                    for j in (1, 2, 3):
                        inst = v.scalar_tensor_tensor(
                            out=ctmp[:, (g % 2) * T:(g % 2) * T + T],
                            in0=x_in[:, base + j: base + j + T],
                            scalar=convw[:, 4 * g + j: 4 * g + j + 1],
                            in1=ctmp[:, (g % 2) * T:(g % 2) * T + T],
                            op0=OP.mult, op1=OP.add)
                    inst = v.tensor_copy(hist[:, g * 3: g * 3 + 3], x_in[:, base + T: base + T + 3])
                    return inst
                cw = [f"ipev_{ci}_{g}", "consts", "aneg"]
                if g >= 2:
                    cw.append(f"xc_{ci}_{g - 2}")
                S.add("vector", fconv, waits=cw, tick="dv", event=f"cv_{ci}_{g}")
                sw = [f"cv_{ci}_{g}"]
                if ci >= 2:
                    sw.append(f"yq_{ci - 2}_3")
                S.add("scalar", lambda A, g=g, cof=cof: A.activation(
                    out=xc[:, cof + g * T: cof + (g + 1) * T],
                    in_=ctmp[:, (g % 2) * T:(g % 2) * T + T],
                    func=AF.Silu, bias=convb[:, g: g + 1]),
                    waits=sw, tick="ex", event=f"xc_{ci}_{g}")
            def fxp(t, cof=cof):
                for kt in range(KI):
                    inst = t.matmul(pp[0][0:96, 0:T], w_xp[:, kt * 96:(kt + 1) * 96],
                                    xc[:, cof + kt * T: cof + kt * T + T],
                                    start=(kt == 0), stop=(kt == KI - 1))
                return inst
            S.add("tensor", fxp, waits=[f"xc_{ci}_{KI - 1}", "wxp", f"ipev_{ci}_{2 * KI - 2}"],
                  tick="pe", event=f"xpmm_{ci}")
            def fxe(v):
                v.tensor_copy(xdbl_b[:], pp[0][0:64, 0:T])
                return v.tensor_copy(bc_sb[:], pp[0][64:96, 0:T])
            S.add("vector", fxe, waits=[f"xpmm_{ci}"], tick="dv", event=f"xde_{ci}")
            S.add("scalar", lambda A: A.dma_start(out=bc_dram[0:32, 0:T], in_=bc_sb[:]),
                  waits=[f"xde_{ci}"], tick="sd", dma=True, event=f"bcst_{ci}")
            bw = [f"bcst_{ci}"]
            if ci > 0:
                bw.append(f"brc_{ci - 1}_15")
            S.add("sync", lambda s: s.dma_start(
                out=bcflat[:], in_=bc_dram[0:32, 0:T].rearrange("r t -> (r t)").unsqueeze(0)),
                waits=bw, tick="ld", dma=True, event=f"bcf_{ci}")
            for mt in range(KI):
                bank = pp[mt % 2]
                bw = [f"xde_{ci}", "wdtp"]
                if mt >= 2:
                    bw.append(f"sp_{ci}_{mt - 2}")
                elif mt == 0:
                    bw.append(f"xde_{ci}")
                else:
                    bw.append(f"ipev_{ci}_{2 * KI - 1}")
                S.add("tensor", lambda t, mt=mt, bank=bank: t.matmul(
                    bank[:, 0:T], w_dtp[:, mt * 128:(mt + 1) * 128], xdbl_b[:],
                    start=True, stop=True),
                    waits=bw, tick="pe", event=f"dtmm_{ci}_{mt}")
                spw = [f"dtmm_{ci}_{mt}", f"cv_{ci}_{KI - 1}"]
                if ci > 0:
                    spw.append(f"ag_{ci - 1}_63")
                def fsp(A, mt=mt, bank=bank):
                    A.activation(out=ctmp[:, (mt % 2) * T:(mt % 2) * T + T], in_=bank[:, 0:T],
                                 func=AF.Exp, bias=dtb[:, mt:mt + 1])
                    return A.activation(out=dt[:, mt * T:(mt + 1) * T],
                                        in_=ctmp[:, (mt % 2) * T:(mt % 2) * T + T],
                                        func=AF.Ln, bias=1.0)
                S.add("scalar", fsp, waits=spw, tick="ex", event=f"sp_{ci}_{mt}")
            for s_ in range(DS):
                sp2 = s_ % 2
                bw = [f"bcf_{ci}"]
                if s_ >= 2:
                    bw.append(f"bev_{ci}_{s_ - 2}")
                elif ci > 0:
                    bw.append(f"bev_{ci - 1}_{14 + s_}")
                def fbc(t, s_=s_, sp2=sp2):
                    bcast_matmul(t, pbc[sp2][:, 0:T], bcflat[0:1, s_ * T:(s_ + 1) * T], T, C.ones_row_b)
                    return t.matmul(pbc[sp2][:, T:2 * T], C.ones_row_b[0:1, 0:128],
                                    bcflat[0:1, (16 + s_) * T:(17 + s_) * T], start=True, stop=True)
                S.add("tensor", fbc, waits=bw, tick="pe", event=f"brc_{ci}_{s_}")
                ev = [f"brc_{ci}_{s_}"]
                if ci > 0:
                    ev.append(f"hc_{ci - 1}_{48 + s_}")
                S.add("scalar", lambda A, s_=s_, sp2=sp2: A.copy(
                    brc[:, s_ * 2 * T:(s_ + 1) * 2 * T], pbc[sp2][:, 0:2 * T]),
                    waits=ev, tick="ex", event=f"bev_{ci}_{s_}")
            if ci + 1 < NCH:
                emit_inproj(ci + 1, 0, 4)

            for q in range(4):
                S.add("vector", lambda v, q=q, cof=cof: v.tensor_tensor(
                    out=wdt[:, q * 4 * T:(q + 1) * 4 * T],
                    in0=dt[:, q * 4 * T:(q + 1) * 4 * T],
                    in1=xc[:, cof + q * 4 * T: cof + (q + 1) * 4 * T], op=OP.mult),
                    waits=[f"sp_{ci}_{q * 4 + 3}", f"xc_{ci}_{q * 4 + 3}"],
                    tick="dv", event=f"w_{ci}_{q}")
                for s_ in range(DS):
                    j = q * 16 + s_
                    pr = j % 4
                    if s_ in (0, 8) and not (q == 0 and s_ == 0) and ci + 1 < NCH:
                        k = 2 * q + (1 if s_ == 8 else 0)
                        emit_inproj(ci + 1, 4 * k, 4 * k + 4)
                    aw = [f"sp_{ci}_{q * 4 + 3}", "aneg"]
                    if j >= 4:
                        aw.append(f"scn_{ci}_{j - 4}")
                    elif ci > 0:
                        aw.append(f"scn_{ci - 1}_{60 + j}")
                    S.add("gpsimd", lambda g_, q=q, s_=s_, pr=pr: g_.tensor_tensor(
                        out=argq[:, pr * 4 * WS:(pr + 1) * 4 * WS]
                            .rearrange("p (g w) -> p g w", g=4)[:, :, 2:WS],
                        in0=dt[:, q * 4 * T:(q + 1) * 4 * T].rearrange("p (g t) -> p g t", g=4),
                        in1=aneg3[:, q * 4:(q + 1) * 4, s_:s_ + 1].broadcast_to([128, 4, T]),
                        op=OP.mult),
                        waits=aw, tick="pl", event=f"ag_{ci}_{j}")
                    S.add("scalar", lambda A, pr=pr: A.activation(
                        out=argq[:, pr * 4 * WS:(pr + 1) * 4 * WS]
                            .rearrange("p (g w) -> p g w", g=4)[:, :, 2:WS],
                        in_=argq[:, pr * 4 * WS:(pr + 1) * 4 * WS]
                            .rearrange("p (g w) -> p g w", g=4)[:, :, 2:WS], func=AF.Exp),
                        waits=[f"ag_{ci}_{j}"], tick="ex", event=f"dA_{ci}_{j}")
                    dr = j % 2
                    def fdbu(v, q=q, s_=s_, dr=dr):
                        v.tensor_copy(
                            dbu[:, dr * 4 * WS:(dr + 1) * 4 * WS]
                                .rearrange("p (g w) -> p g w", g=4)[:, :, 0],
                            hcarry[:, s_ * 16 + q * 4: s_ * 16 + q * 4 + 4])
                        return v.tensor_tensor(
                            out=dbu[:, dr * 4 * WS:(dr + 1) * 4 * WS]
                                .rearrange("p (g w) -> p g w", g=4)[:, :, 2:WS],
                            in0=wdt[:, q * 4 * T:(q + 1) * 4 * T].rearrange("p (g t) -> p g t", g=4),
                            in1=brc[:, s_ * 2 * T: s_ * 2 * T + T].unsqueeze(1).broadcast_to([128, 4, T]),
                            op=OP.mult)
                    S.add("vector", fdbu, waits=[f"bev_{ci}_{s_}"], tick="dv")
                    S.add("vector", lambda v, pr=pr, j=j: v.tensor_tensor_scan(
                        out=h8q[:],
                        data0=argq[:, pr * 4 * WS:(pr + 1) * 4 * WS],
                        data1=dbu[:, (j % 2) * 4 * WS:(j % 2 + 1) * 4 * WS],
                        initial=0.0, op0=OP.mult, op1=OP.add),
                        waits=[f"dA_{ci}_{j}"], tick="dv", event=f"scn_{ci}_{j}")
                    hr = j % 2
                    hw = []
                    if j >= 2:
                        hw.append(f"am_{ci}_{j - 2}")
                    elif ci > 0:
                        hw.append(f"am_{ci - 1}_{62 + j}")
                    def fschc(v, q=q, s_=s_, hr=hr):
                        v.tensor_copy(
                            hcarry[:, s_ * 16 + q * 4: s_ * 16 + q * 4 + 4],
                            h8q[:].rearrange("p (g w) -> p g w", g=4)[:, :, WS - 1])
                        return v.tensor_tensor(
                            out=hct[:, hr * 4 * T:(hr + 1) * 4 * T].rearrange("p (g t) -> p g t", g=4),
                            in0=h8q[:].rearrange("p (g w) -> p g w", g=4)[:, :, 2:WS],
                            in1=brc[:, s_ * 2 * T + T: s_ * 2 * T + 2 * T].unsqueeze(1)
                                .broadcast_to([128, 4, T]),
                            op=OP.mult)
                    S.add("vector", fschc, waits=hw, tick="dv", event=f"hc_{ci}_{j}")
                    amw = [f"hc_{ci}_{j}"]
                    if s_ == 0:
                        if ci * 4 + q >= 2:
                            pq = (ci * 4 + q) - 2
                            amw.append(f"yq_{pq // 4}_{pq % 4}")
                    def fam(t, q=q, s_=s_, hr=hr):
                        t.matmul(py[q % 2][:, 0:512], C.ident_b[:],
                                 hct[:, hr * 4 * T: hr * 4 * T + 512],
                                 start=(s_ == 0), stop=(s_ == DS - 1))
                        return t.matmul(py[q % 2][:, 512:1024], C.ident_b[:],
                                        hct[:, hr * 4 * T + 512: hr * 4 * T + 1024],
                                        start=(s_ == 0), stop=(s_ == DS - 1))
                    S.add("tensor", fam, waits=amw, tick="pe", event=f"am_{ci}_{j}")
                for gg in range(4):
                    g = q * 4 + gg
                    yw = [f"am_{ci}_{q * 16 + 15}", f"ipev_{ci}_{2 * KI - 1}"]
                    if ci * 4 + q >= 2:
                        pq = (ci * 4 + q) - 2
                        yw.append(f"ygst_{pq // 4}_{pq % 4}")
                    def fyq(v, q=q, g=g, gg=gg, cof=cof):
                        v.scalar_tensor_tensor(
                            out=ytmp[:], in0=xc[:, cof + g * T: cof + (g + 1) * T],
                            scalar=dpar[:, g:g + 1],
                            in1=py[q % 2][:, gg * T:(gg + 1) * T],
                            op0=OP.mult, op1=OP.add)
                        return v.tensor_tensor(
                            out=ygq[:, (q % 2) * 4 * T + gg * T: (q % 2) * 4 * T + (gg + 1) * T],
                            in0=ytmp[:], in1=sz[:, cof + g * T: cof + (g + 1) * T], op=OP.mult)
                    S.add("vector", fyq, waits=yw, tick="dv",
                          event=(f"yq_{ci}_{q}" if gg == 3 else None))
                S.add("sync", lambda s, ci=ci, q=q: s.dma_start(
                    out=yg_dram.rearrange("(a p) t -> p a t", p=128)[:, q * 4:(q + 1) * 4,
                                                                    ci * T:(ci + 1) * T],
                    in_=ygq[:, (q % 2) * 4 * T:(q % 2 + 1) * 4 * T].rearrange("p (g t) -> p g t", g=4)),
                    waits=[f"yq_{ci}_{q}"], tick="ld", dma=True, event=f"ygst_{ci}_{q}")
        S.run()

def stage_mamba_b(p, C, W, yg_dram, resid_f, dst_f, dst_b, T=512, name="mb"):
    nc = p.nc
    NCH = L // T
    with ExitStack() as st:
        w_out = st.enter_context(nc.sbuf_tensor(f"{name}_wo", [128, KI * D], BF16))
        w_m1 = st.enter_context(nc.sbuf_tensor(f"{name}_w1", [128, KD * DI], BF16))
        w_m2 = st.enter_context(nc.sbuf_tensor(f"{name}_w2", [128, KI * D], BF16))
        b1T = st.enter_context(nc.sbuf_tensor(f"{name}_b1", [128, 16], F32))
        b2T = st.enter_context(nc.sbuf_tensor(f"{name}_b2", [128, 8], F32))
        lnwT = st.enter_context(nc.sbuf_tensor(f"{name}_lw", [128, 8], F32))
        yg = st.enter_context(nc.sbuf_tensor(f"{name}_yg", [128, KI * T], BF16))
        mo = st.enter_context(nc.sbuf_tensor(f"{name}_mo", [128, KD * T], BF16))
        h1 = st.enter_context(nc.sbuf_tensor(f"{name}_h1", [128, KI * T], BF16))
        mout = st.enter_context(nc.sbuf_tensor(f"{name}_mt", [128, KD * T], F32))
        resid = st.enter_context(nc.sbuf_tensor(f"{name}_rs", [128, KD * T], F32))
        sq = st.enter_context(nc.sbuf_tensor(f"{name}_sq", [128, KD * T], F32))
        stats = st.enter_context(nc.sbuf_tensor(f"{name}_st", [1, T], F32))
        pp = [st.enter_context(nc.psum_tensor(f"{name}_p{i}", [128, T], F32)) for i in range(2)]
        psS = st.enter_context(nc.psum_tensor(f"{name}_ps", [1, T], F32))
        psR = st.enter_context(nc.psum_tensor(f"{name}_pr", [128, T], F32))

        S = Sched(p, name)
        S.add("gpsimd", lambda g: g.dma_start(out=w_out[:].rearrange("p (a m) -> p a m", a=KI),
                                              in_=W["outT"].rearrange("(a p) m -> p a m", p=128)),
              tick="wl", dma=True, event="wo")
        S.add("gpsimd", lambda g: g.dma_start(out=w_m1[:].rearrange("p (a m) -> p a m", a=KD),
                                              in_=W["m1T"].rearrange("(a p) m -> p a m", p=128)),
              tick="wl", dma=True, event="w1")
        S.add("gpsimd", lambda g: g.dma_start(out=w_m2[:].rearrange("p (a m) -> p a m", a=KI),
                                              in_=W["m2T"].rearrange("(a p) m -> p a m", p=128)),
              tick="wl", dma=True, event="w2")
        S.add("sync", lambda s: s.dma_start(out=b1T[:], in_=W["b1T"]), tick="ld", dma=True)
        S.add("sync", lambda s: s.dma_start(out=b2T[:], in_=W["b2T"]), tick="ld", dma=True)
        S.add("sync", lambda s: s.dma_start(out=lnwT[:], in_=W["lnwT"]),
              tick="ld", dma=True, event="consts")

        last_bank_ev = [None, None]
        for ci in range(NCH):
            S.add("sync", lambda s, ci=ci: s.dma_start(
                out=sb3(yg[:], KI, T), in_=feat_slice(yg_dram, KI, ci, T)),
                waits=([f"opmm_{ci - 1}_{KD - 1}"] if ci > 0 else []),
                tick="ld", dma=True, event=f"ygld_{ci}")
            S.add("sync", lambda s, ci=ci: s.dma_start(
                out=sb3(resid[:], KD, T), in_=feat_slice(resid_f, KD, ci, T)),
                waits=([f"add_{ci - 1}"] if ci > 0 else []),
                tick="ld", dma=True, event=f"rsld_{ci}")
            for mt in range(KD):
                b = mt % 2
                bw = [f"ygld_{ci}", "wo"] + ([last_bank_ev[b]] if last_bank_ev[b] else [])
                def fmm(t, mt=mt, b=b):
                    for kt in range(KI):
                        inst = t.matmul(pp[b][:, 0:T],
                                        w_out[:, kt * D + mt * 128: kt * D + mt * 128 + 128],
                                        yg[:, kt * T:(kt + 1) * T], start=(kt == 0), stop=(kt == KI - 1))
                    return inst
                S.add("tensor", fmm, waits=bw, tick="pe", event=f"opmm_{ci}_{mt}")
                ew = [f"opmm_{ci}_{mt}"] + ([f"m1_{ci - 1}_{KI - 1}"] if ci > 0 else [])
                S.add("vector", lambda v, mt=mt, b=b: v.tensor_copy(
                    mo[:, mt * T:(mt + 1) * T], pp[b][:, 0:T]),
                    waits=ew, tick="dv", event=f"moev_{ci}_{mt}")
                last_bank_ev[b] = f"moev_{ci}_{mt}"
            for et in range(KI):
                b = et % 2
                bw = [f"moev_{ci}_{KD - 1}", "w1", last_bank_ev[b]]
                def fmm(t, et=et, b=b):
                    for kt in range(KD):
                        inst = t.matmul(pp[b][:, 0:T],
                                        w_m1[:, kt * DI + et * 128: kt * DI + et * 128 + 128],
                                        mo[:, kt * T:(kt + 1) * T], start=(kt == 0), stop=(kt == KD - 1))
                    return inst
                S.add("tensor", fmm, waits=bw, tick="pe", event=f"m1_{ci}_{et}")
                ew = [f"m1_{ci}_{et}", "consts"] + ([f"m2_{ci - 1}_{KD - 1}"] if ci > 0 and et < 2 else [])
                S.add("scalar", lambda A, et=et, b=b: A.activation(
                    out=h1[:, et * T:(et + 1) * T], in_=pp[b][:, 0:T],
                    func=AF.Gelu, bias=b1T[:, et:et + 1]),
                    waits=ew, tick="ex", event=f"h1_{ci}_{et}")
                last_bank_ev[b] = f"h1_{ci}_{et}"
            for mt in range(KD):
                b = mt % 2
                bw = [f"h1_{ci}_{KI - 1}", "w2", last_bank_ev[b]]
                def fmm(t, mt=mt, b=b):
                    for kt in range(KI):
                        inst = t.matmul(pp[b][:, 0:T],
                                        w_m2[:, kt * D + mt * 128: kt * D + mt * 128 + 128],
                                        h1[:, kt * T:(kt + 1) * T], start=(kt == 0), stop=(kt == KI - 1))
                    return inst
                S.add("tensor", fmm, waits=bw, tick="pe", event=f"m2_{ci}_{mt}")
                ew = [f"m2_{ci}_{mt}"] + ([f"nrm_{ci - 1}"] if ci > 0 and mt < 2 else [])
                S.add("scalar", lambda A, mt=mt, b=b: A.activation(
                    out=mout[:, mt * T:(mt + 1) * T], in_=pp[b][:, 0:T],
                    func=AF.Gelu, bias=b2T[:, mt:mt + 1]),
                    waits=ew, tick="ex", event=f"mt_{ci}_{mt}")
                last_bank_ev[b] = f"mt_{ci}_{mt}"
            S.add("vector", lambda v: v.tensor_tensor(out=mout[:], in0=mout[:], in1=resid[:], op=OP.add),
                  waits=[f"mt_{ci}_{KD - 1}", f"rsld_{ci}"], tick="dv", event=f"add_{ci}")
            sqw = [f"add_{ci}"] + ([f"sums_{ci - 1}", f"stf_{ci - 1}", f"stb_{ci - 1}"] if ci > 0 else [])
            S.add("scalar", lambda A: A.activation(out=sq[:], in_=mout[:], func=AF.Square),
                  waits=sqw, tick="ex", event=f"sq_{ci}")
            S.add("tensor", lambda t: sum_matmuls(p, t, psS, sq[:], KD, T, C.ones_col),
                  waits=[f"sq_{ci}"] + ([f"rstd_{ci - 1}"] if ci > 0 else []),
                  tick="pe", event=f"sums_{ci}")
            S.add("scalar", lambda A: rstd_from_ms(p, A, stats[0:1, 0:T], psS[0:1, 0:T], 1.0 / D, EPS_RMS),
                  waits=[f"sums_{ci}"], tick="ex", event=f"rstd_{ci}")
            S.add("tensor", lambda t: bcast_matmul(t, psR, stats[0:1, 0:T], T, C.ones_row),
                  waits=[f"rstd_{ci}"] + ([f"nrm_{ci - 1}"] if ci > 0 else []),
                  tick="pe", event=f"bc_{ci}")
            def fnorm(v):
                for a in range(KD):
                    sl = slice(a * T, (a + 1) * T)
                    inst = v.scalar_tensor_tensor(out=sq[:, sl], in0=mout[:, sl],
                                                  scalar=lnwT[:, a:a + 1], in1=psR[:, 0:T],
                                                  op0=OP.mult, op1=OP.mult)
                return inst
            S.add("vector", fnorm, waits=[f"bc_{ci}"] +
                  ([f"stf_{ci - 1}", f"stb_{ci - 1}"] if ci > 0 else []),
                  tick="dv", event=f"nrm_{ci}")
            if dst_f is not None:
                S.add("gpsimd", lambda g, ci=ci: g.dma_start(
                    out=feat_slice(dst_f, KD, ci, T), in_=sb3(sq[:], KD, T)),
                    waits=[f"nrm_{ci}"], tick="wl", dma=True, event=f"stf_{ci}")
            else:
                S.events[f"stf_{ci}"] = S.events[f"nrm_{ci}"]
            if dst_b is not None:
                S.add("gpsimd", lambda g, ci=ci: g.dma_start(
                    out=feat_slice(dst_b, KD, ci, T), in_=sb3(sq[:], KD, T)),
                    waits=[f"nrm_{ci}"], tick="wl", dma=True, event=f"stb_{ci}")
            else:
                S.events[f"stb_{ci}"] = S.events[f"nrm_{ci}"]
        S.run()

def stage_mamba_tail(p, C, ln2wT_dram, xcur_f, u1_f, x0_f, dst_f, T=512, name="mt"):
    nc = p.nc
    NCH = L // T
    with ExitStack() as st:
        lnwT = st.enter_context(nc.sbuf_tensor(f"{name}_lw", [128, 8], F32))
        a1 = st.enter_context(nc.sbuf_tensor(f"{name}_a", [128, KD * T], F32))
        a2 = st.enter_context(nc.sbuf_tensor(f"{name}_b", [128, KD * T], F32))
        sq = st.enter_context(nc.sbuf_tensor(f"{name}_sq", [128, KD * T], F32))
        stats = st.enter_context(nc.sbuf_tensor(f"{name}_st", [1, 4 * T], F32))
        psA = st.enter_context(nc.psum_tensor(f"{name}_pA", [1, T], F32))
        psB = st.enter_context(nc.psum_tensor(f"{name}_pB", [1, T], F32))
        psR = st.enter_context(nc.psum_tensor(f"{name}_pR", [128, T], F32))
        psM = st.enter_context(nc.psum_tensor(f"{name}_pM", [128, T], F32))
        S = Sched(p, name)
        S.add("sync", lambda s: s.dma_start(out=lnwT[:], in_=ln2wT_dram),
              tick="ld", dma=True, event="lw")
        for ci in range(NCH):
            S.add("sync", lambda s, ci=ci: s.dma_start(
                out=sb3(a1[:], KD, T), in_=feat_slice(xcur_f, KD, ci, T)),
                waits=(["lw"] if ci == 0 else [f"fn_{ci - 1}"]), tick="ld", dma=True,
                event=f"l1_{ci}")
            S.add("sync", lambda s, ci=ci: s.dma_start(
                out=sb3(a2[:], KD, T), in_=feat_slice(u1_f, KD, ci, T)),
                waits=([f"fn_{ci - 1}"] if ci > 0 else []), tick="ld", dma=True,
                event=f"l2_{ci}")
            S.add("vector", lambda v: v.tensor_tensor(out=a1[:], in0=a1[:], in1=a2[:], op=OP.add),
                  waits=[f"l1_{ci}", f"l2_{ci}"], tick="dv", event=f"ad_{ci}")
            S.add("scalar", lambda A: A.activation(out=sq[:], in_=a1[:], func=AF.Square),
                  waits=[f"ad_{ci}"] + ([f"stc_{ci - 1}"] if ci > 0 else []),
                  tick="ex", event=f"sq1_{ci}")
            S.add("tensor", lambda t: sum_matmuls(p, t, psA, sq[:], KD, T, C.ones_col),
                  waits=[f"sq1_{ci}"] + ([f"vst2_{ci - 1}"] if ci > 0 else []),
                  tick="pe", event=f"sum1_{ci}")
            S.add("scalar", lambda A: rstd_from_ms(p, A, stats[0:1, 0:T], psA[0:1, 0:T],
                                                   1.0 / D, EPS_RMS),
                  waits=[f"sum1_{ci}"] + ([f"bb2_{ci - 1}"] if ci > 0 else []),
                  tick="ex", event=f"rs1_{ci}")
            S.add("tensor", lambda t: bcast_matmul(t, psR, stats[0:1, 0:T], T, C.ones_row),
                  waits=[f"rs1_{ci}"] + ([f"fn_{ci - 1}"] if ci > 0 else []),
                  tick="pe", event=f"bc1_{ci}")
            def fbr(v):
                for a in range(KD):
                    sl = slice(a * T, (a + 1) * T)
                    inst = v.scalar_tensor_tensor(out=a1[:, sl], in0=a1[:, sl],
                                                  scalar=lnwT[:, a:a + 1], in1=psR[:, 0:T],
                                                  op0=OP.mult, op1=OP.mult)
                return inst
            S.add("vector", fbr, waits=[f"bc1_{ci}"], tick="dv", event=f"br_{ci}")
            S.add("scalar", lambda A: A.activation(out=sq[:], in_=a1[:], func=AF.Square),
                  waits=[f"br_{ci}"], tick="ex", event=f"sq2_{ci}")
            def fsums2(t):
                sum_matmuls(p, t, psA, a1[:], KD, T, C.ones_col)
                return sum_matmuls(p, t, psB, sq[:], KD, T, C.ones_col)
            S.add("tensor", fsums2, waits=[f"sq2_{ci}"], tick="pe", event=f"sum2_{ci}")
            def fstats(v):
                v.tensor_scalar(out=stats[0:1, 0:T], in0=psA[0:1, 0:T], scalar1=1.0 / D,
                                scalar2=None, op0=OP.mult)
                v.tensor_tensor(out=stats[0:1, T:2 * T], in0=stats[0:1, 0:T],
                                in1=stats[0:1, 0:T], op=OP.mult)
                return v.scalar_tensor_tensor(out=stats[0:1, T:2 * T], in0=psB[0:1, 0:T],
                                              scalar=1.0 / D, in1=stats[0:1, T:2 * T],
                                              op0=OP.mult, op1=OP.subtract)
            S.add("vector", fstats, waits=[f"sum2_{ci}"], tick="dv", event=f"vst2_{ci}")
            S.add("scalar", lambda A: rstd_from_ms(p, A, stats[0:1, 2 * T:3 * T],
                                                   stats[0:1, T:2 * T], 1.0, EPS_LN),
                  waits=[f"vst2_{ci}"], tick="ex", event=f"rs2_{ci}")
            S.add("vector", lambda v: v.tensor_tensor(out=stats[0:1, 3 * T:4 * T],
                                                      in0=stats[0:1, 0:T],
                                                      in1=stats[0:1, 2 * T:3 * T], op=OP.mult),
                  waits=[f"rs2_{ci}"], tick="dv", event=f"mrs2_{ci}")
            def fb(t):
                bcast_matmul(t, psR, stats[0:1, 2 * T:3 * T], T, C.ones_row)
                return bcast_matmul(t, psM, stats[0:1, 3 * T:4 * T], T, C.ones_row)
            S.add("tensor", fb, waits=[f"mrs2_{ci}"], tick="pe", event=f"bb2_{ci}")
            S.add("sync", lambda s, ci=ci: s.dma_start(
                out=sb3(a2[:], KD, T), in_=feat_slice(x0_f, KD, ci, T)),
                waits=[f"ad_{ci}"], tick="ld", dma=True, event=f"l3_{ci}")
            def fnorm(v):
                for a in range(KD):
                    sl = slice(a * T, (a + 1) * T)
                    v.tensor_tensor(out=sq[:, sl], in0=a1[:, sl], in1=psR[:, 0:T], op=OP.mult)
                    v.tensor_tensor(out=sq[:, sl], in0=sq[:, sl], in1=psM[:, 0:T], op=OP.subtract)
                    inst = v.tensor_tensor(out=sq[:, sl], in0=sq[:, sl], in1=a2[:, sl], op=OP.add)
                return inst
            S.add("vector", fnorm, waits=[f"bb2_{ci}", f"l3_{ci}"] +
                  ([f"stc_{ci - 1}"] if ci > 0 else []), tick="dv", event=f"fn_{ci}")
            S.add("sync", lambda s, ci=ci: s.dma_start(
                out=feat_slice(dst_f, KD, ci, T), in_=sb3(sq[:], KD, T)),
                waits=[f"fn_{ci}"], tick="ld", dma=True, event=f"stc_{ci}")
        S.run()


def _t(a):
    return np.ascontiguousarray(np.asarray(a).T)

def _cols(a, k):
    return np.ascontiguousarray(np.asarray(a).reshape(k, 128).T)


def build_masks():
    m = np.zeros((128, 4 * 512), dtype=ml_dtypes.bfloat16)
    for j in range(4):
        pp = np.arange(128)[:, None] + 128 * j
        ff = np.arange(512)[None, :]
        m[:, j * 512:(j + 1) * 512] = (pp <= ff).astype(np.float32)
    return m


def stage_moe_pre(p, C, W, x1_f, u2_b, comb_dram, T=512, name="mp"):
    nc = p.nc
    NCH = L // T
    with ExitStack() as st:
        gw = st.enter_context(nc.sbuf_tensor(f"{name}_gw", [128, KD * 8], BF16))
        gbr = st.enter_context(nc.sbuf_tensor(f"{name}_gb", [128, 8], F32))
        x = st.enter_context(nc.sbuf_tensor(f"{name}_x", [128, KD * T], F32))
        xb = st.enter_context(nc.sbuf_tensor(f"{name}_xb", [128, KD * T], BF16))
        sq = st.enter_context(nc.sbuf_tensor(f"{name}_sq", [128, KD * T], F32))
        stats = st.enter_context(nc.sbuf_tensor(f"{name}_st", [1, 4 * T], F32))
        gsc = st.enter_context(nc.sbuf_tensor(f"{name}_gs", [128, 64], F32))
        comb = st.enter_context(nc.sbuf_tensor(f"{name}_cb", [8, T], F32))
        psA = st.enter_context(nc.psum_tensor(f"{name}_pA", [1, T], F32))
        psB = st.enter_context(nc.psum_tensor(f"{name}_pB", [1, T], F32))
        psR = st.enter_context(nc.psum_tensor(f"{name}_pR", [128, T], F32))
        psM = st.enter_context(nc.psum_tensor(f"{name}_pM", [128, T], F32))
        psG = st.enter_context(nc.psum_tensor(f"{name}_pG", [128, 8], F32))
        psT = st.enter_context(nc.psum_tensor(f"{name}_pT", [8, 128], F32))

        def lw(p, g):
            p.dma(g.dma_start(out=gw[:].rearrange("p (a m) -> p a m", a=KD),
                              in_=W["gate_w"].rearrange("(a p) m -> p a m", p=128)))
        p.ph("gpsimd", lw)
        p.ph("sync", lambda p, s: p.dma(s.dma_start(out=gbr[:], in_=W["gate_b_rep"])))

        for ci in range(NCH):
            p.ph("sync", lambda p, s, ci=ci: load_chunk(p, s, x, x1_f, KD, ci, T))
            p.ph("scalar", lambda p, A: p.inc(A.activation(out=sq[:], in_=x[:], func=AF.Square)))
            p.ph("tensor", lambda p, t: p.inc(sum_matmuls(p, t, psA, x[:], KD, T, C.ones_col)))
            p.ph("tensor", lambda p, t: p.inc(sum_matmuls(p, t, psB, sq[:], KD, T, C.ones_col)))
            def fstats(p, v):
                v.tensor_scalar(out=stats[0:1, 0:T], in0=psA[0:1, 0:T], scalar1=1.0 / D,
                                scalar2=None, op0=OP.mult)
                v.tensor_tensor(out=stats[0:1, T:2 * T], in0=stats[0:1, 0:T],
                                in1=stats[0:1, 0:T], op=OP.mult)
                p.inc(v.scalar_tensor_tensor(out=stats[0:1, T:2 * T], in0=psB[0:1, 0:T],
                                             scalar=1.0 / D, in1=stats[0:1, T:2 * T],
                                             op0=OP.mult, op1=OP.subtract))
            p.ph("vector", fstats)
            p.ph("scalar", lambda p, A: p.inc(
                rstd_from_ms(p, A, stats[0:1, 2 * T:3 * T], stats[0:1, T:2 * T], 1.0, EPS_LN)))
            p.ph("vector", lambda p, v: p.inc(
                v.tensor_tensor(out=stats[0:1, 3 * T:4 * T], in0=stats[0:1, 0:T],
                                in1=stats[0:1, 2 * T:3 * T], op=OP.mult)))
            def fb(p, t):
                bcast_matmul(t, psR, stats[0:1, 2 * T:3 * T], T, C.ones_row)
                p.inc(bcast_matmul(t, psM, stats[0:1, 3 * T:4 * T], T, C.ones_row))
            p.ph("tensor", fb)
            def fnorm(p, v):
                for a in range(KD):
                    sl = slice(a * T, (a + 1) * T)
                    v.tensor_tensor(out=sq[:, sl], in0=x[:, sl], in1=psR[:, 0:T], op=OP.mult)
                    v.tensor_tensor(out=sq[:, sl], in0=sq[:, sl], in1=psM[:, 0:T], op=OP.subtract)
                    inst = v.tensor_copy(xb[:, sl], sq[:, sl])
                p.inc(inst)
            p.ph("vector", fnorm)
            p.ph("gpsimd", lambda p, g, ci=ci: p.dma(
                g.dma_start(out=feat_slice(u2_b, KD, ci, T), in_=sb3(sq[:], KD, T))))

            for ts in range(T // 128):
                def fg(p, t, ts=ts):
                    for kt in range(KD):
                        inst = t.matmul(psG[:, 0:8], xb[:, kt * T + ts * 128: kt * T + ts * 128 + 128],
                                        gw[:, kt * 8:(kt + 1) * 8], start=(kt == 0), stop=(kt == KD - 1))
                    p.inc(inst)
                p.ph("tensor", fg)
                g0 = gsc[:, 0:8]; m1 = gsc[:, 8:16]; m2 = gsc[:, 16:24]; t0 = gsc[:, 24:32]
                mx = gsc[:, 32:33]; v2 = gsc[:, 33:34]; den = gsc[:, 34:35]
                nv2 = gsc[:, 35:36]; r = gsc[:, 36:37]; c2 = gsc[:, 37:38]
                p.ph("vector", lambda p, v: p.inc(
                    v.tensor_tensor(out=g0, in0=psG[:, 0:8], in1=gbr[:], op=OP.add)))
                p.ph("vector", lambda p, v: p.inc(v.reduce_max(mx, g0, axis=AX.X, negate=True)))
                p.ph("scalar", lambda p, A: p.inc(
                    A.activation(out=g0, in_=g0, func=AF.Exp, bias=mx)))
                p.ph("scalar", lambda p, A: p.inc(
                    A.activation(out=m1, in_=g0, func=AF.Sign, bias=-1.0)))
                p.ph("vector", lambda p, v: p.inc(
                    v.tensor_scalar(out=m1, in0=m1, scalar1=1.0, scalar2=None, op0=OP.add)))
                p.ph("vector", lambda p, v: p.inc(
                    v.scalar_tensor_tensor(out=t0, in0=m1, scalar=-2.0, in1=g0,
                                           op0=OP.mult, op1=OP.add)))
                def fred(p, v):
                    v.reduce_max(v2, t0, axis=AX.X)
                    p.inc(v.reduce_max(nv2, t0, axis=AX.X, negate=True))
                p.ph("vector", fred)
                p.ph("scalar", lambda p, A: p.inc(
                    A.activation(out=m2, in_=t0, func=AF.Sign, bias=nv2)))
                def fcw1(p, v):
                    v.tensor_scalar(out=m2, in0=m2, scalar1=1.0, scalar2=None, op0=OP.add)
                    p.inc(v.tensor_scalar(out=den, in0=v2, scalar1=1.0, scalar2=None, op0=OP.add))
                p.ph("vector", fcw1)
                p.ph("vector", lambda p, v: p.inc(v.reciprocal(out=r, in_=den)))
                def fcw2(p, v):
                    v.tensor_tensor(out=c2, in0=v2, in1=r, op=OP.mult)
                    p.inc(v.tensor_tensor(out=m1, in0=m1, in1=r.broadcast_to([128, 8]), op=OP.mult))
                p.ph("vector", fcw2)
                p.ph("vector", lambda p, v: p.inc(
                    v.tensor_tensor(out=m2, in0=m2, in1=c2.broadcast_to([128, 8]), op=OP.mult)))
                p.ph("vector", lambda p, v: p.inc(
                    v.tensor_tensor(out=gsc[:, 40:48], in0=m1, in1=m2, op=OP.add)))
                p.ph("tensor", lambda p, t, ts=ts: p.inc(
                    t.transpose(psT[:], gsc[:, 40:48], C.ident_f[:])))
                p.ph("vector", lambda p, v, ts=ts: p.inc(
                    v.tensor_copy(comb[:, ts * 128:(ts + 1) * 128], psT[0:8, 0:128])))
            p.ph("sync", lambda p, s, ci=ci: p.dma(s.dma_start(
                out=comb_dram[ci * 8 * T:(ci + 1) * 8 * T].rearrange("(r t) -> r t", r=8),
                in_=comb[:])))


def stage_moe(p, C, W, u2_b, comb_dram, acc_f, T=512, name="moe"):
    nc = p.nc
    NCH = L // T
    with ExitStack() as st:
        w1 = st.enter_context(nc.sbuf_tensor(f"{name}_w1", [128, KD * FF], BF16))
        w2 = st.enter_context(nc.sbuf_tensor(f"{name}_w2", [128, KF * D], BF16))
        b1T = st.enter_context(nc.sbuf_tensor(f"{name}_b1", [128, NE * KF], F32))
        b2m = st.enter_context(nc.sbuf_tensor(f"{name}_b2", [8, D], BF16))
        xb = st.enter_context(nc.sbuf_tensor(f"{name}_xb", [128, 2 * KD * T], BF16))
        h = st.enter_context(nc.sbuf_tensor(f"{name}_h", [128, KF * T], BF16))
        cb8 = st.enter_context(nc.sbuf_tensor(f"{name}_c8", [8, T], BF16))
        cbe = st.enter_context(nc.sbuf_tensor(f"{name}_cf", [1, 2 * T], F32))
        ceb = st.enter_context(nc.sbuf_tensor(f"{name}_ce", [128, T], F32))
        contrib = st.enter_context(nc.sbuf_tensor(f"{name}_ct", [128, KD * T], F32))
        pm = [st.enter_context(nc.psum_tensor(f"{name}_p{i}", [128, T], F32)) for i in range(4)]
        psc = st.enter_context(nc.psum_tensor(f"{name}_pc", [128, T], F32))

        S = Sched(p, name)
        S.add("sync", lambda s: s.dma_start(out=b1T[:].rearrange("p (e m) -> p e m", e=NE),
                                            in_=W["moe_b1T"]), tick="ld", dma=True)
        S.add("gpsimd", lambda g_: g_.dma_start(out=b2m[:], in_=W["moe_b2"]),
              tick="wl", dma=True, event="b2m")

        def wload(e, which):
            def f(g, e=e, which=which):
                if which == 0:
                    return g.dma_start(out=w1[:].rearrange("p (a m) -> p a m", a=KD),
                                       in_=W["moe_w1"][e].rearrange("(a p) m -> p a m", p=128))
                return g.dma_start(out=w2[:].rearrange("p (a m) -> p a m", a=KF),
                                   in_=W["moe_w2"][e].rearrange("(a p) m -> p a m", p=128))
            return f
        S.add("gpsimd", wload(0, 0), tick="wl", dma=True, event="w1_0")
        S.add("gpsimd", wload(0, 1), tick="wl", dma=True, event="w2_0")

        for ci in range(NCH):
            S.add("gpsimd", lambda g_, ci=ci: g_.dma_start(
                out=cb8[:], in_=comb_dram[ci * 8 * T:(ci + 1) * 8 * T].rearrange("(r t) -> r t", r=8)),
                waits=([("st", 16 * ci)] if ci > 0 else []),
                tick="wl", dma=True, event=f"cb8_{ci}")
            for mt in range(KD):
                g = ci * KD + mt
                bank = pm[mt % 2]
                S.add("tensor", lambda t, mt=mt, bank=bank: t.matmul(
                    bank[:, 0:T], b2m[:, mt * 128:(mt + 1) * 128], cb8[:], start=True, stop=True),
                    waits=[f"cb8_{ci}", "b2m"] + ([("dv", g - 1)] if g >= 2 else []),
                    tick="pe", event=f"bmm_{ci}_{mt}")
                S.add("vector", lambda v, mt=mt, bank=bank: v.tensor_copy(
                    contrib[:, mt * T:(mt + 1) * T], bank[:, 0:T]),
                    waits=[f"bmm_{ci}_{mt}"] + ([("st", 16 * ci)] if ci > 0 else []),
                    tick="dv", event=f"bcp_{ci}_{mt}")
            S.add("gpsimd", lambda g_, ci=ci: g_.dma_start(
                out=feat_slice(acc_f, KD, ci, T), in_=sb3(contrib[:], KD, T)),
                waits=[f"bcp_{ci}_{KD - 1}"], tick="st", dma=True, event=f"bst_{ci}")
        bias_dv = S.ctr("dv").n

        for e in range(NE):
            for ci in range(NCH):
                u = e * NCH + ci
                xs = (u % 2) * KD * T
                cs = (u % 2) * T
                lw = [f"mm1_{u - 2}_{KF - 1}"] if u >= 2 else []
                S.add("sync", lambda s, ci=ci, xs=xs: s.dma_start(
                    out=sb3(xb[:, xs:xs + KD * T], KD, T), in_=feat_slice(u2_b, KD, ci, T)),
                    waits=lw, tick="ld", dma=True)
                S.add("sync", lambda s, ci=ci, e=e, cs=cs: s.dma_start(
                    out=cbe[0:1, cs:cs + T],
                    in_=comb_dram[(ci * 8 + e) * T:(ci * 8 + e + 1) * T].unsqueeze(0)),
                    tick="ld", dma=True, event=f"ld_{u}")
                S.add("tensor", lambda t, cs=cs: bcast_matmul(t, psc, cbe[0:1, cs:cs + T], T, C.ones_row),
                      waits=[f"ld_{u}"] + ([f"cecp_{u - 1}"] if u > 0 else []),
                      tick="pe", event=f"cemm_{u}")
                S.add("vector", lambda v: v.tensor_copy(ceb[:], psc[:, 0:T]),
                      waits=[f"cemm_{u}"] + ([f"ct_{u - 1}_{KD - 1}"] if u > 0 else []),
                      tick="dv", event=f"cecp_{u}")
                for et in range(KF):
                    bank = pm[et % 2]
                    def f_mm1(t, et=et, bank=bank, xs=xs):
                        for kt in range(KD):
                            inst = t.matmul(bank[:, 0:T],
                                            w1[:, kt * FF + et * 128: kt * FF + et * 128 + 128],
                                            xb[:, xs + kt * T: xs + kt * T + T],
                                            start=(kt == 0), stop=(kt == KD - 1))
                        return inst
                    wts = [f"ld_{u}", f"w1_{e}"]
                    if et >= 2:
                        wts.append(f"h_{u}_{et - 2}")
                    elif u > 0:
                        wts.append(f"h_{u - 1}_{KF - 2 + et}")
                    else:
                        wts.append(("dv", bias_dv - 2 + et + 1))
                    S.add("tensor", f_mm1, waits=wts, tick="pe", event=f"mm1_{u}_{et}")
                    rw = [f"mm1_{u}_{et}"]
                    if u > 0:
                        rw.append(f"mm2_{u - 1}_{KD - 1}")
                    S.add("scalar", lambda A, et=et, e=e, bank=bank: A.activation(
                        out=h[:, et * T:(et + 1) * T], in_=bank[:, 0:T],
                        func=AF.Relu, bias=b1T[:, e * KF + et: e * KF + et + 1]),
                        waits=rw, tick="ex", event=f"h_{u}_{et}")
                if ci == NCH - 1 and e < NE - 1:
                    S.add("gpsimd", wload(e + 1, 0), waits=[f"mm1_{u}_{KF - 1}"],
                          tick="wl", dma=True, event=f"w1_{e + 1}")
                for mt in range(KD):
                    bank = pm[2 + mt % 2]
                    def f_mm2(t, mt=mt, bank=bank):
                        for kt in range(KF):
                            inst = t.matmul(bank[:, 0:T],
                                            w2[:, kt * D + mt * 128: kt * D + mt * 128 + 128],
                                            h[:, kt * T:(kt + 1) * T],
                                            start=(kt == 0), stop=(kt == KF - 1))
                        return inst
                    wts = [f"h_{u}_{KF - 1}", f"w2_{e}"]
                    if mt >= 2:
                        wts.append(f"ct_{u}_{mt - 2}")
                    elif u > 0:
                        wts.append(f"ct_{u - 1}_{KD - 2 + mt}")
                    S.add("tensor", f_mm2, waits=wts, tick="pe", event=f"mm2_{u}_{mt}")
                    cw = [f"mm2_{u}_{mt}", f"cecp_{u}"]
                    if mt == 0:
                        cw.append(f"acc_{u - 1}" if u > 0 else f"bst_{NCH - 1}")
                    S.add("vector", lambda v, mt=mt: v.tensor_tensor(
                        out=contrib[:, mt * T:(mt + 1) * T], in0=pm[2 + mt % 2][:, 0:T],
                        in1=ceb[:], op=OP.mult),
                        waits=cw, tick="dv", event=f"ct_{u}_{mt}")
                S.add("gpsimd", lambda g_, ci=ci: g_.dma_start(
                    out=feat_slice(acc_f, KD, ci, T), in_=sb3(contrib[:], KD, T), accum_op=OP.add),
                    waits=[f"ct_{u}_{KD - 1}"], tick="st", dma=True, event=f"acc_{u}")
                if ci == NCH - 1 and e < NE - 1:
                    S.add("gpsimd", wload(e + 1, 1), waits=[f"mm2_{u}_{KD - 1}"],
                          tick="wl", dma=True, event=f"w2_{e + 1}")
        S.run()


def stage_moe_old(p, C, W, u2_b, comb_dram, acc_f, T=512, name="moeo"):
    nc = p.nc
    NCH = L // T
    with ExitStack() as st:
        w1 = st.enter_context(nc.sbuf_tensor(f"{name}_w1", [128, KD * FF], BF16))
        w2 = st.enter_context(nc.sbuf_tensor(f"{name}_w2", [128, KF * D], BF16))
        b1T = st.enter_context(nc.sbuf_tensor(f"{name}_b1", [128, NE * KF], F32))
        b2m = st.enter_context(nc.sbuf_tensor(f"{name}_b2", [8, D], F32))
        xb = st.enter_context(nc.sbuf_tensor(f"{name}_xb", [128, KD * T], BF16))
        h = st.enter_context(nc.sbuf_tensor(f"{name}_h", [128, KF * T], BF16))
        cb8 = st.enter_context(nc.sbuf_tensor(f"{name}_c8", [8, T], F32))
        cbe = st.enter_context(nc.sbuf_tensor(f"{name}_cf", [1, T], F32))
        ceb = st.enter_context(nc.sbuf_tensor(f"{name}_ce", [128, T], F32))
        contrib = st.enter_context(nc.sbuf_tensor(f"{name}_ct", [128, KD * T], F32))
        ps0 = st.enter_context(nc.psum_tensor(f"{name}_p0", [128, T], F32))
        ps1 = st.enter_context(nc.psum_tensor(f"{name}_p1", [128, T], F32))
        psc = st.enter_context(nc.psum_tensor(f"{name}_pc", [128, T], F32))
        psx = [ps0, ps1]

        def lb(p, s):
            p.dma(s.dma_start(out=b1T[:].rearrange("p (e m) -> p e m", e=NE),
                              in_=W["moe_b1T"]))
            p.dma(s.dma_start(out=b2m[:], in_=W["moe_b2"]))
        p.ph("sync", lb)

        for ci in range(NCH):
            def lcm(p, s, ci=ci):
                p.dma(s.dma_start(out=cb8[:], in_=comb_dram[ci * 8 * T:(ci + 1) * 8 * T]
                                  .rearrange("(r t) -> r t", r=8)))
            p.ph("sync", lcm)
            for mt in range(KD):
                bank = psx[mt % 2]
                p.ph("tensor", lambda p, t, mt=mt, bank=bank: p.inc(
                    t.matmul(bank[:, 0:T], b2m[:, mt * 128:(mt + 1) * 128], cb8[:],
                             start=True, stop=True)))
                p.ph("vector", lambda p, v, mt=mt, bank=bank: p.inc(
                    v.tensor_copy(contrib[:, mt * T:(mt + 1) * T], bank[:, 0:T])))
            p.ph("sync", lambda p, s, ci=ci: store_chunk(p, s, acc_f, contrib, KD, ci, T))

        for e in range(NE):
            def lw(p, g, e=e):
                p.dma(g.dma_start(out=w1[:].rearrange("p (a m) -> p a m", a=KD),
                                  in_=W["moe_w1"][e].rearrange("(a p) m -> p a m", p=128)))
                p.dma(g.dma_start(out=w2[:].rearrange("p (a m) -> p a m", a=KF),
                                  in_=W["moe_w2"][e].rearrange("(a p) m -> p a m", p=128)))
            p.ph("gpsimd", lw)
            for ci in range(NCH):
                def lx(p, s, ci=ci, e=e):
                    load_chunk(p, s, xb, u2_b, KD, ci, T)
                    p.dma(s.dma_start(out=cbe[:], in_=comb_dram[(ci * 8 + e) * T:(ci * 8 + e + 1) * T]
                                      .unsqueeze(0)))
                p.ph("sync", lx)
                p.ph("tensor", lambda p, t: p.inc(
                    bcast_matmul(t, psc, cbe[0:1, 0:T], T, C.ones_row)))
                p.ph("vector", lambda p, v: p.inc(v.tensor_copy(ceb[:], psc[:, 0:T])))
                for et in range(KF):
                    bank = psx[et % 2]
                    def fmm(p, t, et=et, bank=bank):
                        for kt in range(KD):
                            inst = t.matmul(bank[:, 0:T],
                                            w1[:, kt * FF + et * 128: kt * FF + et * 128 + 128],
                                            xb[:, kt * T:(kt + 1) * T],
                                            start=(kt == 0), stop=(kt == KD - 1))
                        p.inc(inst)
                    p.ph("tensor", fmm)
                    p.ph("scalar", lambda p, A, et=et, e=e, bank=bank: p.inc(
                        A.activation(out=h[:, et * T:(et + 1) * T], in_=bank[:, 0:T],
                                     func=AF.Relu, bias=b1T[:, e * KF + et: e * KF + et + 1])))
                for mt in range(KD):
                    bank = psx[mt % 2]
                    def fmm(p, t, mt=mt, bank=bank):
                        for kt in range(KF):
                            inst = t.matmul(bank[:, 0:T],
                                            w2[:, kt * D + mt * 128: kt * D + mt * 128 + 128],
                                            h[:, kt * T:(kt + 1) * T],
                                            start=(kt == 0), stop=(kt == KF - 1))
                        p.inc(inst)
                    p.ph("tensor", fmm)
                    p.ph("vector", lambda p, v, mt=mt, bank=bank: p.inc(
                        v.tensor_tensor(out=contrib[:, mt * T:(mt + 1) * T], in0=bank[:, 0:T],
                                        in1=ceb[:], op=OP.mult)))
                p.ph("gpsimd", lambda p, g, ci=ci: p.dma(
                    g.dma_start(out=feat_slice(acc_f, KD, ci, T), in_=sb3(contrib[:], KD, T),
                                accum_op=OP.add)))


def stage_add_ln(p, C, src_f, base_f, dst_f, T=512, name="al"):
    nc = p.nc
    NCH = L // T
    with ExitStack() as st:
        a1 = st.enter_context(nc.sbuf_tensor(f"{name}_a", [128, KD * T], F32))
        a2 = st.enter_context(nc.sbuf_tensor(f"{name}_b", [128, KD * T], F32))
        sq = st.enter_context(nc.sbuf_tensor(f"{name}_sq", [128, KD * T], F32))
        stats = st.enter_context(nc.sbuf_tensor(f"{name}_st", [1, 4 * T], F32))
        psA = st.enter_context(nc.psum_tensor(f"{name}_pA", [1, T], F32))
        psB = st.enter_context(nc.psum_tensor(f"{name}_pB", [1, T], F32))
        psR = st.enter_context(nc.psum_tensor(f"{name}_pR", [128, T], F32))
        psM = st.enter_context(nc.psum_tensor(f"{name}_pM", [128, T], F32))
        S = Sched(p, name)
        for ci in range(NCH):
            S.add("sync", lambda s, ci=ci: s.dma_start(
                out=sb3(a1[:], KD, T), in_=feat_slice(src_f, KD, ci, T)),
                waits=([f"fn_{ci - 1}"] if ci > 0 else []), tick="ld", dma=True, event=f"l1_{ci}")
            S.add("sync", lambda s, ci=ci: s.dma_start(
                out=sb3(a2[:], KD, T), in_=feat_slice(base_f, KD, ci, T)),
                waits=([f"fn_{ci - 1}"] if ci > 0 else []), tick="ld", dma=True, event=f"l2_{ci}")
            S.add("scalar", lambda A: A.activation(out=sq[:], in_=a1[:], func=AF.Square),
                  waits=[f"l1_{ci}"] + ([f"stc_{ci - 1}"] if ci > 0 else []),
                  tick="ex", event=f"sqv_{ci}")
            def fsums(t):
                sum_matmuls(p, t, psA, a1[:], KD, T, C.ones_col)
                return sum_matmuls(p, t, psB, sq[:], KD, T, C.ones_col)
            S.add("tensor", fsums, waits=[f"sqv_{ci}"] + ([f"vst_{ci - 1}"] if ci > 0 else []),
                  tick="pe", event=f"sm_{ci}")
            def fstats(v):
                v.tensor_scalar(out=stats[0:1, 0:T], in0=psA[0:1, 0:T], scalar1=1.0 / D,
                                scalar2=None, op0=OP.mult)
                v.tensor_tensor(out=stats[0:1, T:2 * T], in0=stats[0:1, 0:T],
                                in1=stats[0:1, 0:T], op=OP.mult)
                return v.scalar_tensor_tensor(out=stats[0:1, T:2 * T], in0=psB[0:1, 0:T],
                                              scalar=1.0 / D, in1=stats[0:1, T:2 * T],
                                              op0=OP.mult, op1=OP.subtract)
            S.add("vector", fstats, waits=[f"sm_{ci}"], tick="dv", event=f"vst_{ci}")
            S.add("scalar", lambda A: rstd_from_ms(p, A, stats[0:1, 2 * T:3 * T],
                                                   stats[0:1, T:2 * T], 1.0, EPS_LN),
                  waits=[f"vst_{ci}"] + ([f"bb_{ci - 1}"] if ci > 0 else []),
                  tick="ex", event=f"rst_{ci}")
            S.add("vector", lambda v: v.tensor_tensor(out=stats[0:1, 3 * T:4 * T],
                                                      in0=stats[0:1, 0:T],
                                                      in1=stats[0:1, 2 * T:3 * T], op=OP.mult),
                  waits=[f"rst_{ci}"], tick="dv", event=f"mrs_{ci}")
            def fb(t):
                bcast_matmul(t, psR, stats[0:1, 2 * T:3 * T], T, C.ones_row)
                return bcast_matmul(t, psM, stats[0:1, 3 * T:4 * T], T, C.ones_row)
            S.add("tensor", fb, waits=[f"mrs_{ci}"] + ([f"fn_{ci - 1}"] if ci > 0 else []),
                  tick="pe", event=f"bb_{ci}")
            def fnorm(v):
                for a in range(KD):
                    sl = slice(a * T, (a + 1) * T)
                    v.tensor_tensor(out=sq[:, sl], in0=a1[:, sl], in1=psR[:, 0:T], op=OP.mult)
                    v.tensor_tensor(out=sq[:, sl], in0=sq[:, sl], in1=psM[:, 0:T], op=OP.subtract)
                    inst = v.tensor_tensor(out=sq[:, sl], in0=sq[:, sl], in1=a2[:, sl], op=OP.add)
                return inst
            S.add("vector", fnorm, waits=[f"bb_{ci}", f"l2_{ci}"] +
                  ([f"stc_{ci - 1}"] if ci > 0 else []), tick="dv", event=f"fn_{ci}")
            S.add("sync", lambda s, ci=ci: s.dma_start(
                out=feat_slice(dst_f, KD, ci, T), in_=sb3(sq[:], KD, T)),
                waits=[f"fn_{ci}"], tick="ld", dma=True, event=f"stc_{ci}")
        S.run()


def stage_attn_qkv(p, C, W, x2_f, qkv_dram, vtm_dram, T=512, name="aq"):
    nc = p.nc
    NCH = L // T
    KQ = 3 * D // 128
    with ExitStack() as st:
        wa = st.enter_context(nc.sbuf_tensor(f"{name}_wa", [128, KD * 3 * D], BF16))
        baT = st.enter_context(nc.sbuf_tensor(f"{name}_ba", [128, KQ], F32))
        x = st.enter_context(nc.sbuf_tensor(f"{name}_x", [128, KD * T], F32))
        xb = st.enter_context(nc.sbuf_tensor(f"{name}_xb", [128, KD * T], BF16))
        sq = st.enter_context(nc.sbuf_tensor(f"{name}_sq", [128, KD * T], F32))
        stats = st.enter_context(nc.sbuf_tensor(f"{name}_st", [1, 4 * T], F32))
        qkv = st.enter_context(nc.sbuf_tensor(f"{name}_q", [128, KQ * T], BF16))
        vt = st.enter_context(nc.sbuf_tensor(f"{name}_vt", [128, 2 * 8 * 128], BF16))
        psA = st.enter_context(nc.psum_tensor(f"{name}_pA", [1, T], F32))
        psB = st.enter_context(nc.psum_tensor(f"{name}_pB", [1, T], F32))
        psR = st.enter_context(nc.psum_tensor(f"{name}_pR", [128, T], F32))
        psM = st.enter_context(nc.psum_tensor(f"{name}_pM", [128, T], F32))
        ps0 = st.enter_context(nc.psum_tensor(f"{name}_p0", [128, T], F32))
        ps1 = st.enter_context(nc.psum_tensor(f"{name}_p1", [128, T], F32))
        psT = [st.enter_context(nc.psum_tensor(f"{name}_pT{i}", [128, 128], BF16)) for i in range(2)]
        psx = [ps0, ps1]

        S = Sched(p, name)
        S.add("gpsimd", lambda g: g.dma_start(out=wa[:].rearrange("p (a m) -> p a m", a=KD),
                                              in_=W["W_attn"].rearrange("(a p) m -> p a m", p=128)),
              tick="wl", dma=True, event="wa")
        S.add("sync", lambda s: s.dma_start(out=baT[:], in_=W["battnT"]),
              tick="ld", dma=True, event="ba")

        for ci in range(NCH):
            S.add("sync", lambda s, ci=ci: s.dma_start(
                out=sb3(x[:], KD, T), in_=feat_slice(x2_f, KD, ci, T)),
                waits=([f"fn_{ci - 1}"] if ci > 0 else []), tick="ld", dma=True, event=f"xl_{ci}")
            S.add("scalar", lambda A: A.activation(out=sq[:], in_=x[:], func=AF.Square),
                  waits=[f"xl_{ci}"] + ([f"fn_{ci - 1}"] if ci > 0 else []),
                  tick="ex", event=f"sqv_{ci}")
            def fsums(t):
                sum_matmuls(p, t, psA, x[:], KD, T, C.ones_col)
                return sum_matmuls(p, t, psB, sq[:], KD, T, C.ones_col)
            S.add("tensor", fsums, waits=[f"sqv_{ci}"] + ([f"vst_{ci - 1}"] if ci > 0 else []),
                  tick="pe", event=f"sm_{ci}")
            def fstats(v):
                v.tensor_scalar(out=stats[0:1, 0:T], in0=psA[0:1, 0:T], scalar1=1.0 / D,
                                scalar2=None, op0=OP.mult)
                v.tensor_tensor(out=stats[0:1, T:2 * T], in0=stats[0:1, 0:T],
                                in1=stats[0:1, 0:T], op=OP.mult)
                return v.scalar_tensor_tensor(out=stats[0:1, T:2 * T], in0=psB[0:1, 0:T],
                                              scalar=1.0 / D, in1=stats[0:1, T:2 * T],
                                              op0=OP.mult, op1=OP.subtract)
            S.add("vector", fstats, waits=[f"sm_{ci}"], tick="dv", event=f"vst_{ci}")
            S.add("scalar", lambda A: rstd_from_ms(p, A, stats[0:1, 2 * T:3 * T],
                                                   stats[0:1, T:2 * T], 1.0, EPS_LN),
                  waits=[f"vst_{ci}"] + ([f"bb_{ci - 1}"] if ci > 0 else []),
                  tick="ex", event=f"rst_{ci}")
            S.add("vector", lambda v: v.tensor_tensor(out=stats[0:1, 3 * T:4 * T],
                                                      in0=stats[0:1, 0:T],
                                                      in1=stats[0:1, 2 * T:3 * T], op=OP.mult),
                  waits=[f"rst_{ci}"], tick="dv", event=f"mrs_{ci}")
            def fb(t):
                bcast_matmul(t, psR, stats[0:1, 2 * T:3 * T], T, C.ones_row)
                return bcast_matmul(t, psM, stats[0:1, 3 * T:4 * T], T, C.ones_row)
            S.add("tensor", fb, waits=[f"mrs_{ci}"] + ([f"fn_{ci - 1}"] if ci > 0 else []),
                  tick="pe", event=f"bb_{ci}")
            def fnorm(v):
                for a in range(KD):
                    sl = slice(a * T, (a + 1) * T)
                    v.tensor_tensor(out=sq[:, sl], in0=x[:, sl], in1=psR[:, 0:T], op=OP.mult)
                    v.tensor_tensor(out=sq[:, sl], in0=sq[:, sl], in1=psM[:, 0:T], op=OP.subtract)
                    inst = v.tensor_copy(xb[:, sl], sq[:, sl])
                return inst
            S.add("vector", fnorm, waits=[f"bb_{ci}"] + ([f"qm_{ci - 1}_{KQ - 1}"] if ci > 0 else []),
                  tick="dv", event=f"fn_{ci}")
            for mt in range(KQ):
                bank = psx[mt % 2]
                bw = [f"fn_{ci}", "wa"]
                if mt >= 2:
                    bw.append(f"qb_{ci}_{mt - 2}")
                elif ci > 0:
                    bw.append(f"qb_{ci - 1}_{22 + mt}")
                def fmm(t, mt=mt, bank=bank):
                    for kt in range(KD):
                        inst = t.matmul(bank[:, 0:T], wa[:, kt * 3 * D + mt * 128: kt * 3 * D + mt * 128 + 128],
                                        xb[:, kt * T:(kt + 1) * T], start=(kt == 0), stop=(kt == KD - 1))
                    return inst
                S.add("tensor", fmm, waits=bw, tick="pe", event=f"qm_{ci}_{mt}")
                ew = [f"qm_{ci}_{mt}", "ba"]
                if ci > 0:
                    ew.append(f"qst_{ci - 1}")
                    if mt >= 16:
                        ew.append(f"tcp_{ci - 1}_{24 + (mt - 16)}")
                S.add("scalar", lambda A, mt=mt, bank=bank: A.activation(
                    out=qkv[:, mt * T:(mt + 1) * T], in_=bank[:, 0:T],
                    func=AF.Identity, bias=baT[:, mt:mt + 1]),
                    waits=ew, tick="ex", event=f"qb_{ci}_{mt}")
            S.add("sync", lambda s, ci=ci: s.dma_start(
                out=qkv_dram.rearrange("(a p) t -> p a t", p=128)[:, :, ci * T:(ci + 1) * T],
                in_=sb3(qkv[:], KQ, T)),
                waits=[f"qb_{ci}_{KQ - 1}"], tick="ld", dma=True, event=f"qst_{ci}")
            for tsub in range(T // 128):
                vof = (tsub % 2) * 8 * 128
                for mt in range(8):
                    k = tsub * 8 + mt
                    b = k % 2
                    tw = [f"qb_{ci}_{16 + mt}"]
                    if k >= 2:
                        tw.append(f"tcp_{ci}_{k - 2}")
                    elif ci > 0:
                        tw.append(f"tcp_{ci - 1}_{30 + k}")
                    S.add("tensor", lambda t, mt=mt, tsub=tsub, b=b: t.transpose(
                        psT[b][:], qkv[:, (16 + mt) * T + tsub * 128:
                                        (16 + mt) * T + tsub * 128 + 128], C.ident_b[:]),
                        waits=tw, tick="pe", event=f"tp_{ci}_{k}")
                    cw = [f"tp_{ci}_{k}"]
                    if mt == 0:
                        cidx = ci * 4 + tsub - 2
                        if cidx >= 0:
                            cw.append(f"vtst_{cidx // 4}_{cidx % 4}")
                    S.add("vector", lambda v, mt=mt, b=b, vof=vof: v.tensor_copy(
                        vt[:, vof + mt * 128: vof + (mt + 1) * 128], psT[b][:]),
                        waits=cw, tick="dv", event=f"tcp_{ci}_{k}")
                S.add("sync", lambda s, ci=ci, tsub=tsub, vof=vof: s.dma_start(
                    out=vtm_dram[ci * T + tsub * 128: ci * T + tsub * 128 + 128, :],
                    in_=vt[:, vof: vof + 8 * 128]),
                    waits=[f"tcp_{ci}_{tsub * 8 + 7}"], tick="ld", dma=True,
                    event=f"vtst_{ci}_{tsub}")
        S.run()


def stage_attn_core(p, C, W, qkv_dram, vtm_dram, x2_f, dst_f, T=512, name="ac"):
    nc = p.nc
    NCH = L // T
    NTK = L // 128
    scale = 1.0 / 8.0
    with ExitStack() as st:
        kT = st.enter_context(nc.sbuf_tensor(f"{name}_k", [64, L], BF16))
        qT = st.enter_context(nc.sbuf_tensor(f"{name}_q", [64, L], BF16))
        vtm = st.enter_context(nc.sbuf_tensor(f"{name}_v", [128, NTK * 65], BF16))
        esb = st.enter_context(nc.sbuf_tensor(f"{name}_e", [128, 2 * T], BF16))
        osb = st.enter_context(nc.sbuf_tensor(f"{name}_o", [64, T], F32))
        rden = st.enter_context(nc.sbuf_tensor(f"{name}_r", [1, T], F32))
        att = st.enter_context(nc.sbuf_tensor(f"{name}_at", [128, KD * L], BF16))

        S = Sched(p, name)
        with ExitStack() as ps_head:
            psS = [ps_head.enter_context(nc.psum_tensor(f"{name}_pS{i}", [128, T], F32)) for i in range(2)]
            psO = [ps_head.enter_context(nc.psum_tensor(f"{name}_pO{i}", [65, T], F32)) for i in range(2)]
            psD = ps_head.enter_context(nc.psum_tensor(f"{name}_pD", [64, T], F32))
            prev_ex = None
            prev_av = None
            prev_fn = None
            prev_fn2 = {}
            for h_ in range(NH):
                hw = [f"sl_{h_ - 1}"] if h_ > 0 else []
                S.add("sync", lambda s, h_=h_: s.dma_start(
                    out=qT[:], in_=qkv_dram[h_ * 64:(h_ + 1) * 64, :]),
                    waits=hw, tick="ld", dma=True)
                S.add("sync", lambda s, h_=h_: s.dma_start(
                    out=kT[:], in_=qkv_dram[D + h_ * 64: D + (h_ + 1) * 64, :]),
                    tick="ld", dma=True)
                vw = [f"avl_{h_ - 1}"] if h_ > 0 else []
                S.add("vector", lambda v: v.memset(vtm[:], 1.0), waits=vw,
                      tick="dv", event=f"vm_{h_}")
                S.add("sync", lambda s, h_=h_: s.dma_start(
                    out=vtm[:].rearrange("p (a d) -> p a d", d=65)[:, :, 0:64],
                    in_=vtm_dram.rearrange("(a p) d -> p a d", p=128)[:, :, h_ * 64:(h_ + 1) * 64]),
                    waits=[f"vm_{h_}"], tick="ld", dma=True, event=f"ldv_{h_}")
                for ci in range(NCH):
                    ntk = (ci + 1) * (T // 128)
                    unit_ex = []
                    unit_av = []
                    for tk in range(ntk + 1):
                        if tk < ntk:
                            sw = [f"ldv_{h_}"]
                            if tk >= 2:
                                sw.append(f"ex_{h_}_{ci}_{tk - 2}")
                            elif prev_ex is not None:
                                sw.append(prev_ex)
                            S.add("tensor", lambda t, h_=h_, ci=ci, tk=tk: t.matmul(
                                psS[tk % 2][:, 0:T], kT[:, tk * 128:(tk + 1) * 128],
                                qT[:, ci * T:(ci + 1) * T], start=True, stop=True),
                                waits=sw, tick="pe",
                                event=(f"sl_{h_}" if ci == NCH - 1 and tk == ntk - 1 else f"s_{h_}_{ci}_{tk}"))
                        if tk >= 1:
                            tkp = tk - 1
                            diag = tkp - ci * (T // 128)
                            sev = (f"sl_{h_}" if ci == NCH - 1 and tkp == ntk - 1 else f"s_{h_}_{ci}_{tkp}")
                            ew = [sev]
                            if tkp >= 2:
                                ew.append(unit_av[tkp - 2])
                            elif prev_av is not None:
                                ew.append(prev_av)
                            S.add("scalar", lambda A, tkp=tkp: A.activation(
                                out=esb[:, (tkp % 2) * T:(tkp % 2 + 1) * T], in_=psS[tkp % 2][:, 0:T],
                                func=AF.Exp, scale=scale),
                                waits=ew, tick="ex", event=f"ex_{h_}_{ci}_{tkp}")
                            unit_ex.append(f"ex_{h_}_{ci}_{tkp}")
                            mev = f"ex_{h_}_{ci}_{tkp}"
                            if diag >= 0:
                                S.add("vector", lambda v, tkp=tkp, diag=diag: v.tensor_tensor(
                                    out=esb[:, (tkp % 2) * T:(tkp % 2 + 1) * T],
                                    in0=esb[:, (tkp % 2) * T:(tkp % 2 + 1) * T],
                                    in1=C.masks[:, diag * 512: diag * 512 + T], op=OP.mult),
                                    waits=[mev], tick="dv", event=f"mk_{h_}_{ci}_{tkp}")
                                mev = f"mk_{h_}_{ci}_{tkp}"
                            aw = [mev, f"ldv_{h_}"]
                            if tkp == 0 and (ci % 2) in prev_fn2:
                                aw.append(prev_fn2[ci % 2])
                            avev = (f"avl_{h_}" if ci == NCH - 1 and tkp == ntk - 1 else f"av_{h_}_{ci}_{tkp}")
                            S.add("tensor", lambda t, h_=h_, ci=ci, tkp=tkp, ntk=ntk: t.matmul(
                                psO[ci % 2][:, 0:T], vtm[:, tkp * 65:(tkp + 1) * 65],
                                esb[:, (tkp % 2) * T:(tkp % 2 + 1) * T],
                                start=(tkp == 0), stop=(tkp == ntk - 1)),
                                waits=aw, tick="pe", event=avev)
                            unit_av.append(avev)
                    avev = unit_av[-1]
                    S.add("vector", lambda v, ci=ci: v.reciprocal(out=rden[:], in_=psO[ci % 2][64:65, 0:T]),
                          waits=[avev], tick="dv", event=f"rc_{h_}_{ci}")
                    dbw = [f"rc_{h_}_{ci}"] + ([prev_fn] if prev_fn else [])
                    S.add("tensor", lambda t: bcast_matmul(t, psD, rden[0:1, 0:T], T, C.ones_row, m=64),
                          waits=dbw, tick="pe", event=f"db_{h_}_{ci}")
                    obw = [avev] + ([prev_fn] if prev_fn else [])
                    S.add("scalar", lambda A, ci=ci: A.copy(osb[:], psO[ci % 2][0:64, 0:T]),
                          waits=obw, tick="ex", event=f"ob_{h_}_{ci}")
                    def fnr(v, h_=h_, ci=ci):
                        row0 = (h_ % 2) * 64
                        return v.tensor_tensor(
                            out=att[row0:row0 + 64, (h_ // 2) * L + ci * T: (h_ // 2) * L + ci * T + T],
                            in0=osb[:], in1=psD[:, 0:T], op=OP.mult)
                    S.add("vector", fnr, waits=[f"db_{h_}_{ci}", f"ob_{h_}_{ci}"],
                          tick="dv", event=f"fn_{h_}_{ci}")
                    prev_fn2[ci % 2] = f"fn_{h_}_{ci}"
                    prev_fn = f"fn_{h_}_{ci}"
                    prev_ex = unit_ex[-1]
                    prev_av = unit_av[-1]
            S.events["fn_prev"] = S.events[prev_fn]
        wp = st.enter_context(nc.sbuf_tensor(f"{name}_wp", [128, KD * D], BF16))
        bpT = st.enter_context(nc.sbuf_tensor(f"{name}_bp", [128, 8], F32))
        br = st.enter_context(nc.sbuf_tensor(f"{name}_br", [128, KD * T], F32))
        x2c = st.enter_context(nc.sbuf_tensor(f"{name}_x2", [128, KD * T], F32))
        sq = st.enter_context(nc.sbuf_tensor(f"{name}_sq", [128, KD * T], F32))
        stats = st.enter_context(nc.sbuf_tensor(f"{name}_st", [1, 4 * T], F32))
        with ExitStack() as ps2:
            pp = [ps2.enter_context(nc.psum_tensor(f"{name}_pp{i}", [128, T], F32)) for i in range(2)]
            psA = ps2.enter_context(nc.psum_tensor(f"{name}_pA", [1, T], F32))
            psB = ps2.enter_context(nc.psum_tensor(f"{name}_pB", [1, T], F32))
            psR = ps2.enter_context(nc.psum_tensor(f"{name}_pR", [128, T], F32))
            psM = ps2.enter_context(nc.psum_tensor(f"{name}_pM", [128, T], F32))
            S.add("gpsimd", lambda g: g.dma_start(out=wp[:].rearrange("p (a m) -> p a m", a=KD),
                                                  in_=W["W_proj"].rearrange("(a p) m -> p a m", p=128)),
                  tick="wl", dma=True, event="wp")
            S.add("sync", lambda s: s.dma_start(out=bpT[:], in_=W["bprojT"]),
                  tick="ld", dma=True, event="bp")
            lastb = [None, None]
            for ci in range(NCH):
                S.add("sync", lambda s, ci=ci: s.dma_start(
                    out=sb3(x2c[:], KD, T), in_=feat_slice(x2_f, KD, ci, T)),
                    waits=([f"fn3_{ci - 1}"] if ci > 0 else []), tick="ld", dma=True, event=f"x2_{ci}")
                for mt in range(KD):
                    b = mt % 2
                    bw = [f"avl_{NH - 1}", "fn_prev", "wp"] + ([lastb[b]] if lastb[b] else [])
                    def fmm(t, mt=mt, ci=ci, b=b):
                        for kt in range(KD):
                            inst = t.matmul(pp[b][:, 0:T], wp[:, kt * D + mt * 128: kt * D + mt * 128 + 128],
                                            att[:, kt * L + ci * T: kt * L + ci * T + T],
                                            start=(kt == 0), stop=(kt == KD - 1))
                        return inst
                    S.add("tensor", fmm, waits=bw, tick="pe", event=f"pj_{ci}_{mt}")
                    S.add("scalar", lambda A, mt=mt, b=b: A.activation(
                        out=br[:, mt * T:(mt + 1) * T], in_=pp[b][:, 0:T],
                        func=AF.Identity, bias=bpT[:, mt:mt + 1]),
                        waits=[f"pj_{ci}_{mt}", "bp"] + ([f"fn3_{ci - 1}"] if ci > 0 and mt == 0 else []),
                        tick="ex", event=f"bre_{ci}_{mt}")
                    lastb[b] = f"bre_{ci}_{mt}"
                S.add("scalar", lambda A: A.activation(out=sq[:], in_=br[:], func=AF.Square),
                      waits=[f"bre_{ci}_{KD - 1}"] + ([f"st3_{ci - 1}"] if ci > 0 else []),
                      tick="ex", event=f"sq2_{ci}")
                def fsums(t):
                    sum_matmuls(p, t, psA, br[:], KD, T, C.ones_col)
                    return sum_matmuls(p, t, psB, sq[:], KD, T, C.ones_col)
                S.add("tensor", fsums, waits=[f"sq2_{ci}"] + ([f"rst_{ci - 1}"] if ci > 0 else []),
                      tick="pe", event=f"sm_{ci}")
                def fstats(v):
                    v.tensor_scalar(out=stats[0:1, 0:T], in0=psA[0:1, 0:T], scalar1=1.0 / D,
                                    scalar2=None, op0=OP.mult)
                    v.tensor_tensor(out=stats[0:1, T:2 * T], in0=stats[0:1, 0:T],
                                    in1=stats[0:1, 0:T], op=OP.mult)
                    return v.scalar_tensor_tensor(out=stats[0:1, T:2 * T], in0=psB[0:1, 0:T],
                                                  scalar=1.0 / D, in1=stats[0:1, T:2 * T],
                                                  op0=OP.mult, op1=OP.subtract)
                S.add("vector", fstats, waits=[f"sm_{ci}"], tick="dv", event=f"vst_{ci}")
                S.add("scalar", lambda A: rstd_from_ms(p, A, stats[0:1, 2 * T:3 * T],
                                                       stats[0:1, T:2 * T], 1.0, EPS_LN),
                      waits=[f"vst_{ci}"], tick="ex", event=f"rst_{ci}")
                S.add("vector", lambda v: v.tensor_tensor(out=stats[0:1, 3 * T:4 * T],
                                                          in0=stats[0:1, 0:T],
                                                          in1=stats[0:1, 2 * T:3 * T], op=OP.mult),
                      waits=[f"rst_{ci}"], tick="dv", event=f"mrs_{ci}")
                def fbb(t):
                    bcast_matmul(t, psR, stats[0:1, 2 * T:3 * T], T, C.ones_row)
                    return bcast_matmul(t, psM, stats[0:1, 3 * T:4 * T], T, C.ones_row)
                S.add("tensor", fbb, waits=[f"mrs_{ci}"] + ([f"fn3_{ci - 1}"] if ci > 0 else []),
                      tick="pe", event=f"bb_{ci}")
                def fnorm(v):
                    for a in range(KD):
                        sl = slice(a * T, (a + 1) * T)
                        v.tensor_tensor(out=sq[:, sl], in0=br[:, sl], in1=psR[:, 0:T], op=OP.mult)
                        v.tensor_tensor(out=sq[:, sl], in0=sq[:, sl], in1=psM[:, 0:T], op=OP.subtract)
                        inst = v.tensor_tensor(out=sq[:, sl], in0=sq[:, sl], in1=x2c[:, sl], op=OP.add)
                    return inst
                S.add("vector", fnorm, waits=[f"bb_{ci}", f"x2_{ci}"], tick="dv", event=f"fn3_{ci}")
                S.add("gpsimd", lambda g, ci=ci: g.dma_start(
                    out=feat_slice(dst_f, KD, ci, T), in_=sb3(sq[:], KD, T)),
                    waits=[f"fn3_{ci}"], tick="wl", dma=True, event=f"st3_{ci}")
        S.run()

def stage_ffn(p, C, W, x3_f, y_out, T=256, name="ff"):
    nc = p.nc
    NCH = L // T
    with ExitStack() as st:
        w1 = st.enter_context(nc.sbuf_tensor(f"{name}_w1", [128, KD * FF], BF16))
        w2 = st.enter_context(nc.sbuf_tensor(f"{name}_w2", [128, KF * D], BF16))
        b1T = st.enter_context(nc.sbuf_tensor(f"{name}_b1", [128, KF], F32))
        hb2 = st.enter_context(nc.sbuf_tensor(f"{name}_b2", [128, 8], F32))
        gamT = st.enter_context(nc.sbuf_tensor(f"{name}_g", [128, 8], F32))
        betT = st.enter_context(nc.sbuf_tensor(f"{name}_be", [128, 8], F32))

        S = Sched(p, name)
        S.add("sync", lambda s: s.dma_start(out=b1T[:], in_=W["fc1bT"]), tick="ld", dma=True)
        S.add("sync", lambda s: s.dma_start(out=hb2[:], in_=W["fc2bT"]), tick="ld", dma=True)
        S.add("sync", lambda s: s.dma_start(out=gamT[:], in_=W["gamT"]), tick="ld", dma=True)
        S.add("sync", lambda s: s.dma_start(out=betT[:], in_=W["betT"]),
              tick="ld", dma=True, event="consts")
        S.add("vector", lambda v: v.tensor_scalar(out=hb2[:], in0=hb2[:], scalar1=0.5,
                                                  scalar2=None, op0=OP.mult),
              waits=["consts"], tick="dv", event="hb2")

        with ExitStack() as dq:
            qf = dq.enter_context(nc.sbuf_tensor(f"{name}_qf", [128, 2 * FF], F32))
            srow = dq.enter_context(nc.sbuf_tensor(f"{name}_sr", [1, 2 * FF], F32))
            psc = dq.enter_context(nc.psum_tensor(f"{name}_pc", [128, 512], F32))
            psc2 = dq.enter_context(nc.psum_tensor(f"{name}_pc2", [128, 512], F32))
            units = [("fc1_q", "fc1_scales", FF, 8) , ("fc2_q", "fc2_scales", D, 32)]
            gidx = 0
            for (qk, sk, N, NG) in units:
                for g in range(NG):
                    u = gidx; gidx += 1
                    par = u % 2
                    lw = [f"dqs_{u - 2}"] if u >= 2 else []
                    S.add("gpsimd", lambda gp, qk=qk, g=g, par=par, N=N: gp.dma_start(
                        out=qf[:, par * FF: par * FF + N], in_=W[qk][g * 128:(g + 1) * 128, :]),
                        waits=lw, tick="wl", dma=True, event=f"dqq_{u}")
                    S.add("sync", lambda s, sk=sk, g=g, par=par, N=N: s.dma_start(
                        out=srow[0:1, par * FF: par * FF + N], in_=W[sk][g:g + 1, :]),
                        waits=lw, tick="ld", dma=True, event=f"dqr_{u}")
                    for ns in range(N // 512):
                        bank = psc if ns % 2 == 0 else psc2
                        bw = [f"dqr_{u}"]
                        if ns >= 2:
                            bw.append(f"dqs_{u}_{ns - 2}")
                        elif u >= 1:
                            bw.append(f"dqs_{u - 1}")
                        S.add("tensor", lambda t, par=par, ns=ns, bank=bank: bcast_matmul(
                            t, bank, srow[0:1, par * FF + ns * 512: par * FF + (ns + 1) * 512],
                            512, C.ones_row),
                            waits=bw, tick="pe", event=f"dqb_{u}_{ns}")
                        wdst = w1 if qk == "fc1_q" else w2
                        S.add("vector", lambda v, qk=qk, g=g, ns=ns, par=par, N=N, wdst=wdst, bank=bank:
                              v.scalar_tensor_tensor(
                                  out=wdst[:, g * N + ns * 512: g * N + (ns + 1) * 512],
                                  in0=qf[:, par * FF + ns * 512: par * FF + (ns + 1) * 512],
                                  scalar=-128.0, in1=bank[:, 0:512], op0=OP.add, op1=OP.mult),
                              waits=[f"dqb_{u}_{ns}", f"dqq_{u}"], tick="dv",
                              event=(f"dqs_{u}_{ns}" if ns < N // 512 - 1 else f"dqs_{u}"))
            S.run()

        x = st.enter_context(nc.sbuf_tensor(f"{name}_x", [128, 2 * KD * T], F32))
        u4 = st.enter_context(nc.sbuf_tensor(f"{name}_u4", [128, KD * T], F32))
        hq = st.enter_context(nc.sbuf_tensor(f"{name}_hq", [128, KD * T], BF16))
        hs = st.enter_context(nc.sbuf_tensor(f"{name}_hs", [128, KF * T], BF16))
        vv = st.enter_context(nc.sbuf_tensor(f"{name}_vv", [128, KD * T], F32))
        sq = st.enter_context(nc.sbuf_tensor(f"{name}_sq", [128, KD * T], F32))
        stats = st.enter_context(nc.sbuf_tensor(f"{name}_st", [1, 4 * T], F32))
        S = Sched(p, name + "c")
        with ExitStack() as ps2:
            pp = [ps2.enter_context(nc.psum_tensor(f"{name}_pp{i}", [128, T], F32)) for i in range(2)]
            psA = ps2.enter_context(nc.psum_tensor(f"{name}_pA", [1, T], F32))
            psB = ps2.enter_context(nc.psum_tensor(f"{name}_pB", [1, T], F32))
            psR = ps2.enter_context(nc.psum_tensor(f"{name}_pR", [128, T], F32))
            psM = ps2.enter_context(nc.psum_tensor(f"{name}_pM", [128, T], F32))

            lnctr = [0]
            def ln_chain(ci, key, src, deps):
                k = lnctr[0]; lnctr[0] += 1
                sqw = list(deps) + ([f"lnsum_{k - 1}"] if k > 0 else [])
                S.add("scalar", lambda A, src=src: A.activation(out=sq[:], in_=src[:] if not isinstance(src, tuple) else x[:, src[1]:src[2]], func=AF.Square),
                      waits=sqw, tick="ex", event=f"lnsq_{k}")
                def fsums(t, src=src):
                    ssrc = src[:] if not isinstance(src, tuple) else x[:, src[1]:src[2]]
                    sum_matmuls(p, t, psA, ssrc, KD, T, C.ones_col)
                    return sum_matmuls(p, t, psB, sq[:], KD, T, C.ones_col)
                S.add("tensor", fsums, waits=[f"lnsq_{k}"] + ([f"lnrst_{k - 1}"] if k > 0 else []),
                      tick="pe", event=f"lnsum_{k}")
                def fstats(v):
                    v.tensor_scalar(out=stats[0:1, 0:T], in0=psA[0:1, 0:T], scalar1=1.0 / D,
                                    scalar2=None, op0=OP.mult)
                    v.tensor_tensor(out=stats[0:1, T:2 * T], in0=stats[0:1, 0:T],
                                    in1=stats[0:1, 0:T], op=OP.mult)
                    return v.scalar_tensor_tensor(out=stats[0:1, T:2 * T], in0=psB[0:1, 0:T],
                                                  scalar=1.0 / D, in1=stats[0:1, T:2 * T],
                                                  op0=OP.mult, op1=OP.subtract)
                S.add("vector", fstats, waits=[f"lnsum_{k}"], tick="dv", event=f"lnst_{k}")
                S.add("scalar", lambda A: rstd_from_ms(p, A, stats[0:1, 2 * T:3 * T],
                                                       stats[0:1, T:2 * T], 1.0, EPS_LN),
                      waits=[f"lnst_{k}"], tick="ex", event=f"lnrst_{k}")
                S.add("vector", lambda v: v.tensor_tensor(out=stats[0:1, 3 * T:4 * T],
                                                          in0=stats[0:1, 0:T],
                                                          in1=stats[0:1, 2 * T:3 * T], op=OP.mult),
                      waits=[f"lnrst_{k}"], tick="dv", event=f"lnmr_{k}")
                def fbb(t):
                    bcast_matmul(t, psR, stats[0:1, 2 * T:3 * T], T, C.ones_row)
                    return bcast_matmul(t, psM, stats[0:1, 3 * T:4 * T], T, C.ones_row)
                S.add("tensor", fbb, waits=[f"lnmr_{k}", f"lnbusy_{k}"] if f"lnbusy_{k}" in S.events
                      else [f"lnmr_{k}"], tick="pe", event=f"lnbb_{k}")
                return f"lnbb_{k}", k
            lastb = [None, None]
            for ci in range(NCH):
                xo = (ci % 2) * KD * T
                S.add("sync", lambda s, ci=ci, xo=xo: s.dma_start(
                    out=sb3(x[:, xo:xo + KD * T], KD, T), in_=feat_slice(x3_f, KD, ci, T)),
                    waits=([f"yst_{ci - 2}"] if ci >= 2 else []), tick="ld", dma=True, event=f"x_{ci}")
                e1, k1 = ln_chain(ci, "u4", ("x", xo, xo + KD * T), [f"x_{ci}"] +
                                  ([f"u4n_{ci - 1}", f"yst_{ci - 1}"] if ci > 0 else []))
                def fu4(v, xo=xo):
                    for a in range(KD):
                        sl = slice(a * T, (a + 1) * T)
                        v.tensor_tensor(out=u4[:, sl], in0=x[:, xo + a * T: xo + (a + 1) * T],
                                        in1=psR[:, 0:T], op=OP.mult)
                        inst = v.tensor_tensor(out=u4[:, sl], in0=u4[:, sl], in1=psM[:, 0:T],
                                               op=OP.subtract)
                    return inst
                u4w = [e1] + ([f"vvn_{ci - 1}"] if ci > 0 else [])
                S.add("vector", fu4, waits=u4w, tick="dv", event=f"u4n_{ci}")
                e2, k2 = ln_chain(ci, "hq", u4, [f"u4n_{ci}"])
                def fhq(v):
                    for a in range(KD):
                        sl = slice(a * T, (a + 1) * T)
                        v.tensor_tensor(out=sq[:, sl], in0=u4[:, sl], in1=psR[:, 0:T], op=OP.mult)
                        v.tensor_tensor(out=sq[:, sl], in0=sq[:, sl], in1=psM[:, 0:T], op=OP.subtract)
                        inst = v.tensor_scalar(out=hq[:, sl], in0=sq[:, sl], scalar1=gamT[:, a:a + 1],
                                               scalar2=betT[:, a:a + 1], op0=OP.mult, op1=OP.add)
                    return inst
                hqw = [e2] + ([f"fc1_{ci - 1}_{KF - 1}"] if ci > 0 else [])
                S.add("vector", fhq, waits=hqw, tick="dv", event=f"hq_{ci}")
                S.events[f"lnbusy_{k2 + 1}"] = S.events[f"hq_{ci}"]
                for et in range(KF):
                    b = et % 2
                    bw = [f"hq_{ci}", ("dqwall", None)] if False else [f"hq_{ci}"]
                    if lastb[b]:
                        bw.append(lastb[b])
                    def fmm(t, et=et, b=b):
                        for kt in range(KD):
                            inst = t.matmul(pp[b][:, 0:T], w1[:, kt * FF + et * 128: kt * FF + et * 128 + 128],
                                            hq[:, kt * T:(kt + 1) * T], start=(kt == 0), stop=(kt == KD - 1))
                        return inst
                    S.add("tensor", fmm, waits=bw, tick="pe", event=f"fc1_{ci}_{et}")
                    ew = [f"fc1_{ci}_{et}"] + ([f"fc2_{ci - 1}_{KD - 1}"] if ci > 0 and et == 0 else [])
                    S.add("scalar", lambda A, et=et, b=b: A.activation(
                        out=hs[:, et * T:(et + 1) * T], in_=pp[b][:, 0:T],
                        func=AF.Silu, bias=b1T[:, et:et + 1]),
                        waits=ew, tick="ex", event=f"hse_{ci}_{et}")
                    lastb[b] = f"hse_{ci}_{et}"
                for mt in range(KD):
                    b = mt % 2
                    bw = [f"hse_{ci}_{KF - 1}", lastb[b]]
                    def fmm(t, mt=mt, b=b):
                        for kt in range(KF):
                            inst = t.matmul(pp[b][:, 0:T], w2[:, kt * D + mt * 128: kt * D + mt * 128 + 128],
                                            hs[:, kt * T:(kt + 1) * T], start=(kt == 0), stop=(kt == KF - 1))
                        return inst
                    S.add("tensor", fmm, waits=bw, tick="pe", event=f"fc2_{ci}_{mt}")
                    def fv(v, mt=mt, b=b):
                        sl = slice(mt * T, (mt + 1) * T)
                        v.scalar_tensor_tensor(out=vv[:, sl], in0=pp[b][:, 0:T], scalar=0.5,
                                               in1=u4[:, sl], op0=OP.mult, op1=OP.add)
                        return v.tensor_scalar(out=vv[:, sl], in0=vv[:, sl], scalar1=hb2[:, mt:mt + 1],
                                               scalar2=None, op0=OP.add)
                    vw = [f"fc2_{ci}_{mt}"] + ([f"vvn_{ci - 1}"] if ci > 0 and mt == 0 else [])
                    S.add("vector", fv, waits=vw, tick="dv", event=f"vve_{ci}_{mt}")
                    lastb[b] = f"vve_{ci}_{mt}"
                e3, k3 = ln_chain(ci, "y", vv, [f"vve_{ci}_{KD - 1}"])
                def fy(v, xo=xo):
                    for a in range(KD):
                        sl = slice(a * T, (a + 1) * T)
                        v.tensor_tensor(out=sq[:, sl], in0=vv[:, sl], in1=psR[:, 0:T], op=OP.mult)
                        v.tensor_tensor(out=sq[:, sl], in0=sq[:, sl], in1=psM[:, 0:T], op=OP.subtract)
                        inst = v.tensor_tensor(out=sq[:, sl], in0=sq[:, sl],
                                               in1=x[:, xo + a * T: xo + (a + 1) * T], op=OP.add)
                    return inst
                S.add("vector", fy, waits=[e3] + ([f"yst_{ci - 1}"] if ci > 0 else []),
                      tick="dv", event=f"vvn_{ci}")
                S.add("gpsimd", lambda g, ci=ci: g.dma_start(
                    out=feat_slice(y_out, KD, ci, T), in_=sb3(sq[:], KD, T)),
                    waits=[f"vvn_{ci}"], tick="wl", dma=True, event=f"yst_{ci}")
            S.run()

def declare_inputs(nc, shapes):
    aps = {}
    for name, (shape, dt) in shapes.items():
        aps[name] = nc.dram_tensor(name, list(shape), dt, kind="ExternalInput").ap()
    return aps


def input_shapes():
    sh = {
        "xT": ((D, L), F32),
        "gate_w": ((D, NE), F32),
        "gate_b_rep": ((128, NE), F32),
        "moe_w1": ((NE, D, FF), F32),
        "moe_w2": ((NE, FF, D), F32),
        "moe_b1T": ((NE, 128, KF), F32),
        "moe_b2": ((NE, D), F32),
        "W_attn": ((D, 3 * D), F32),
        "battnT": ((128, 24), F32),
        "W_proj": ((D, D), F32),
        "bprojT": ((128, 8), F32),
        "gamT": ((128, 8), F32),
        "betT": ((128, 8), F32),
        "fc1_q": ((D, FF), I32),
        "fc1_scales": ((8, FF), F32),
        "fc1bT": ((128, KF), F32),
        "fc2_q": ((FF, D), I32),
        "fc2_scales": ((32, D), F32),
        "fc2bT": ((128, 8), F32),
        "mlp_w1T": ((D, DI), F32),
        "mlp_b1T": ((128, 16), F32),
        "mlp_w2T": ((DI, D), F32),
        "mlp_b2T": ((128, 8), F32),
        "ln_wT": ((128, 8), F32),
        "ln2_wT": ((128, 8), F32),
    }
    for i in range(NLAY):
        sh.update({
            f"m{i}_inT": ((D, 2 * DI), F32),
            f"m{i}_convw": ((128, 64), F32),
            f"m{i}_convb": ((128, 16), F32),
            f"m{i}_xpT": ((DI, 96), F32),
            f"m{i}_dtpT": ((DTR, DI), F32),
            f"m{i}_dtb": ((128, 16), F32),
            f"m{i}_alog": ((128, 256), F32),
            f"m{i}_dp": ((128, 16), F32),
            f"m{i}_outT": ((DI, D), F32),
        })
    return sh


def prepare_host_inputs(inputs):
    w = {}
    w["gate_w"] = np.ascontiguousarray(inputs["L0_ffn_gate_w"], np.float32)
    w["gate_b_rep"] = np.ascontiguousarray(
        np.broadcast_to(np.asarray(inputs["L0_ffn_gate_b"], np.float32), (128, NE)))
    w["moe_w1"] = np.ascontiguousarray(inputs["L0_ffn_w1"], np.float32)
    w["moe_w2"] = np.ascontiguousarray(inputs["L0_ffn_w2"], np.float32)
    w["moe_b1T"] = np.ascontiguousarray(
        np.asarray(inputs["L0_ffn_b1"], np.float32).reshape(NE, KF, 128).transpose(0, 2, 1))
    w["moe_b2"] = np.ascontiguousarray(inputs["L0_ffn_b2"], np.float32)
    w["W_attn"] = np.ascontiguousarray(inputs["L1_attn_W_attn"], np.float32)
    w["battnT"] = _cols(inputs["L1_attn_b_attn"], 24)
    w["W_proj"] = np.ascontiguousarray(inputs["L1_attn_W_proj"], np.float32)
    w["bprojT"] = _cols(inputs["L1_attn_b_proj"], 8)
    w["gamT"] = _cols(inputs["L1_ffn_ln_gamma"], 8)
    w["betT"] = _cols(inputs["L1_ffn_ln_beta"], 8)
    w["fc1_q"] = np.ascontiguousarray(inputs["L1_ffn_fc1_q"], np.int32)
    w["fc1_scales"] = np.ascontiguousarray(inputs["L1_ffn_fc1_scales"], np.float32)
    w["fc1bT"] = _cols(inputs["L1_ffn_fc1_bias"], KF)
    w["fc2_q"] = np.ascontiguousarray(inputs["L1_ffn_fc2_q"], np.int32)
    w["fc2_scales"] = np.ascontiguousarray(inputs["L1_ffn_fc2_scales"], np.float32)
    w["fc2bT"] = _cols(inputs["L1_ffn_fc2_bias"], 8)
    w["mlp_w1T"] = _t(inputs["L0_attn_mlp_w1"])
    w["mlp_b1T"] = _cols(inputs["L0_attn_mlp_b1"], 16)
    w["mlp_w2T"] = _t(inputs["L0_attn_mlp_w2"])
    w["mlp_b2T"] = _cols(inputs["L0_attn_mlp_b2"], 8)
    w["ln_wT"] = _cols(inputs["L0_attn_ln_w"], 8)
    w["ln2_wT"] = _cols(inputs["L0_attn_ln2_w"], 8)
    for i in range(NLAY):
        w[f"m{i}_inT"] = _t(inputs["L0_attn_in_proj_w"][i])
        w[f"m{i}_convw"] = np.ascontiguousarray(
            np.asarray(inputs["L0_attn_conv_w"][i], np.float32).reshape(16, 128, 4)
            .transpose(1, 0, 2).reshape(128, 64))
        w[f"m{i}_convb"] = _cols(inputs["L0_attn_conv_b"][i], 16)
        w[f"m{i}_xpT"] = _t(inputs["L0_attn_x_proj_w"][i])
        w[f"m{i}_dtpT"] = _t(inputs["L0_attn_dt_proj_w"][i])
        w[f"m{i}_dtb"] = _cols(inputs["L0_attn_dt_proj_b"][i], 16)
        w[f"m{i}_alog"] = np.ascontiguousarray(
            np.asarray(inputs["L0_attn_A_log"][i], np.float32).reshape(16, 128, 16)
            .transpose(1, 0, 2).reshape(128, 256))
        w[f"m{i}_dp"] = _cols(inputs["L0_attn_D_param"][i], 16)
        w[f"m{i}_outT"] = _t(inputs["L0_attn_out_proj_w"][i])
    xs = [np.ascontiguousarray(np.asarray(inputs["L0_attn_x"][b], np.float32).T)
          for b in range(8)]
    return w, xs


def mamba_W(aps, i):
    return {"inT": aps[f"m{i}_inT"], "convw": aps[f"m{i}_convw"], "convb": aps[f"m{i}_convb"],
            "xpT": aps[f"m{i}_xpT"], "dtpT": aps[f"m{i}_dtpT"], "dtb": aps[f"m{i}_dtb"],
            "alog": aps[f"m{i}_alog"], "dp": aps[f"m{i}_dp"], "outT": aps[f"m{i}_outT"],
            "m1T": aps["mlp_w1T"], "b1T": aps["mlp_b1T"], "m2T": aps["mlp_w2T"],
            "b2T": aps["mlp_b2T"], "lnwT": aps["ln_wT"]}


def build_program():
    nc = bass.Bass("TRN2", target_bir_lowering=False, debug=False)
    aps = declare_inputs(nc, input_shapes())
    y_out = nc.dram_tensor("y", [D, L], F32, kind="ExternalOutput").ap()

    def scr(name, shape, dt):
        return nc.dram_tensor(name, list(shape), dt, kind="Internal").ap()
    u1f = scr("u1f", [D, L], F32)
    u1b = scr("u1b", [D, L], BF16)
    ygd = scr("ygd", [DI, L], BF16)
    bcd = scr("bcd", [32, 256], BF16)
    xm_f = [scr(f"xm{j}f", [D, L], F32) for j in range(NLAY)]
    xm_b = [scr(f"xm{j}b", [D, L], BF16) for j in range(NLAY)]
    x1f = scr("x1f", [D, L], F32)
    u2b = scr("u2b", [D, L], BF16)
    combd = scr("combd", [NE * L], F32)
    accf = scr("accf", [D, L], F32)
    x2f = scr("x2f", [D, L], F32)
    qkvd = scr("qkvd", [3 * D, L], BF16)
    vtmd = scr("vtmd", [L, D], BF16)
    x3f = scr("x3f", [D, L], F32)

    ident_dram = nc.inline_tensor(np.eye(128, dtype=np.float32), name="identc")
    masks_dram = nc.inline_tensor(build_masks(), name="masksc")

    with ExitStack() as stack:
        tok = stack.enter_context(nc.semaphore("tok"))
        blk = stack.enter_context(nc.Block())
        p = Pr(nc, blk, tok)
        C = alloc_consts(p, stack, masks_dram, ident_dram)

        stage_ln(p, C, aps["xT"], dst_f=u1f, dst_b=u1b, name="ln1")
        src_f, src_b = u1f, u1b
        for i in range(NLAY):
            W = mamba_W(aps, i)
            stage_mamba_a(p, C, W, src_b, ygd, bcd, name=f"ma{i}")
            stage_mamba_b(p, C, W, ygd, src_f, xm_f[i], xm_b[i], name=f"mb{i}")
            src_f, src_b = xm_f[i], xm_b[i]
        stage_mamba_tail(p, C, aps["ln2_wT"], src_f, u1f, aps["xT"], x1f, name="mtl")
        stage_moe_pre(p, C, aps, x1f, u2b, combd, name="mp")
        stage_moe(p, C, aps, u2b, combd, accf, name="moex")
        stage_add_ln(p, C, accf, x1f, x2f, name="mo2")
        stage_attn_qkv(p, C, aps, x2f, qkvd, vtmd, name="aq")
        stage_attn_core(p, C, aps, qkvd, vtmd, x2f, x3f, name="ac")
        stage_ffn(p, C, aps, x3f, y_out, T=256, name="ff")

        p.ph("sync", lambda p, s: s.wait_ge(tok, p.n))
    return nc


def kernel(**inputs):
    w, xs = prepare_host_inputs(inputs)
    nc = build_program()
    in_maps = [dict(w, xT=xs[b]) for b in range(8)]
    res = run_bass_kernel_spmd(nc, in_maps, core_ids=list(range(8)))
    out = np.empty((8, L, D), dtype=np.float32)
    for b in range(8):
        out[b] = np.asarray(res.results[b]["y"]).T
    return out
